# version 01
# baseline (speedup 1.0000x reference)
import contextlib
import math
import numpy as np
import ml_dtypes
import concourse.bass as bass
import concourse.mybir as mybir
from concourse.bass_utils import run_bass_kernel_spmd

F32 = mybir.dt.float32
BF16 = mybir.dt.bfloat16
U8 = mybir.dt.uint8
BF = ml_dtypes.bfloat16
AF = mybir.ActivationFunctionType
OP = mybir.AluOpType
AX = mybir.AxisListType.X
AXC = mybir.AxisListType.C
MAGIC = 1.5 * 2.0 ** 23

B, T, D = 2, 8192, 2048
NCORE, NRANK = 8, 4
NT = 16
TLOC = NT * 128
DC = D // 128
NTG, TGW = 4, 512
H0, KVH0, DH0 = 8, 2, 256
LAM0 = 0.25
H1, DH1 = 16, 128
LAM1 = float(2048.0 ** -0.25)
J0, JC0 = 5461, 43
J1, JC1 = 5632, 44
ENGS = ("sync", "tensor", "vector", "scalar", "gpsimd")

_N_SUBLAYERS = 6


class Sem:
    def __init__(self, nc, es, name):
        self.h = es.enter_context(nc.semaphore(name))
        self.v = 0


class PB:
    def __init__(self, block):
        self.block = block
        self.qs = {e: [] for e in ENGS}

    def on(self, eng):
        def deco(fn):
            self.qs[eng].append(fn)
        return deco

    def flush(self):
        for eng in ENGS:
            fns = self.qs[eng]
            if not fns:
                continue
            def run(E, fns=fns):
                for f in fns:
                    f(E)
            getattr(self.block, eng)(run)
        self.qs = {e: [] for e in ENGS}


class Arena:
    def __init__(self, nc, es, name, nbytes):
        self.t = es.enter_context(nc.sbuf_tensor(name, [128, nbytes], U8))
        self.off = 0
        self.nbytes = nbytes
        self.marks = []

    def alloc(self, shape, dtype, align=64):
        esz = {F32: 4, BF16: 2, U8: 1}[dtype]
        self.off = (self.off + align - 1) // align * align
        nb = shape[1] * esz
        assert self.off + nb <= self.nbytes, f"arena overflow: {self.off}+{nb}>{self.nbytes}"
        v = self.t[0:shape[0], self.off:self.off + nb].bitcast(dtype)
        self.off += nb
        return v

    def push(self):
        self.marks.append(self.off)

    def pop(self):
        self.off = self.marks.pop()


def build_program(n_sublayers):
    nc = bass.Bass("TRN2", target_bir_lowering=False, debug=False)

    din = {}
    def dram_in(name, shape, dtype=F32):
        din[name] = nc.dram_tensor(name, list(shape), dtype, kind="ExternalInput").ap()
        return din[name]

    xin = dram_in("xloc", [TLOC, D])
    dram_in("L0_Wqb", [H0, 128, DC * DH0])
    dram_in("L0_Wkb", [KVH0, 128, DC * DH0])
    dram_in("L0_Wvn", [KVH0 * DH0 // 512, 128, DC * 512])
    dram_in("L0_Wob", [DC, 128, DC * 128])
    dram_in("L0_hs", [1, 8])
    dram_in("L0_w1", [J0, D]); dram_in("L0_w3", [J0, D]); dram_in("L0_w2", [D, J0])
    dram_in("L1_wqb", [H1, 128, DC * DH1])
    dram_in("L1_wkb", [H1, 128, DC * DH1])
    dram_in("L1_wvn", [D // 512, 128, DC * 512])
    dram_in("L1_wob", [DC, 128, DC * 128])
    for b_ in ("bq", "bk", "bv", "bo"):
        dram_in("L1_" + b_, [1, D])
    dram_in("L1_w1", [J1, D]); dram_in("L1_w3", [J1, D]); dram_in("L1_w2", [D, J1])
    dram_in("ident", [128, 128])
    dram_in("identb", [128, 128], BF16)
    dram_in("rope_cos", [DH0, TLOC]); dram_in("rope_sin", [DH0, TLOC])
    dram_in("dirs", [3, H0], BF16)
    dram_in("tiemask", [128, NRANK * 128], BF16)
    dram_in("onesd", [1, 8192], BF16)

    yout = nc.dram_tensor("yout", [TLOC, D], F32, kind="ExternalOutput").ap()

    xres = nc.dram_tensor("xres", [NTG, DC, 128, TGW], F32).ap()
    wTs = {
        (0, 1): nc.dram_tensor("w1T0", [JC0, 128, DC * 128], BF16).ap(),
        (0, 3): nc.dram_tensor("w3T0", [JC0, 128, DC * 128], BF16).ap(),
        (0, 2): nc.dram_tensor("w2T0", [DC, 128, JC0 * 128], BF16).ap(),
        (1, 1): nc.dram_tensor("w1T1", [JC1, 128, DC * 128], BF16).ap(),
        (1, 3): nc.dram_tensor("w3T1", [JC1, 128, DC * 128], BF16).ap(),
        (1, 2): nc.dram_tensor("w2T1", [DC, 128, JC1 * 128], BF16).ap(),
    }
    cc_in = {0: nc.dram_tensor("cc0_in", [2, KVH0 * DH0 * TLOC + 128], BF16),
             1: nc.dram_tensor("cc1_in", [2, D * TLOC + 128], BF16)}
    cc_out = {0: nc.dram_tensor("cc0_out", [2 * NRANK, KVH0 * DH0 * TLOC + 128], BF16),
              1: nc.dram_tensor("cc1_out", [2 * NRANK, D * TLOC + 128], BF16)}

    es = contextlib.ExitStack()
    arena = Arena(nc, es, "arena", 168 * 1024)
    identt = es.enter_context(nc.sbuf_tensor("identt", [128, 128], F32))
    identb = es.enter_context(nc.sbuf_tensor("identbt", [128, 128], BF16))
    statt = es.enter_context(nc.sbuf_tensor("statt", [128, 64], F32))
    onesc = es.enter_context(nc.sbuf_tensor("onesc", [128, 8], F32))
    onesb = es.enter_context(nc.sbuf_tensor("onesb", [128, 8], BF16))
    onesr = es.enter_context(nc.sbuf_tensor("onesr", [1, 128], F32))
    onesrb = es.enter_context(nc.sbuf_tensor("onesrb", [1, 512], BF16))
    rowsf = es.enter_context(nc.sbuf_tensor("rowsf", [1, 8 * TGW], F32))

    tiet = es.enter_context(nc.sbuf_tensor("tiet", [128, NRANK * 128], BF16))
    dirst = es.enter_context(nc.sbuf_tensor("dirst", [3, 8], BF16))
    hst = es.enter_context(nc.sbuf_tensor("hst", [1, 16], F32))
    bqr = es.enter_context(nc.sbuf_tensor("bqr", [1, 3 * D], BF16))
    bor = es.enter_context(nc.sbuf_tensor("bor", [1, D], BF16))
    pb = [es.enter_context(nc.psum_tensor(f"pb{i}", [128, 512], F32)) for i in range(8)]
    pbh = [p[:, :].bitcast(BF16) for p in pb]

    sems = {}
    def S(name):
        if name not in sems:
            sems[name] = Sem(nc, es, name)
        return sems[name]

    SC = {"s1_0": 0, "s3_0": 1, "s2h_0": 2, "r_0_0": 3, "r_1_0": 4, "r_2_0": 5,
          "s1_1": 8, "s3_1": 9, "s2h_1": 10, "r_0_1": 11, "r_1_1": 12, "r_2_1": 13,
          "acc0": 16, "acc1": 17, "acc2": 18, "tmp": 19,
          "nmel": 20, "invn": 21, "s11": 22, "r11": 23, "negmagic": 24, "eps": 25,
          "negm0": 26, "negm1": 27, "zero": 28, "tmpr": 29,
          **{f"qmx{i}": 32 + i for i in range(16)},
          **{f"kmx{i}": 48 + i for i in range(16)}}
    def sc(name):
        return statt[:, SC[name]:SC[name] + 1]
    def sc1(name):
        return statt[0:1, SC[name]:SC[name] + 1]

    block = es.enter_context(nc.Block())
    pbld = PB(block)

    xres_guards = []

    with es:
        arena.push()
        xtiles = [arena.alloc([128, D], F32) for _ in range(4)]
        stg0 = [arena.alloc([128, TGW], F32) for _ in range(2)]
        xld, p0mm, p0cp = S("xld"), S("p0mm"), S("p0cp")
        init = S("init")
        p0st = [S("p0st0"), S("p0st1")]
        cld = S("cld")

        @pbld.on("sync")
        def _(s):
            s.dma_start(out=identb[:, :], in_=din["identb"][:, :]).then_inc(cld.h, 16)
            s.dma_start(out=tiet[:, :], in_=din["tiemask"][:, :]).then_inc(cld.h, 16)
            s.dma_start(out=dirst[:, :], in_=din["dirs"][:, :]).then_inc(cld.h, 16)
            s.dma_start(out=hst[0:1, 0:8], in_=din["L0_hs"][:, :]).then_inc(cld.h, 16)
            s.dma_start(out=identt[:, :], in_=din["ident"][:, :]).then_inc(cld.h, 16)
            for tg in range(NTG):
                for tt in range(4):
                    k = tg * 4 + tt
                    s.dma_start(out=xtiles[tt][:, :],
                                in_=xin[k * 128:(k + 1) * 128, :]).then_inc(xld.h, 16)
                for dc in range(DC):
                    i = tg * DC + dc
                    s.wait_ge(p0cp.h, i + 1)
                    s.dma_start(out=xres[tg, dc, :, :],
                                in_=stg0[i % 2][:, :]).then_inc(p0st[i % 2].h, 16)
        cld.v += 80
        xld.v += NTG * 4 * 16
        p0st[0].v = NTG * DC // 2 * 16
        p0st[1].v = NTG * DC // 2 * 16

        cldg = S("cldg")

        @pbld.on("gpsimd")
        def _(g):
            for i_, nm in enumerate(("bq", "bk", "bv")):
                g.dma_start(out=bqr[0:1, i_ * D:(i_ + 1) * D],
                            in_=din["L1_" + nm][:, :]).then_inc(cldg.h, 16)
            g.dma_start(out=bor[0:1, :], in_=din["L1_bo"][:, :]).then_inc(cldg.h, 16)
        cldg.v += 64

        @pbld.on("vector")
        def _(v):
            v.wait_ge(cld.h, 80)
            v.wait_ge(S("cldg").h, 64)
            v.tensor_scalar(hst[0:1, 8:16], hst[0:1, 0:8], -1.0, 1.0, OP.mult, OP.add)
            v.drain()
            v.memset(onesc[:, :], 1.0)
            v.memset(onesb[:, :], 1.0)
            v.memset(onesr[:, :], 1.0)
            v.memset(onesrb[:, :], 1.0)
            v.memset(statt[:, 24:25], -MAGIC)
            v.memset(statt[:, 25:26], 1e-5).then_inc(init.h, 1)
            for tg in range(NTG):
                for dc in range(DC):
                    i = tg * DC + dc
                    v.wait_ge(p0mm.h, i + 1)
                    if i >= 2:
                        v.wait_ge(p0st[i % 2].h, 16 * (i // 2))
                    v.tensor_copy(stg0[i % 2][:, :], pb[i % 2][:, :]).then_inc(p0cp.h, 1)
        p0cp.v = NTG * DC
        p0mm.v = NTG * DC

        @pbld.on("tensor")
        def _(t):
            t.wait_ge(cld.h, 80)
            t.wait_ge(S("init").h, 1)
            for tg in range(NTG):
                t.wait_ge(xld.h, (tg + 1) * 4 * 16)
                for dc in range(DC):
                    i = tg * DC + dc
                    if i >= 2:
                        t.wait_ge(p0cp.h, i - 1)
                    for tt in range(4):
                        inst = t.transpose(pb[i % 2][:, tt * 128:(tt + 1) * 128],
                                           xtiles[tt][:, dc * 128:(dc + 1) * 128],
                                           identt[:, :])
                    inst.then_inc(p0mm.h, 1)
        pbld.flush()
        arena.pop()
        xres_guards = [(p0st[0], p0st[0].v), (p0st[1], p0st[1].v)]


        f_td = [S("ftd0"), S("ftd1")]

        def emit_tree(g, buf, K, mode, out_row, tbw):
            W = TGW
            CH = 4 if mode == "max" else 2
            nch = (K + CH - 1) // CH
            tdv = [f_td[0].v, f_td[1].v]
            for c in range(nch):
                k0, k1 = c * CH, min((c + 1) * CH, K)
                kw = (k1 - k0) * W
                if c >= 2:
                    g.wait_ge(f_td[c % 2].h, tdv[c % 2])
                if mode == "max":
                    src = buf[:, k0 * W:k1 * W]
                    rb = (c % 2) * 4 * W
                else:
                    if mode == "negmax":
                        g.tensor_scalar_mul(tbw[:, 0:kw], buf[:, k0 * W:k1 * W], -1.0)
                    else:
                        g.tensor_tensor(tbw[:, 0:kw], buf[:, k0 * W:k1 * W],
                                        buf[:, k0 * W:k1 * W], OP.mult)
                    g.drain()
                    src = tbw[:, 0:kw]
                    rb = 2 * W + (c % 2) * 2 * W
                g.tensor_reduce(tbw[0:1, rb:rb + kw], src, AXC, OP.max)
                g.drain()
                g.dma_start(out=tbw[k0:k1, 8 * W:9 * W],
                            in_=tbw[0:1, rb:rb + kw]).then_inc(f_td[c % 2].h, 16)
                tdv[c % 2] += 16
            f_td[0].v, f_td[1].v = tdv
            g.wait_ge(f_td[0].h, tdv[0])
            g.wait_ge(f_td[1].h, tdv[1])
            g.tensor_reduce(out_row, tbw[0:K, 8 * W:9 * W], AXC, OP.max)
            g.drain()

        f_wl = [S("fwl0"), S("fwl1")]
        f_qv, f_qg, f_qs = S("fqv"), S("fqg"), S("fqs")
        f_tp, f_te = S("ftp"), S("fte")
        f_ts = [S("fts0"), S("fts1")]
        f_mq = S("fmq")
        f_mm = S("fmm")
        f_row = S("frow")
        f_xl = S("fxl")
        f_nq = S("fnq")
        f_sl = [S(f"fsl{i}") for i in range(4)]
        f_hm = S("fhm")
        f_gs = S("fgs")
        f_ge = S("fge")
        f_gq = S("fgq")
        f_w2l = [S("fw2l0"), S("fw2l1")]
        f_om = S("fom")
        f_oe = S("foe")
        f_os = [S("fos0"), S("fos1")]

        def emit_quant_prep(L, prior=()):
            prior = list(prior)
            Jreal = J0 if L == 0 else J1
            JC = JC0 if L == 0 else JC1
            numel = float(Jreal * D)
            mats = [("1", din[f"L{L}_w1"], Jreal, D, wTs[(L, 1)], DC),
                    ("3", din[f"L{L}_w3"], Jreal, D, wTs[(L, 3)], DC),
                    ("2", din[f"L{L}_w2"], D, Jreal, wTs[(L, 2)], JC)]

            arena.push()
            wrow = [arena.alloc([128, 5632], F32) for _ in range(2)]
            t1q = arena.alloc([128, 5632], F32)
            t2q = arena.alloc([128, 5632], F32)
            ternq = arena.alloc([128, 5632], BF16)
            evt = [arena.alloc([128, max(DC, JC) * 128], BF16) for _ in range(2)]
            accb = arena.alloc([128, 128], F32)

            pieces = []
            for mi, (_, win, R, Cw, _, _) in enumerate(mats):
                nch = (R + 127) // 128
                sub = []
                for rc in range(nch):
                    r0, r1 = rc * 128, min((rc + 1) * 128, R)
                    for c0 in range(0, Cw, 4096):
                        c1 = min(c0 + 4096, Cw)
                        sub.append((mi, win, r0, r1, c0, c1))
                pieces.extend([(p, i == len(sub) - 1) for i, p in enumerate(sub)])

            base_wl = [f_wl[0].v, f_wl[1].v]
            base_mq = f_mq.v
            nld = [0, 0]
            ldv = []
            for i, ((mi, win, r0, r1, c0, c1), last) in enumerate(pieces):
                nld[i % 2] += 1
                ldv.append(base_wl[i % 2] + nld[i % 2] * 16)
            f_wl[0].v = base_wl[0] + nld[0] * 16
            f_wl[1].v = base_wl[1] + nld[1] * 16

            colof = list(range(len(pieces)))
            colrange = {}
            start = 0
            cur = None
            for i, ((mi, win, r0, r1, c0, c1), last) in enumerate(pieces):
                if cur is None:
                    cur, start = mi, i
                if last:
                    colrange[mi] = (start, i + 1)
                    cur = None
            assert len(pieces) <= 128

            @pbld.on("sync")
            def _(s, pieces=pieces, base_mq=base_mq, ldv=ldv, prior=prior):
                for sem, val in prior:
                    s.wait_ge(sem.h, val)
                for i, ((mi, win, r0, r1, c0, c1), last) in enumerate(pieces):
                    if i >= 2:
                        s.wait_ge(f_mq.h, base_mq + (i - 2) + 1)
                    s.dma_start(out=wrow[i % 2][0:r1 - r0, 0:c1 - c0],
                                in_=win[r0:r1, c0:c1]).then_inc(f_wl[i % 2].h, 16)

            base_accz = f_row.v + 1
            f_row.v += 1

            @pbld.on("vector")
            def _(v, prior=prior):
                for sem, val in prior:
                    v.wait_ge(sem.h, val)
                v.memset(accb[:, :], 0.0).then_inc(f_row.h, 1)

            @pbld.on("scalar")
            def _(s, pieces=pieces, ldv=ldv, base_mq=base_mq, colof=colof, base_accz=base_accz):
                s.wait_ge(S("init").h, 1)
                s.wait_ge(f_row.h, base_accz)
                for i, ((mi, win, r0, r1, c0, c1), last) in enumerate(pieces):
                    s.wait_ge(f_wl[i % 2].h, ldv[i])
                    co = colof[i]
                    s.activation(t1q[0:r1 - r0, 0:c1 - c0],
                                 wrow[i % 2][0:r1 - r0, 0:c1 - c0], AF.Abs,
                                 accum_out=accb[0:r1 - r0, co:co + 1]).then_inc(f_mq.h, 1)
                    s.drain()

            mq_end = {}
            cnt = base_mq
            for i, ((mi, win, r0, r1, c0, c1), last) in enumerate(pieces):
                cnt += 1
                if last:
                    mq_end[mi] = cnt
            f_mq.v = cnt

            base_rowred = f_row.v
            row_red = {mi: base_rowred + mi + 1 for mi in range(3)}

            @pbld.on("vector")
            def _(v, mq_end=mq_end, colrange=colrange):
                for mi in range(3):
                    v.wait_ge(f_mq.h, mq_end[mi])
                    c0_, c1_ = colrange[mi]
                    v.reduce_sum(sc(f"acc{mi}"), accb[:, c0_:c1_],
                                 axis=AX).then_inc(f_row.h, 1)
                    v.drain()
            f_row.v += 3

            base_mm = f_mm.v

            @pbld.on("tensor")
            def _(t, row_red=row_red):
                for mi in range(3):
                    t.wait_ge(f_row.h, row_red[mi])
                    t.matmul(pb[mi][0:1, 0:1], sc(f"acc{mi}"), onesc[:, 0:1],
                             start=True, stop=True).then_inc(f_mm.h, 1)
            f_mm.v += 3

            base_row = f_row.v

            @pbld.on("vector")
            def _(v, base_mm=base_mm, numel=numel, L=L):
                v.memset(sc1("nmel"), numel)
                v.drain()
                v.reciprocal(sc1("invn"), sc1("nmel"))
                v.drain()
                for mi in range(3):
                    v.wait_ge(f_mm.h, base_mm + mi + 1)
                    v.tensor_tensor(sc1("s11"), pb[mi][0:1, 0:1], sc1("invn"), OP.mult)
                    v.drain()
                    if L == 1:
                        v.tensor_scalar_max(sc1("s11"), sc1("s11"), 1e-5)
                        v.drain()
                    v.reciprocal(sc1("r11"), sc1("s11"))
                    v.drain()
                    v.tensor_copy(rowsf[0:1, mi * 2:mi * 2 + 1], sc1("s11"))
                    v.tensor_copy(rowsf[0:1, mi * 2 + 1:mi * 2 + 2],
                                  sc1("r11")).then_inc(f_row.h, 1)
                    v.drain()
            f_row.v += 3

            @pbld.on("tensor")
            def _(t, base_row=base_row):
                for mi in range(3):
                    t.wait_ge(f_row.h, base_row + mi + 1)
                    t.matmul(pb[3 + mi][:, 0:1], onesr[:, :],
                             rowsf[0:1, mi * 2:mi * 2 + 1], start=True, stop=True)
                    t.matmul(pb[3 + mi][:, 1:2], onesr[:, :],
                             rowsf[0:1, mi * 2 + 1:mi * 2 + 2],
                             start=True, stop=True).then_inc(f_mm.h, 1)
            base_mm2 = f_mm.v
            f_mm.v += 3

            @pbld.on("vector")
            def _(v, base_mm2=base_mm2, L=L):
                names = [f"s1_{L}", f"s3_{L}", f"s2h_{L}"]
                for mi in range(3):
                    v.wait_ge(f_mm.h, base_mm2 + mi + 1)
                    if mi == 2:
                        v.tensor_scalar_mul(sc(names[mi]), pb[3 + mi][:, 0:1], 0.5)
                    else:
                        v.tensor_copy(sc(names[mi]), pb[3 + mi][:, 0:1])
                    v.tensor_copy(sc(f"r_{mi}_{L}"),
                                  pb[3 + mi][:, 1:2]).then_inc(f_row.h, 1)
            prep_rows_done = f_row.v + 3
            f_row.v += 3

            chunks = []
            for mi, (tagm, win, R, Cw, wTd, IC) in enumerate(mats):
                nch = (R + 127) // 128
                for rc in range(nch):
                    r0, r1 = rc * 128, min((rc + 1) * 128, R)
                    chunks.append((mi, win, R, Cw, wTd, IC, rc, r0, r1))

            base_wl2 = [f_wl[0].v, f_wl[1].v]
            base_qv, base_qg, base_qs = f_qv.v, f_qg.v, f_qs.v
            base_tp, base_te = f_tp.v, f_te.v
            base_ts = [f_ts[0].v, f_ts[1].v]
            nld = [0, 0]
            ldv2 = []
            for i, ch in enumerate(chunks):
                nld[i % 2] += 1
                ldv2.append(base_wl2[i % 2] + nld[i % 2] * 16)
            f_wl[0].v = base_wl2[0] + nld[0] * 16
            f_wl[1].v = base_wl2[1] + nld[1] * 16

            tp_val, te_val = [], []
            tpc, tec = base_tp, base_te
            grp_of = []
            for i, (mi, win, R, Cw, wTd, IC, rc, r0, r1) in enumerate(chunks):
                ng = (IC + 7) // 8
                grp_of.append((tpc, tec, ng))
                tpc += ng
                tec += ng
                tp_val.append(tpc)
                te_val.append(tec)
            f_tp.v, f_te.v = tpc, tec

            stv = []
            nst = [0, 0]
            for i, ch in enumerate(chunks):
                nst[i % 2] += 1
                stv.append(base_ts[i % 2] + nst[i % 2] * 16)
            f_ts[0].v = base_ts[0] + nst[0] * 16
            f_ts[1].v = base_ts[1] + nst[1] * 16

            @pbld.on("sync")
            def _(s, chunks=chunks, base_qv=base_qv, prep_rows_done=prep_rows_done):
                s.wait_ge(f_row.h, prep_rows_done)
                for i, (mi, win, R, Cw, wTd, IC, rc, r0, r1) in enumerate(chunks):
                    if i >= 2:
                        s.wait_ge(f_qv.h, base_qv + i - 1)
                    s.dma_start(out=wrow[i % 2][0:r1 - r0, 0:Cw],
                                in_=win[r0:r1, 0:Cw]).then_inc(f_wl[i % 2].h, 16)

            @pbld.on("vector")
            def _(v, chunks=chunks, ldv2=ldv2, base_qg=base_qg, L=L):
                for i, (mi, win, R, Cw, wTd, IC, rc, r0, r1) in enumerate(chunks):
                    v.wait_ge(f_wl[i % 2].h, ldv2[i])
                    if i >= 1:
                        v.wait_ge(f_qg.h, base_qg + i)
                    v.tensor_scalar(t1q[0:r1 - r0, 0:Cw], wrow[i % 2][0:r1 - r0, 0:Cw],
                                    sc(f"r_{mi}_{L}")[0:r1 - r0, :], 1.0,
                                    OP.mult, OP.min).then_inc(f_qv.h, 1)
            f_qv.v = base_qv + len(chunks)

            @pbld.on("gpsimd")
            def _(g, chunks=chunks, base_qv=base_qv, base_qs=base_qs):
                for i, (mi, win, R, Cw, wTd, IC, rc, r0, r1) in enumerate(chunks):
                    g.wait_ge(f_qv.h, base_qv + i + 1)
                    if i >= 1:
                        g.wait_ge(f_qs.h, base_qs + i)
                    g.tensor_scalar(t2q[0:r1 - r0, 0:Cw], t1q[0:r1 - r0, 0:Cw],
                                    -1.0, MAGIC, OP.max, OP.add).then_inc(f_qg.h, 1)
            f_qg.v = base_qg + len(chunks)

            @pbld.on("scalar")
            def _(s, chunks=chunks, base_qg=base_qg, base_tp=base_tp, grp_of=grp_of,
                  stv=stv, te_val=te_val):
                for i, (mi, win, R, Cw, wTd, IC, rc, r0, r1) in enumerate(chunks):
                    s.wait_ge(f_qg.h, base_qg + i + 1)
                    if i >= 1:
                        s.wait_ge(f_tp.h, tp_val[i - 1])
                    s.add(ternq[0:r1 - r0, 0:Cw], t2q[0:r1 - r0, 0:Cw],
                          sc("negmagic")[0:r1 - r0, :]).then_inc(f_qs.h, 1)
                    tpc0, tec0, ng = grp_of[i]
                    rows = r1 - r0
                    if i >= 2:
                        s.wait_ge(f_ts[i % 2].h, stv[i - 2])
                    if i < 2:
                        s.memzero(evt[i % 2][:, :])
                        s.drain()
                    elif (mi < 2 and rows < 128) or (mi == 2 and Cw < IC * 128):
                        s.memzero(evt[i % 2][:, 0:IC * 128])
                        s.drain()
                    for g_ in range(ng):
                        gidx = tpc0 + g_
                        s.wait_ge(f_tp.h, gidx + 1)
                        bank = 4 + (gidx % 4)
                        w = min(g_ * 8 + 8, IC) - g_ * 8
                        if mi < 2 and rows == 128:
                            s.copy(evt[i % 2][:, g_ * 8 * 128:(g_ * 8 + w) * 128],
                                   pbh[bank][:, 0:w * 128]).then_inc(f_te.h, 1)
                        elif mi < 2:
                            for u in range(g_ * 8, g_ * 8 + w):
                                sl = (u - g_ * 8) * 128
                                inst = s.copy(evt[i % 2][0:128, u * 128:u * 128 + rows],
                                              pbh[bank][0:128, sl:sl + rows])
                            inst.then_inc(f_te.h, 1)
                        else:
                            full = all(min((u + 1) * 128, Cw) - u * 128 == 128
                                       for u in range(g_ * 8, g_ * 8 + w))
                            if full:
                                s.copy(evt[i % 2][:, g_ * 8 * 128:(g_ * 8 + w) * 128],
                                       pbh[bank][:, 0:w * 128]).then_inc(f_te.h, 1)
                            else:
                                for u in range(g_ * 8, g_ * 8 + w):
                                    sl = (u - g_ * 8) * 128
                                    c0, c1 = u * 128, min((u + 1) * 128, Cw)
                                    inst = s.copy(
                                        evt[i % 2][0:c1 - c0, u * 128:(u + 1) * 128],
                                        pbh[bank][0:c1 - c0, sl:sl + 128])
                                inst.then_inc(f_te.h, 1)
                        s.drain()
                    s.dma_start(out=wTd[rc, :, :],
                                in_=evt[i % 2][:, 0:IC * 128]).then_inc(f_ts[i % 2].h, 16)
            f_qs.v = base_qs + len(chunks)

            @pbld.on("tensor")
            def _(t, chunks=chunks, base_qs=base_qs, grp_of=grp_of, te_val=te_val):
                for i, (mi, win, R, Cw, wTd, IC, rc, r0, r1) in enumerate(chunks):
                    t.wait_ge(f_qs.h, base_qs + i + 1)
                    tpc0, tec0, ng = grp_of[i]
                    rows = r1 - r0
                    for g_ in range(ng):
                        gidx = tpc0 + g_
                        if gidx - base_tp >= 4:
                            t.wait_ge(f_te.h, base_te + (gidx - base_tp) - 3)
                        bank = 4 + (gidx % 4)
                        for u in range(g_ * 8, min(g_ * 8 + 8, IC)):
                            sl = (u - g_ * 8) * 128
                            c0, c1 = u * 128, min((u + 1) * 128, Cw)
                            if mi < 2:
                                inst = t.transpose(pbh[bank][0:128, sl:sl + rows],
                                                   ternq[0:rows, c0:c1],
                                                   identb[0:rows, 0:rows])
                            else:
                                inst = t.transpose(pbh[bank][0:c1 - c0, sl:sl + 128],
                                                   ternq[0:128, c0:c1],
                                                   identb[0:128, 0:128])
                        inst.then_inc(f_tp.h, 1)

            pbld.flush()
            arena.pop()

        def emit_ffn(L, tag):
            nonlocal xres_guards
            Jreal = J0 if L == 0 else J1
            JC = JC0 if L == 0 else JC1
            w1T, w3T, w2T = wTs[(L, 1)], wTs[(L, 3)], wTs[(L, 2)]
            s1c, s3c, s2hc = sc(f"s1_{L}"), sc(f"s3_{L}"), sc(f"s2h_{L}")

            arena.push()
            xres_tg = arena.alloc([128, DC * TGW], F32)
            n_tg = arena.alloc([128, DC * TGW], BF16)
            gbuf = arena.alloc([128, JC * TGW], BF16)
            rbb = arena.alloc([128, TGW], F32)
            mrbb = arena.alloc([128, TGW], F32)
            sxb = arena.alloc([128, TGW], F32)
            stg2 = [arena.alloc([128, TGW], F32) for _ in range(2)]
            g1b = [arena.alloc([128, TGW], BF16) for _ in range(2)]
            wTt = [arena.alloc([128, DC * 128], BF16) for _ in range(4)]
            w2Tt = [arena.alloc([128, JC * 128], BF16) for _ in range(2)]
            tbw = arena.alloc([128, 9 * TGW], F32)

            prior = list(xres_guards) + [(f_ts[0], f_ts[0].v), (f_ts[1], f_ts[1].v)]
            new_guards = []
            os_prev_end = [f_os[0].v, f_os[1].v]

            for tg in range(NTG):
                base_xl = f_xl.v
                oe_prev_all = f_oe.v

                @pbld.on("sync")
                def _(s, tg=tg, prior=prior, oe_prev_all=oe_prev_all):
                    if tg == 0:
                        for sem, val in prior:
                            s.wait_ge(sem.h, val)
                    s.wait_ge(f_oe.h, oe_prev_all)
                    for dc in range(DC):
                        s.dma_start(out=xres_tg[:, dc * TGW:(dc + 1) * TGW],
                                    in_=xres[tg, dc, :, :]).then_inc(f_xl.h, 16)
                f_xl.v += DC * 16
                xlv = f_xl.v

                base_mm = f_mm.v

                @pbld.on("tensor")
                def _(t, xlv=xlv):
                    t.wait_ge(f_xl.h, xlv)
                    for dc in range(DC):
                        inst = t.matmul(pb[4][0:1, 0:TGW], onesc[:, 0:1],
                                        xres_tg[:, dc * TGW:(dc + 1) * TGW],
                                        start=(dc == 0), stop=(dc == DC - 1))
                    inst.then_inc(f_mm.h, 1)
                f_mm.v += 1
                mm_mu = f_mm.v

                base_qs2 = f_qs.v

                @pbld.on("scalar")
                def _(s, xlv=xlv):
                    s.wait_ge(f_xl.h, xlv)
                    for dc in range(DC):
                        s.square(n_tg[:, dc * TGW:(dc + 1) * TGW],
                                 xres_tg[:, dc * TGW:(dc + 1) * TGW]).then_inc(f_qs.h, 1)
                f_qs.v += DC
                qs_x2 = f_qs.v

                @pbld.on("tensor")
                def _(t, base_qs2=base_qs2):
                    for dc in range(DC):
                        t.wait_ge(f_qs.h, base_qs2 + dc + 1)
                        inst = t.matmul(pb[5][0:1, 0:TGW], onesb[:, 0:1],
                                        n_tg[:, dc * TGW:(dc + 1) * TGW],
                                        start=(dc == 0), stop=(dc == DC - 1))
                    inst.then_inc(f_mm.h, 1)
                f_mm.v += 1
                mm_x2 = f_mm.v

                if L == 1:
                    @pbld.on("gpsimd")
                    def _(g, xlv=xlv):
                        g.wait_ge(f_xl.h, xlv)
                        emit_tree(g, xres_tg, DC, "max",
                                  rowsf[0:1, 0 * TGW:1 * TGW], tbw)
                        emit_tree(g, xres_tg, DC, "negmax",
                                  rowsf[0:1, 1 * TGW:2 * TGW], tbw)
                        g.sem_inc(f_row.h, 1)
                    f_row.v += 1

                rvA = f_row.v + 1
                rvS = f_row.v + 2
                rvB = f_row.v + 3

                @pbld.on("vector")
                def _(v, mm_mu=mm_mu, mm_x2=mm_x2, rvS=rvS, L=L):
                    mu = rowsf[0:1, 2 * TGW:3 * TGW]
                    var = rowsf[0:1, 3 * TGW:4 * TGW]
                    r = rowsf[0:1, 4 * TGW:5 * TGW]
                    mr = rowsf[0:1, 5 * TGW:6 * TGW]
                    v.wait_ge(f_mm.h, mm_mu)
                    v.tensor_scalar_mul(mu, pb[4][0:1, 0:TGW], 1.0 / 2048.0)
                    v.wait_ge(f_mm.h, mm_x2)
                    v.tensor_scalar_mul(var, pb[5][0:1, 0:TGW], 1.0 / 2048.0)
                    v.drain()
                    v.tensor_tensor(rowsf[0:1, 6 * TGW:7 * TGW], mu, mu, OP.mult)
                    v.drain()
                    v.tensor_tensor(var, var, rowsf[0:1, 6 * TGW:7 * TGW],
                                    OP.subtract).then_inc(f_row.h, 1)
                    v.wait_ge(f_row.h, rvS)
                    v.reciprocal(r, rowsf[0:1, 7 * TGW:8 * TGW])
                    v.drain()
                    inst = v.tensor_tensor(mr, mu, r, OP.mult)
                    if L == 1:
                        Rx = rowsf[0:1, 0 * TGW:1 * TGW]
                        Rn = rowsf[0:1, 1 * TGW:2 * TGW]
                        v.tensor_tensor(Rx, Rx, mu, OP.subtract)
                        v.tensor_tensor(Rn, Rn, mu, OP.add)
                        v.drain()
                        v.tensor_tensor(Rx, Rx, Rn, OP.max)
                        v.drain()
                        v.tensor_tensor(Rx, Rx, r, OP.mult)
                        v.drain()
                        v.tensor_scalar_max(Rx, Rx, 1e-5)
                        v.drain()
                        v.reciprocal(Rn, Rx)
                        v.drain()
                        v.tensor_scalar_mul(Rn, Rn, 127.0)
                        v.tensor_scalar_mul(Rx, Rx, 1.0 / 127.0)
                        v.drain()
                        v.tensor_tensor(r, r, Rn, OP.mult)
                        inst = v.tensor_tensor(mr, mr, Rn, OP.mult)
                    inst.then_inc(f_row.h, 1)
                    v.drain()

                @pbld.on("scalar")
                def _(s, rvA=rvA):
                    s.wait_ge(f_row.h, rvA)
                    s.activation(rowsf[0:1, 7 * TGW:8 * TGW], rowsf[0:1, 3 * TGW:4 * TGW],
                                 AF.Sqrt, bias=sc1("eps")).then_inc(f_row.h, 1)
                f_row.v = rvB

                @pbld.on("tensor")
                def _(t, rvB=rvB, L=L):
                    t.wait_ge(f_row.h, rvB)
                    t.matmul(pb[6][:, 0:TGW], onesr[:, :], rowsf[0:1, 4 * TGW:5 * TGW],
                             start=True, stop=True)
                    inst = t.matmul(pb[7][:, 0:TGW], onesr[:, :], rowsf[0:1, 5 * TGW:6 * TGW],
                                    start=True, stop=True)
                    if L == 1:
                        inst = t.matmul(pb[4][:, 0:TGW], onesr[:, :],
                                        rowsf[0:1, 0 * TGW:1 * TGW], start=True, stop=True)
                    inst.then_inc(f_mm.h, 1)
                f_mm.v += 1
                mm_bc = f_mm.v

                @pbld.on("vector")
                def _(v, mm_bc=mm_bc, L=L):
                    v.wait_ge(f_mm.h, mm_bc)
                    v.tensor_copy(rbb[:, :], pb[6][:, 0:TGW])
                    inst = v.tensor_copy(mrbb[:, :], pb[7][:, 0:TGW])
                    if L == 1:
                        inst = v.tensor_copy(sxb[:, :], pb[4][:, 0:TGW])
                    inst.then_inc(f_row.h, 1)
                f_row.v += 1
                row_bc = f_row.v

                base_p1 = f_nq.v
                base_gq = f_gq.v

                @pbld.on("gpsimd")
                def _(g, row_bc=row_bc, xlv=xlv, base_nq=f_nq.v, ose=list(os_prev_end)):
                    g.wait_ge(f_row.h, row_bc)
                    for dc in range(DC):
                        if dc < 2:
                            g.wait_ge(f_os[dc % 2].h, ose[dc % 2])
                        else:
                            g.wait_ge(f_nq.h, base_nq + dc - 1)
                        g.tensor_tensor(stg2[dc % 2][:, :],
                                        xres_tg[:, dc * TGW:(dc + 1) * TGW],
                                        rbb[:, :], OP.mult).then_inc(f_gq.h, 1)
                f_gq.v += DC

                @pbld.on("vector")
                def _(v, base_gq=base_gq, mm_x2=mm_x2, L=L):
                    v.wait_ge(f_mm.h, mm_x2)
                    for dc in range(DC):
                        v.wait_ge(f_gq.h, base_gq + dc + 1)
                        nt = n_tg[:, dc * TGW:(dc + 1) * TGW]
                        if L == 0:
                            v.tensor_tensor(nt, stg2[dc % 2][:, :], mrbb[:, :],
                                            OP.subtract).then_inc(f_nq.h, 1)
                        else:
                            v.tensor_tensor(stg2[dc % 2][:, :], stg2[dc % 2][:, :],
                                            mrbb[:, :], OP.subtract)
                            v.drain()
                            v.tensor_scalar(stg2[dc % 2][:, :], stg2[dc % 2][:, :],
                                            MAGIC, MAGIC, OP.add, OP.subtract)
                            v.drain()
                            v.tensor_tensor(nt, stg2[dc % 2][:, :], sxb[:, :],
                                            OP.mult).then_inc(f_nq.h, 1)
                f_nq.v += DC
                nq_tg = f_nq.v

                base_sl = [s_.v for s_ in f_sl]
                base_hm = f_hm.v
                base_gs = f_gs.v
                base_ge = f_ge.v
                hm_val = {}
                for j in range(JC):
                    for m in range(2):
                        hm_val[(j, m)] = base_hm + j * 2 + m + 1
                gs_val = {j: base_gs + j + 1 for j in range(JC)}
                ge_val = {j: base_ge + j + 1 for j in range(JC)}

                @pbld.on("sync")
                def _(s, base_hm=base_hm, L=L, JC=JC, w1T=w1T, w3T=w3T):
                    for j in range(JC):
                        for m, wTd in ((0, w1T), (1, w3T)):
                            slot = m * 2 + (j % 2)
                            if j >= 2:
                                s.wait_ge(f_hm.h, base_hm + (j - 2) * 2 + m + 1)
                            s.dma_start(out=wTt[slot][:, :],
                                        in_=wTd[j, :, :]).then_inc(f_sl[slot].h, 16)

                slv = {}
                cnt_sl = list(base_sl)
                for j in range(JC):
                    for m in range(2):
                        slot = m * 2 + (j % 2)
                        cnt_sl[slot] += 16
                        slv[(j, m)] = cnt_sl[slot]
                for slot in range(4):
                    f_sl[slot].v = cnt_sl[slot]

                @pbld.on("tensor")
                def _(t, slv=slv, nq_tg=nq_tg, base_gs=base_gs, base_ge=base_ge, JC=JC):
                    t.wait_ge(f_nq.h, nq_tg)
                    for j in range(JC):
                        for m in range(2):
                            slot = m * 2 + (j % 2)
                            t.wait_ge(f_sl[slot].h, slv[(j, m)])
                            if j >= 2:
                                if m == 0:
                                    t.wait_ge(f_gs.h, base_gs + (j - 2) + 1)
                                else:
                                    t.wait_ge(f_ge.h, base_ge + (j - 2) + 1)
                            bank = m * 2 + (j % 2)
                            for dc in range(DC):
                                inst = t.matmul(pb[bank][:, 0:TGW],
                                                wTt[slot][:, dc * 128:(dc + 1) * 128],
                                                n_tg[:, dc * TGW:(dc + 1) * TGW],
                                                start=(dc == 0), stop=(dc == DC - 1))
                            inst.then_inc(f_hm.h, 1)
                f_hm.v += 2 * JC

                @pbld.on("scalar")
                def _(s, hm_val=hm_val, base_ge=base_ge, s1c=s1c, JC=JC):
                    for j in range(JC):
                        s.wait_ge(f_hm.h, hm_val[(j, 0)])
                        if j >= 2:
                            s.wait_ge(f_ge.h, base_ge + (j - 2) + 1)
                        s.activation(g1b[j % 2][:, :], pb[j % 2][:, 0:TGW], AF.Silu,
                                     scale=s1c).then_inc(f_gs.h, 1)
                f_gs.v += JC

                @pbld.on("vector")
                def _(v, hm_val=hm_val, gs_val=gs_val, s3c=s3c, JC=JC):
                    for j in range(JC):
                        v.wait_ge(f_hm.h, hm_val[(j, 1)])
                        v.wait_ge(f_gs.h, gs_val[j])
                        v.scalar_tensor_tensor(gbuf[:, j * TGW:(j + 1) * TGW],
                                               pb[2 + j % 2][:, 0:TGW], s3c,
                                               g1b[j % 2][:, :], OP.mult,
                                               OP.mult).then_inc(f_ge.h, 1)
                f_ge.v += JC
                ge_all = f_ge.v

                if L == 1:
                    rvt = f_row.v + 1
                    f_row.v += 1

                    @pbld.on("gpsimd")
                    def _(g, ge_all=ge_all, JC=JC):
                        g.wait_ge(f_ge.h, ge_all)
                        emit_tree(g, gbuf[:, 0:JC * TGW], JC, "sqmax",
                                  rowsf[0:1, 0 * TGW:1 * TGW], tbw)
                        g.sem_inc(f_row.h, 1)

                    rvsq = f_row.v + 1
                    f_row.v += 1

                    @pbld.on("scalar")
                    def _(s, rvt=rvt):
                        s.wait_ge(f_row.h, rvt)
                        s.activation(rowsf[0:1, 1 * TGW:2 * TGW],
                                     rowsf[0:1, 0 * TGW:1 * TGW], AF.Sqrt,
                                     bias=sc1("zero")).then_inc(f_row.h, 1)

                    rvq = f_row.v + 1

                    @pbld.on("vector")
                    def _(v, rvsq=rvsq):
                        v.wait_ge(f_row.h, rvsq)
                        Rx = rowsf[0:1, 0 * TGW:1 * TGW]
                        Rn = rowsf[0:1, 1 * TGW:2 * TGW]
                        tmp = rowsf[0:1, 6 * TGW:7 * TGW]
                        v.tensor_scalar_max(Rn, Rn, 1e-5)
                        v.drain()
                        v.tensor_scalar_mul(Rx, Rn, 1.0 / 127.0)
                        v.drain()
                        v.reciprocal(tmp, Rn)
                        v.drain()
                        v.tensor_scalar_mul(Rn, tmp, 127.0).then_inc(f_row.h, 1)
                        v.drain()
                    f_row.v += 1

                    @pbld.on("tensor")
                    def _(t, rvq=rvq):
                        t.wait_ge(f_row.h, rvq)
                        t.matmul(pb[4][:, 0:TGW], onesr[:, :], rowsf[0:1, 1 * TGW:2 * TGW],
                                 start=True, stop=True)
                        t.matmul(pb[5][:, 0:TGW], onesr[:, :], rowsf[0:1, 0 * TGW:1 * TGW],
                                 start=True, stop=True).then_inc(f_mm.h, 1)
                    f_mm.v += 1
                    mm_gq = f_mm.v

                    @pbld.on("vector")
                    def _(v, mm_gq=mm_gq):
                        v.wait_ge(f_mm.h, mm_gq)
                        v.tensor_copy(rbb[:, :], pb[4][:, 0:TGW])
                        v.tensor_copy(mrbb[:, :], pb[5][:, 0:TGW]).then_inc(f_row.h, 1)
                    f_row.v += 1
                    row_gq = f_row.v

                    base_gq2 = f_gq.v

                    @pbld.on("gpsimd")
                    def _(g, row_gq=row_gq, JC=JC):
                        g.wait_ge(f_row.h, row_gq)
                        for j in range(JC):
                            g.tensor_tensor(stg2[j % 2][:, :],
                                            gbuf[:, j * TGW:(j + 1) * TGW],
                                            rbb[:, :], OP.mult)
                            g.drain()
                            g.tensor_scalar(stg2[j % 2][:, :], stg2[j % 2][:, :],
                                            MAGIC, MAGIC, OP.add, OP.subtract)
                            g.drain()
                            g.tensor_tensor(gbuf[:, j * TGW:(j + 1) * TGW],
                                            stg2[j % 2][:, :],
                                            mrbb[:, :], OP.mult).then_inc(f_gq.h, 1)
                            g.drain()
                    f_gq.v += JC
                    g_ready = ("gq", f_gq.v)
                else:
                    g_ready = ("ge", ge_all)

                base_w2l = [f_w2l[0].v, f_w2l[1].v]
                base_om = f_om.v
                base_oe = f_oe.v
                om_val = {ec: base_om + ec + 1 for ec in range(DC)}
                oe_val = {ec: base_oe + ec + 1 for ec in range(DC)}

                @pbld.on("sync")
                def _(s, base_om=base_om, w2T=w2T):
                    for ec in range(DC):
                        if ec >= 2:
                            s.wait_ge(f_om.h, base_om + ec - 1)
                        s.dma_start(out=w2Tt[ec % 2][:, :],
                                    in_=w2T[ec, :, :]).then_inc(f_w2l[ec % 2].h, 16)

                w2lv = {}
                cw = list(base_w2l)
                for ec in range(DC):
                    cw[ec % 2] += 16
                    w2lv[ec] = cw[ec % 2]
                f_w2l[0].v, f_w2l[1].v = cw[0], cw[1]

                @pbld.on("tensor")
                def _(t, w2lv=w2lv, g_ready=g_ready, base_oe=base_oe, JC=JC):
                    sem = f_gq if g_ready[0] == "gq" else f_ge
                    t.wait_ge(sem.h, g_ready[1])
                    for ec in range(DC):
                        t.wait_ge(f_w2l[ec % 2].h, w2lv[ec])
                        bank = ec % 8
                        if ec >= 8:
                            t.wait_ge(f_oe.h, base_oe + ec - 8 + 1)
                        for j in range(JC):
                            inst = t.matmul(pb[bank][:, 0:TGW],
                                            w2Tt[ec % 2][:, j * 128:(j + 1) * 128],
                                            gbuf[:, j * TGW:(j + 1) * TGW],
                                            start=(j == 0), stop=(j == JC - 1))
                        inst.then_inc(f_om.h, 1)
                f_om.v += DC

                os_val = {}
                for ec in range(DC):
                    f_os[ec % 2].v += 16
                    os_val[ec] = f_os[ec % 2].v

                @pbld.on("vector")
                def _(v, om_val=om_val, s2hc=s2hc, tg=tg, os_val=os_val):
                    for ec in range(DC):
                        v.wait_ge(f_om.h, om_val[ec])
                        if ec >= 2:
                            v.wait_ge(f_os[ec % 2].h, os_val[ec - 2])
                        v.scalar_tensor_tensor(stg2[ec % 2][:, :], pb[ec % 8][:, 0:TGW],
                                               s2hc, xres_tg[:, ec * TGW:(ec + 1) * TGW],
                                               OP.mult, OP.add).then_inc(f_oe.h, 1)
                f_oe.v += DC

                @pbld.on("scalar")
                def _(s, oe_val=oe_val, tg=tg):
                    for ec in range(DC):
                        s.wait_ge(f_oe.h, oe_val[ec])
                        s.dma_start(out=xres[tg, ec, :, :],
                                    in_=stg2[ec % 2][:, :]).then_inc(f_os[ec % 2].h, 16)

                os_prev_end = [f_os[0].v, f_os[1].v]
                new_guards = [(f_os[0], f_os[0].v), (f_os[1], f_os[1].v)]

            pbld.flush()
            arena.pop()
            xres_guards = new_guards

        a_xl = S("axl"); a_sq = S("asq"); a_row = S("arow"); a_mm = S("amm")
        a_p1 = S("ap1"); a_nq = S("anq")
        a_wl = [S("awl0"), S("awl1")]
        a_qp = S("aqp"); a_ev = S("aev")
        a_cs = [S("acs0"), S("acs1")]
        a_cc = S("acc"); a_kl = S("akl")
        a_klp = [S("aklp0"), S("aklp1")]
        a_ko = S("ako"); a_gq = S("agq"); a_kp = S("akp")
        s_sm = S("ssm"); s_ex = S("sex"); s_mk = S("smk"); s_pv = S("spv")
        s_fin = S("sfin"); s_pre = S("spre")
        s_oe = S("soe"); s_os = [S("sos0"), S("sos1")]
        s_ol = S("sol"); s_om = S("som"); s_xl2 = [S("sxl20"), S("sxl21")]

        def emit_attn(L):
            nonlocal xres_guards
            H = H0 if L == 0 else H1
            KVH = KVH0 if L == 0 else H1
            DH = DH0 if L == 0 else DH1
            CH = DH // 128
            KVW = KVH * DH
            NQG = NT // 4
            GLEN = NRANK * TLOC
            lam2 = (LAM0 * LAM0) if L == 0 else (LAM1 * LAM1)
            ccin = cc_in[L].ap()
            ccout = cc_out[L].ap()
            ktail = KVW * TLOC
            if L == 0:
                wqb, wkb, wvn, wob = din["L0_Wqb"], din["L0_Wkb"], din["L0_Wvn"], din["L0_Wob"]
            else:
                wqb, wkb, wvn, wob = din["L1_wqb"], din["L1_wkb"], din["L1_wvn"], din["L1_wob"]
            NEV = KVW // 512
            kS = ccin[0:1, 0:KVW * TLOC].rearrange("o (f t) -> (o f) t", t=TLOC)
            vS = ccin[1:2, 0:KVW * TLOC].rearrange("o (g t e) -> (o g t) e", e=DH, t=TLOC)
            koutS = [ccout[2 * r:2 * r + 1, 0:KVW * TLOC].rearrange("o (f t) -> (o f) t", t=TLOC)
                     for r in range(NRANK)]
            voutS = [ccout[2 * r + 1:2 * r + 2, 0:KVW * TLOC].rearrange("o (g t e) -> (o g t) e", e=DH, t=TLOC)
                     for r in range(NRANK)]

            arena.push()
            QO = arena.alloc([128, H * CH * TLOC], BF16)
            stg2 = [arena.alloc([128, TGW], F32) for _ in range(2)]
            stgb = [arena.alloc([128, TGW], BF16) for _ in range(2)]
            cmb = [arena.alloc([128, TGW], F32) for _ in range(2)]
            prior = list(xres_guards)

            wqt_g = {0: None, 1: None}
            stgb_g = {0: None, 1: None}
            bank_g = {0: None, 1: None}
            vbank_g = {0: None, 1: None}
            stat_g = [None]

            def W(eng, gd):
                if gd is None:
                    return
                if isinstance(gd, list):
                    for g2 in gd:
                        W(eng, g2)
                    return
                eng.wait_ge(gd[0].h, gd[1])

            arena.push()
            xres_tg = arena.alloc([128, DC * TGW], F32)
            xh_tg = arena.alloc([128, DC * TGW], BF16)
            rbb = arena.alloc([128, TGW], F32)
            mrbb = arena.alloc([128, TGW], F32)
            wqt = [arena.alloc([128, DC * 512], BF16) for _ in range(2)]
            sqs = arena.alloc([128, TGW], BF16)
            ropec = [arena.alloc([128, TGW], F32) for _ in range(2)] if L == 0 else [None] * 2
            ropes = [arena.alloc([128, TGW], F32) for _ in range(2)] if L == 0 else [None] * 2

            for tg in range(NTG):
                @pbld.on("sync")
                def _(s, tg=tg, prior=prior):
                    if tg == 0:
                        for sem, val in prior:
                            s.wait_ge(sem.h, val)
                    for dc in range(DC):
                        s.dma_start(out=xres_tg[:, dc * TGW:(dc + 1) * TGW],
                                    in_=xres[tg, dc, :, :]).then_inc(a_xl.h, 16)
                    if L == 0:
                        s.dma_start(out=ropec[0][:, :],
                                    in_=din["rope_cos"][0:128, tg * TGW:(tg + 1) * TGW]).then_inc(a_xl.h, 16)
                        s.dma_start(out=ropec[1][:, :],
                                    in_=din["rope_cos"][128:256, tg * TGW:(tg + 1) * TGW]).then_inc(a_xl.h, 16)
                        s.dma_start(out=ropes[0][:, :],
                                    in_=din["rope_sin"][0:128, tg * TGW:(tg + 1) * TGW]).then_inc(a_xl.h, 16)
                        s.dma_start(out=ropes[1][:, :],
                                    in_=din["rope_sin"][128:256, tg * TGW:(tg + 1) * TGW]).then_inc(a_xl.h, 16)
                a_xl.v += DC * 16 + (64 if L == 0 else 0)
                xlv = a_xl.v

                @pbld.on("tensor")
                def _(t, xlv=xlv):
                    t.wait_ge(a_xl.h, xlv)
                    for dc in range(DC):
                        inst = t.matmul(pb[4][0:1, 0:TGW], onesc[:, 0:1],
                                        xres_tg[:, dc * TGW:(dc + 1) * TGW],
                                        start=(dc == 0), stop=(dc == DC - 1))
                    inst.then_inc(a_mm.h, 1)
                a_mm.v += 1
                mm_mu = a_mm.v

                @pbld.on("scalar")
                def _(s, xlv=xlv):
                    s.wait_ge(a_xl.h, xlv)
                    for dc in range(DC):
                        s.square(xh_tg[:, dc * TGW:(dc + 1) * TGW],
                                 xres_tg[:, dc * TGW:(dc + 1) * TGW]).then_inc(a_sq.h, 1)
                a_sq.v += DC
                sq_x2 = a_sq.v

                @pbld.on("tensor")
                def _(t, sq_x2=sq_x2):
                    for dc in range(DC):
                        t.wait_ge(a_sq.h, sq_x2 - DC + dc + 1)
                        inst = t.matmul(pb[5][0:1, 0:TGW], onesb[:, 0:1],
                                        xh_tg[:, dc * TGW:(dc + 1) * TGW],
                                        start=(dc == 0), stop=(dc == DC - 1))
                    inst.then_inc(a_mm.h, 1)
                a_mm.v += 1
                mm_x2 = a_mm.v

                rvA, rvS, rvB = a_row.v + 1, a_row.v + 2, a_row.v + 3

                @pbld.on("vector")
                def _(v, mm_mu=mm_mu, mm_x2=mm_x2, rvS=rvS):
                    mu = rowsf[0:1, 2 * TGW:3 * TGW]
                    var = rowsf[0:1, 3 * TGW:4 * TGW]
                    r = rowsf[0:1, 4 * TGW:5 * TGW]
                    mr = rowsf[0:1, 5 * TGW:6 * TGW]
                    v.wait_ge(a_mm.h, mm_mu)
                    v.tensor_scalar_mul(mu, pb[4][0:1, 0:TGW], 1.0 / 2048.0)
                    v.wait_ge(a_mm.h, mm_x2)
                    v.tensor_scalar_mul(var, pb[5][0:1, 0:TGW], 1.0 / 2048.0)
                    v.drain()
                    v.tensor_tensor(rowsf[0:1, 6 * TGW:7 * TGW], mu, mu, OP.mult)
                    v.drain()
                    v.tensor_tensor(var, var, rowsf[0:1, 6 * TGW:7 * TGW],
                                    OP.subtract).then_inc(a_row.h, 1)
                    v.wait_ge(a_row.h, rvS)
                    v.reciprocal(r, rowsf[0:1, 7 * TGW:8 * TGW])
                    v.drain()
                    v.tensor_tensor(mr, mu, r, OP.mult).then_inc(a_row.h, 1)
                    v.drain()

                @pbld.on("scalar")
                def _(s, rvA=rvA):
                    s.wait_ge(a_row.h, rvA)
                    s.activation(rowsf[0:1, 7 * TGW:8 * TGW], rowsf[0:1, 3 * TGW:4 * TGW],
                                 AF.Sqrt, bias=sc1("eps")).then_inc(a_row.h, 1)
                a_row.v = rvB

                @pbld.on("tensor")
                def _(t, rvB=rvB):
                    t.wait_ge(a_row.h, rvB)
                    t.matmul(pb[6][:, 0:TGW], onesr[:, :], rowsf[0:1, 4 * TGW:5 * TGW],
                             start=True, stop=True)
                    t.matmul(pb[7][:, 0:TGW], onesr[:, :], rowsf[0:1, 5 * TGW:6 * TGW],
                             start=True, stop=True).then_inc(a_mm.h, 1)
                a_mm.v += 1
                mm_bc = a_mm.v

                @pbld.on("vector")
                def _(v, mm_bc=mm_bc):
                    v.wait_ge(a_mm.h, mm_bc)
                    v.tensor_copy(rbb[:, :], pb[6][:, 0:TGW])
                    v.tensor_copy(mrbb[:, :], pb[7][:, 0:TGW]).then_inc(a_row.h, 1)
                a_row.v += 1
                row_bc = a_row.v

                base_nq = a_nq.v

                @pbld.on("gpsimd")
                def _(g, row_bc=row_bc, base_nq=base_nq):
                    g.wait_ge(a_row.h, row_bc)
                    for dc in range(DC):
                        if dc >= 2:
                            g.wait_ge(a_nq.h, base_nq + dc - 1)
                        g.tensor_tensor(stg2[dc % 2][:, :],
                                        xres_tg[:, dc * TGW:(dc + 1) * TGW],
                                        rbb[:, :], OP.mult).then_inc(a_p1.h, 1)
                base_p1 = a_p1.v
                a_p1.v += DC

                @pbld.on("vector")
                def _(v, base_p1=base_p1, mm_x2=mm_x2):
                    v.wait_ge(a_mm.h, mm_x2)
                    for dc in range(DC):
                        v.wait_ge(a_p1.h, base_p1 + dc + 1)
                        v.tensor_tensor(xh_tg[:, dc * TGW:(dc + 1) * TGW],
                                        stg2[dc % 2][:, :], mrbb[:, :],
                                        OP.subtract).then_inc(a_nq.h, 1)
                a_nq.v += DC
                nq_tg = a_nq.v

                for kind, idxs, wsrc in (("q", list(range(H)), wqb),
                                         ("k", list(range(KVH)), wkb)):
                    for n_i, idx in enumerate(idxs):
                        par = (n_i + (0 if kind == "q" else H)) % 2
                        banks = [par * 2, par * 2 + 1]
                        gd_w = wqt_g[par]
                        a_wl[par].v += 16
                        wlv = a_wl[par].v

                        @pbld.on("gpsimd")
                        def _(g, idx=idx, par=par, gd_w=gd_w, wsrc=wsrc):
                            W(g, gd_w)
                            g.dma_start(out=wqt[par][:, 0:DC * DH],
                                        in_=wsrc[idx, :, :]).then_inc(a_wl[par].h, 16)

                        gd_b = bank_g[par]
                        a_qp.v += 1
                        qpv = a_qp.v

                        @pbld.on("tensor")
                        def _(t, idx=idx, par=par, banks=banks, wlv=wlv, gd_b=gd_b,
                              kind=kind, nq_tg=nq_tg):
                            t.wait_ge(a_nq.h, nq_tg)
                            t.wait_ge(a_wl[par].h, wlv)
                            W(t, gd_b)
                            for c in range(CH):
                                for dc in range(DC):
                                    last = (dc == DC - 1) and (L == 0)
                                    inst = t.matmul(pb[banks[c]][:, 0:TGW],
                                                    wqt[par][:, dc * DH + c * 128:dc * DH + c * 128 + 128],
                                                    xh_tg[:, dc * TGW:(dc + 1) * TGW],
                                                    start=(dc == 0), stop=last)
                                if L == 1:
                                    boff = 0 if kind == "q" else D
                                    br = bqr[0:1, boff + idx * 128:boff + idx * 128 + 128]
                                    inst = t.matmul(pb[banks[c]][:, 0:TGW], br,
                                                    onesrb[0:1, 0:TGW], start=False, stop=True)
                            inst.then_inc(a_qp.h, 1)

                        sqv, mmv = [], []
                        for c in range(CH):
                            a_sq.v += 1
                            sqv.append(a_sq.v)
                            a_mm.v += 1
                            mmv.append(a_mm.v)

                        @pbld.on("scalar")
                        def _(s, qpv=qpv, banks=banks, mmv=mmv):
                            s.wait_ge(a_qp.h, qpv)
                            for c in range(CH):
                                if c > 0:
                                    s.wait_ge(a_mm.h, mmv[c - 1])
                                s.square(sqs[:, :], pb[banks[c]][:, 0:TGW]).then_inc(a_sq.h, 1)
                                s.drain()

                        gd_stat = stat_g[0]

                        @pbld.on("tensor")
                        def _(t, sqv=sqv, gd_stat=gd_stat):
                            W(t, gd_stat)
                            for c in range(CH):
                                t.wait_ge(a_sq.h, sqv[c])
                                t.matmul(pb[6][0:1, 0:TGW], onesb[:, 0:1], sqs[:, :],
                                         start=(c == 0), stop=(c == CH - 1)).then_inc(a_mm.h, 1)

                        col = SC[f"qmx{idx}"] if kind == "q" else SC[f"kmx{idx}"]
                        a_row.v += 1
                        stat_g[0] = (a_row, a_row.v)

                        @pbld.on("vector")
                        def _(v, mmv=mmv, col=col, tg=tg):
                            v.wait_ge(a_mm.h, mmv[-1])
                            v.reduce_max(statt[0:1, SC["tmpr"]:SC["tmpr"] + 1],
                                         pb[6][0:1, 0:TGW], axis=AX)
                            v.drain()
                            if tg == 0:
                                v.tensor_copy(statt[0:1, col:col + 1],
                                              statt[0:1, SC["tmpr"]:SC["tmpr"] + 1]).then_inc(a_row.h, 1)
                            else:
                                v.tensor_tensor(statt[0:1, col:col + 1],
                                                statt[0:1, col:col + 1],
                                                statt[0:1, SC["tmpr"]:SC["tmpr"] + 1],
                                                OP.max).then_inc(a_row.h, 1)
                            v.drain()
                        statv = a_row.v

                        a_ev.v += 1
                        evv = a_ev.v
                        if kind == "q":
                            @pbld.on("vector")
                            def _(v, idx=idx, banks=banks, qpv=qpv, tg=tg):
                                v.wait_ge(a_qp.h, qpv)
                                if L == 0:
                                    qo0 = QO[:, (idx * CH + 0) * TLOC + tg * TGW:(idx * CH + 0) * TLOC + (tg + 1) * TGW]
                                    qo1 = QO[:, (idx * CH + 1) * TLOC + tg * TGW:(idx * CH + 1) * TLOC + (tg + 1) * TGW]
                                    v.tensor_tensor(stg2[0][:, :], pb[banks[0]][:, 0:TGW], ropec[0][:, :], OP.mult)
                                    v.tensor_tensor(stg2[1][:, :], pb[banks[1]][:, 0:TGW], ropes[0][:, :], OP.mult)
                                    v.drain()
                                    v.tensor_tensor(qo0, stg2[0][:, :], stg2[1][:, :], OP.subtract)
                                    v.drain()
                                    v.tensor_tensor(stg2[0][:, :], pb[banks[1]][:, 0:TGW], ropec[1][:, :], OP.mult)
                                    v.tensor_tensor(stg2[1][:, :], pb[banks[0]][:, 0:TGW], ropes[1][:, :], OP.mult)
                                    v.drain()
                                    inst = v.tensor_tensor(qo1, stg2[0][:, :], stg2[1][:, :], OP.add)
                                else:
                                    qo0 = QO[:, idx * TLOC + tg * TGW:idx * TLOC + (tg + 1) * TGW]
                                    inst = v.tensor_scalar_mul(qo0, pb[banks[0]][:, 0:TGW], LAM1)
                                inst.then_inc(a_ev.h, 1)
                                v.drain()
                        else:
                            kslot = 0 if CH == 2 else (n_i % 2)
                            gsd = [stgb_g[0], stgb_g[1]] if CH == 2 else [stgb_g[kslot]]
                            stv = []
                            for c in range(CH):
                                sl_ = c if CH == 2 else kslot
                                a_cs[sl_ % 2].v += 16
                                stv.append(a_cs[sl_ % 2].v)

                            @pbld.on("vector")
                            def _(v, banks=banks, qpv=qpv, gsd=gsd, kslot=kslot):
                                v.wait_ge(a_qp.h, qpv)
                                for g2 in gsd:
                                    W(v, g2)
                                if L == 0:
                                    v.tensor_tensor(stg2[0][:, :], pb[banks[0]][:, 0:TGW], ropec[0][:, :], OP.mult)
                                    v.tensor_tensor(stg2[1][:, :], pb[banks[1]][:, 0:TGW], ropes[0][:, :], OP.mult)
                                    v.drain()
                                    v.tensor_tensor(stgb[0][:, :], stg2[0][:, :], stg2[1][:, :], OP.subtract)
                                    v.drain()
                                    v.tensor_tensor(stg2[0][:, :], pb[banks[1]][:, 0:TGW], ropec[1][:, :], OP.mult)
                                    v.tensor_tensor(stg2[1][:, :], pb[banks[0]][:, 0:TGW], ropes[1][:, :], OP.mult)
                                    v.drain()
                                    inst = v.tensor_tensor(stgb[1][:, :], stg2[0][:, :], stg2[1][:, :], OP.add)
                                else:
                                    inst = v.tensor_scalar_mul(stgb[kslot][:, :], pb[banks[0]][:, 0:TGW], LAM1)
                                inst.then_inc(a_ev.h, 1)
                                v.drain()

                            @pbld.on("sync")
                            def _(s, idx=idx, evv=evv, tg=tg, kslot=kslot):
                                s.wait_ge(a_ev.h, evv)
                                for c in range(CH):
                                    dst = kS[idx * DH + c * 128:idx * DH + (c + 1) * 128,
                                             tg * TGW:(tg + 1) * TGW]
                                    s.dma_start(out=dst,
                                                in_=stgb[c if CH == 2 else kslot][0:128, :]).then_inc(a_cs[(c if CH == 2 else kslot) % 2].h, 16)
                            for c in range(CH):
                                stgb_g[(c if CH == 2 else kslot) % 2] = (a_cs[(c if CH == 2 else kslot) % 2], stv[c])

                        wqt_g[par] = (a_qp, qpv)
                        bank_g[par] = [(a_ev, evv), (a_sq, sqv[-1])]

                for ev in range(NEV):
                    par = ev % 2
                    gd_w = wqt_g[par]
                    a_wl[par].v += 16
                    wlv = a_wl[par].v

                    @pbld.on("gpsimd")
                    def _(g, ev=ev, par=par, gd_w=gd_w):
                        W(g, gd_w)
                        g.dma_start(out=wqt[par][:, 0:DC * 512],
                                    in_=wvn[ev, :, :]).then_inc(a_wl[par].h, 16)

                    qpv = None
                    for tt in range(4):
                        vbank = 4 + tt % 2
                        gd_b = vbank_g[tt % 2]
                        a_qp.v += 1
                        qpv = a_qp.v

                        @pbld.on("tensor")
                        def _(t, ev=ev, par=par, tt=tt, vbank=vbank, wlv=wlv, gd_b=gd_b,
                              nq_tg=nq_tg):
                            t.wait_ge(a_nq.h, nq_tg)
                            t.wait_ge(a_wl[par].h, wlv)
                            W(t, gd_b)
                            for dc in range(DC):
                                last = (dc == DC - 1) and (L == 0)
                                inst = t.matmul(pb[vbank][:, 0:TGW],
                                                xh_tg[:, dc * TGW + tt * 128:dc * TGW + tt * 128 + 128],
                                                wqt[par][:, dc * 512:(dc + 1) * 512],
                                                start=(dc == 0), stop=last)
                            if L == 1:
                                inst = t.matmul(pb[vbank][:, 0:TGW], onesrb[0:1, 0:128],
                                                bqr[0:1, 2 * D + ev * 512:2 * D + (ev + 1) * 512],
                                                start=False, stop=True)
                            inst.then_inc(a_qp.h, 1)

                        a_ev.v += 1
                        evv = a_ev.v
                        gsd = stgb_g[tt % 2]

                        @pbld.on("vector")
                        def _(v, qpv=qpv, vbank=vbank, tt=tt, gsd=gsd):
                            v.wait_ge(a_qp.h, qpv)
                            W(v, gsd)
                            v.tensor_copy(stgb[tt % 2][:, :], pb[vbank][:, 0:TGW]).then_inc(a_ev.h, 1)
                            v.drain()

                        nst = KVH if L == 0 else 4
                        a_cs[tt % 2].v += 16 * nst
                        stv = a_cs[tt % 2].v

                        @pbld.on("sync")
                        def _(s, ev=ev, tt=tt, evv=evv, tg=tg):
                            s.wait_ge(a_ev.h, evv)
                            if L == 0:
                                for g_ in range(KVH):
                                    dst = vS[g_ * TLOC + tg * TGW + tt * 128:
                                             g_ * TLOC + tg * TGW + (tt + 1) * 128, :]
                                    s.dma_start(out=dst,
                                                in_=stgb[tt % 2][:, g_ * DH:(g_ + 1) * DH]).then_inc(a_cs[tt % 2].h, 16)
                            else:
                                for sub in range(4):
                                    hh = ev * 4 + sub
                                    dst = vS[hh * TLOC + tg * TGW + tt * 128:
                                             hh * TLOC + tg * TGW + (tt + 1) * 128, :]
                                    s.dma_start(out=dst,
                                                in_=stgb[tt % 2][:, sub * 128:(sub + 1) * 128]).then_inc(a_cs[tt % 2].h, 16)
                        stgb_g[tt % 2] = (a_cs[tt % 2], stv)
                        vbank_g[tt % 2] = (a_ev, evv)
                    wqt_g[par] = (a_qp, qpv)

            a_row.v += 1
            rowv = a_row.v
            gsd = stgb_g[0]

            @pbld.on("vector")
            def _(v, gsd=gsd, KVH=KVH):
                W(v, gsd)
                v.memset(stgb[0][0:1, 0:128], 0.0)
                v.drain()
                for g_ in range(KVH):
                    v.tensor_copy(stgb[0][0:1, g_:g_ + 1],
                                  statt[0:1, SC[f"kmx{g_}"]:SC[f"kmx{g_}"] + 1])
                v.drain()
                v.sem_inc(a_row.h, 1)

            a_cs[0].v += 32
            cstv = a_cs[0].v

            @pbld.on("sync")
            def _(s, rowv=rowv):
                s.wait_ge(a_row.h, rowv)
                s.dma_start(out=ccin[0:1, ktail:ktail + 128],
                            in_=stgb[0][0:1, 0:128]).then_inc(a_cs[0].h, 16)
                s.dma_start(out=ccin[1:2, ktail:ktail + 128],
                            in_=stgb[0][0:1, 0:128]).then_inc(a_cs[0].h, 16)
            stgb_g[0] = (a_cs[0], cstv)

            a_cc.v += 1
            ccv = a_cc.v

            @pbld.on("gpsimd")
            def _(g, cs0=a_cs[0].v, cs1=a_cs[1].v):
                g.wait_ge(a_cs[0].h, cs0)
                g.wait_ge(a_cs[1].h, cs1)
                g.collective_compute(
                    "AllGather", OP.bypass,
                    replica_groups=[[0, 1, 2, 3], [4, 5, 6, 7]],
                    ins=[ccin[:, :]], outs=[ccout[:, :]],
                ).then_inc(a_cc.h, 1)

            a_kl.v += 16 * NRANK
            klv = a_kl.v

            @pbld.on("sync")
            def _(s, ccv=ccv):
                s.wait_ge(a_cc.h, ccv)
                for r_ in range(NRANK):
                    s.dma_start(out=stgb[0][0:1, r_ * 128:(r_ + 1) * 128],
                                in_=ccout[2 * r_:2 * r_ + 1, ktail:ktail + 128]).then_inc(a_kl.h, 16)

            a_row.v += 1
            tails_done = a_row.v

            @pbld.on("vector")
            def _(v, klv=klv, KVH=KVH):
                v.wait_ge(a_kl.h, klv)
                for g_ in range(KVH):
                    cg = SC[f"kmx{g_}"]
                    v.tensor_copy(statt[0:1, cg:cg + 1], stgb[0][0:1, g_:g_ + 1])
                    v.drain()
                    for r_ in range(1, NRANK):
                        v.tensor_tensor(statt[0:1, cg:cg + 1], statt[0:1, cg:cg + 1],
                                        stgb[0][0:1, r_ * 128 + g_:r_ * 128 + g_ + 1], OP.max)
                        v.drain()
                v.memset(statt[:, SC["zero"]:SC["zero"] + 1], 0.0)
                v.sem_inc(a_row.h, 1)

            arena.pop()

            arena.push()
            nkv = 1 if L == 0 else 2
            Kh = [arena.alloc([128, CH * GLEN], BF16) for _ in range(nkv)]
            Vh = [arena.alloc([128, (GLEN // 128) * DH], BF16) for _ in range(nkv)]
            Pt = [arena.alloc([128, TGW], BF16) for _ in range(2)]
            if L == 0:
                Pg = [arena.alloc([128, TGW], BF16) for _ in range(2)]
                kplT = arena.alloc([2, GLEN], BF16)
                gqr = arena.alloc([2, TLOC], BF16)
                strow = arena.alloc([1, TLOC], BF16)

            if L == 0:
                a_ko.v += 16
                rowk1 = a_ko.v

                @pbld.on("sync")
                def _(s, ccv=ccv):
                    s.wait_ge(a_cc.h, ccv)
                    s.dma_start(out=kplT[1:2, :], in_=din["onesd"][0:1, 0:GLEN]).then_inc(a_ko.h, 16)
            else:
                rowk1 = 0

            def chunks_for(qg):
                out = []
                for j in range(NRANK):
                    kmax = (4 * qg + 3) if L == 0 else (NT - 1)
                    for kp_ in range(kmax + 1):
                        qoff = (kp_ - 4 * qg) * 128 if (L == 0 and kp_ >= 4 * qg) else 0
                        out.append((j, kp_, qoff, 512 - qoff))
                return out

            kv_groups = list(range(KVH0)) if L == 0 else list(range(H1))
            kvld_g = {0: None, 1: None}
            for gi, g_ in enumerate(kv_groups):
                kpar = gi % nkv
                gd_kv = kvld_g[kpar]
                a_klp[kpar].v += 16 * (CH * NRANK + NRANK * NT)
                klv2 = a_klp[kpar].v

                @pbld.on("sync")
                def _(s, g_=g_, kpar=kpar, gd_kv=gd_kv, ccv=ccv):
                    s.wait_ge(a_cc.h, ccv)
                    W(s, gd_kv)
                    for c in range(CH):
                        for r_ in range(NRANK):
                            s.dma_start(out=Kh[kpar][:, c * GLEN + r_ * TLOC:c * GLEN + (r_ + 1) * TLOC],
                                        in_=koutS[r_][g_ * DH + c * 128:g_ * DH + (c + 1) * 128, :]).then_inc(a_klp[kpar].h, 16)
                    for r_ in range(NRANK):
                        for kp_ in range(NT):
                            vt = r_ * NT + kp_
                            s.dma_start(out=Vh[kpar][:, vt * DH:(vt + 1) * DH],
                                        in_=voutS[r_][g_ * TLOC + kp_ * 128:g_ * TLOC + (kp_ + 1) * 128, :]).then_inc(a_klp[kpar].h, 16)

                qheads = [4 * g_ + u for u in range(4)] if L == 0 else [g_]
                for h in qheads:
                    negc = SC["negm0"] if h % 2 == 0 else SC["negm1"]
                    a_row.v += 1
                    prow = a_row.v

                    @pbld.on("vector")
                    def _(v, h=h, g_=g_, tails_done=tails_done, lam2=lam2, finv=s_fin.v):
                        v.wait_ge(a_row.h, tails_done)
                        v.wait_ge(s_fin.h, finv)
                        v.tensor_tensor(statt[0:1, SC["tmpr"]:SC["tmpr"] + 1],
                                        statt[0:1, SC[f"qmx{h}"]:SC[f"qmx{h}"] + 1],
                                        statt[0:1, SC[f"kmx{g_}"]:SC[f"kmx{g_}"] + 1], OP.add)
                        v.drain()
                        v.tensor_scalar_mul(statt[0:1, SC["tmpr"]:SC["tmpr"] + 1],
                                            statt[0:1, SC["tmpr"]:SC["tmpr"] + 1],
                                            -0.5 * lam2).then_inc(a_row.h, 1)
                        v.drain()

                    a_mm.v += 1
                    pmm = a_mm.v

                    @pbld.on("tensor")
                    def _(t, prow=prow, finv=s_fin.v):
                        t.wait_ge(a_row.h, prow)
                        t.wait_ge(s_fin.h, finv)
                        t.matmul(pb[7][:, 0:1], onesr[:, :],
                                 statt[0:1, SC["tmpr"]:SC["tmpr"] + 1],
                                 start=True, stop=True).then_inc(a_mm.h, 1)

                    a_row.v += 1
                    prow2 = a_row.v

                    @pbld.on("vector")
                    def _(v, pmm=pmm, negc=negc):
                        v.wait_ge(a_mm.h, pmm)
                        v.tensor_copy(statt[:, negc:negc + 1], pb[7][:, 0:1]).then_inc(a_row.h, 1)
                        v.drain()

                    gqrow = 0
                    gqdma_l = [0]
                    if L == 0:
                        nsl = GLEN // TGW
                        kpmm0 = a_mm.v
                        a_mm.v += nsl

                        base_kp = a_kp.v

                        @pbld.on("tensor")
                        def _(t, h=h, kpar=kpar, klv2=klv2, finv=s_fin.v, rowk1=rowk1,
                              prow2=prow2, base_kp=base_kp):
                            t.wait_ge(a_klp[kpar].h, klv2)
                            t.wait_ge(a_ko.h, rowk1)
                            t.wait_ge(a_row.h, prow2)
                            t.wait_ge(s_fin.h, finv)
                            for sl in range(GLEN // TGW):
                                if sl >= 2:
                                    t.wait_ge(a_kp.h, base_kp + sl - 1)
                                t.matmul(pb[6 + sl % 2][0:1, 0:TGW], dirst[:, h:h + 1],
                                         Kh[kpar][0:3, sl * TGW:(sl + 1) * TGW],
                                         start=True, stop=True).then_inc(a_mm.h, 1)

                        a_row.v += 1
                        kprow = a_row.v

                        @pbld.on("vector")
                        def _(v, kpmm0=kpmm0):
                            for sl in range(GLEN // TGW):
                                v.wait_ge(a_mm.h, kpmm0 + sl + 1)
                                v.tensor_copy(kplT[0:1, sl * TGW:(sl + 1) * TGW],
                                              pb[6 + sl % 2][0:1, 0:TGW]).then_inc(a_kp.h, 1)
                            v.drain()
                            v.reduce_max(statt[0:1, SC["tmpr"]:SC["tmpr"] + 1],
                                         kplT[0:1, :], axis=AX)
                            v.tensor_reduce(statt[0:1, SC["tmpr"] + 1:SC["tmpr"] + 2],
                                            kplT[0:1, :], axis=AX, op=OP.min)
                            v.drain()
                            v.sem_inc(a_row.h, 1)
                        a_kp.v += GLEN // TGW

                        nql = TLOC // TGW
                        qpmm0 = a_mm.v
                        a_mm.v += nql

                        base_qp2 = a_kp.v

                        @pbld.on("tensor")
                        def _(t, h=h, qpmm0=qpmm0, kprow=kprow, base_qp2=base_qp2):
                            t.wait_ge(a_row.h, kprow)
                            for sl in range(TLOC // TGW):
                                if sl >= 1:
                                    t.wait_ge(a_kp.h, base_qp2 + sl)
                                t.matmul(pb[6][0:1, 0:TGW], dirst[:, h:h + 1],
                                         QO[0:3, (h * CH) * TLOC + sl * TGW:(h * CH) * TLOC + (sl + 1) * TGW],
                                         start=True, stop=True).then_inc(a_mm.h, 1)

                        a_row.v += 1
                        gqrow = a_row.v

                        @pbld.on("vector")
                        def _(v, qpmm0=qpmm0):
                            qp = rowsf[0:1, 0:TGW]
                            mg = rowsf[0:1, TGW:2 * TGW]
                            t2 = rowsf[0:1, 2 * TGW:3 * TGW]
                            for sl in range(TLOC // TGW):
                                v.wait_ge(a_mm.h, qpmm0 + sl + 1)
                                v.tensor_copy(qp, pb[6][0:1, 0:TGW]).then_inc(a_kp.h, 1)
                                v.drain()
                                v.tensor_single_scalar(mg, qp, statt[0:1, SC["tmpr"]:SC["tmpr"] + 1], OP.mult)
                                v.tensor_single_scalar(t2, qp, statt[0:1, SC["tmpr"] + 1:SC["tmpr"] + 2], OP.mult)
                                v.drain()
                                v.tensor_tensor(mg, mg, t2, OP.max)
                                v.drain()
                                v.tensor_scalar_mul(mg, mg, -1.0)
                                v.tensor_copy(gqr[0:1, sl * TGW:(sl + 1) * TGW], qp)
                                v.drain()
                                v.tensor_copy(strow[0:1, sl * TGW:(sl + 1) * TGW], mg)
                                v.drain()
                            v.sem_inc(a_row.h, 1)

                        a_kp.v += TLOC // TGW
                        a_gq.v += 16
                        gqdma = a_gq.v

                        @pbld.on("sync")
                        def _(s, gqrow_v=a_row.v):
                            s.wait_ge(a_row.h, gqrow_v)
                            s.dma_start(out=gqr[1:2, :], in_=strow[0:1, :]).then_inc(a_gq.h, 16)
                        gqdma_l[0] = gqdma

                    for qg in range(NQG):
                        chl = chunks_for(qg)
                        nch = len(chl)
                        smv, exv, mkv, pvv = {}, {}, {}, {}
                        for ci, (j, kp_, qoff, N) in enumerate(chl):
                            s_sm.v += 1
                            smv[ci] = s_sm.v
                            s_ex.v += 2 if L == 0 else 1
                            exv[ci] = s_ex.v
                            if L == 0 and kp_ >= 4 * qg:
                                s_mk.v += 2
                                mkv[ci] = s_mk.v
                            s_pv.v += 1
                            pvv[ci] = s_pv.v
                        finv0 = s_fin.v
                        s_fin.v += 1
                        finv2 = s_fin.v

                        @pbld.on("tensor")
                        def _(t, h=h, g_=g_, qg=qg, chl=chl, smv=smv, exv=exv, mkv=mkv,
                              pvv=pvv, kpar=kpar, klv2=klv2, prow2=prow2, gqrow=gqrow,
                              finv0=finv0, nch=nch, gqdma_l=tuple(gqdma_l)):
                            qbase = qg * 512

                            def pv(ci):
                                j, kp_, qoff, N = chl[ci]
                                tie = (L == 0 and kp_ >= 4 * qg)
                                if tie:
                                    t.wait_ge(s_mk.h, mkv[ci])
                                else:
                                    t.wait_ge(s_ex.h, exv[ci])
                                vt = j * NT + kp_
                                par = ci % 2
                                first = (ci == 0)
                                lastc = (ci == nch - 1)
                                for c in range(CH):
                                    t.matmul(pb[c][0:128, qoff:qoff + N],
                                             Vh[kpar][:, vt * DH + c * 128:vt * DH + (c + 1) * 128],
                                             Pt[par][0:128, 0:N], start=first, stop=lastc)
                                inst = t.matmul(pb[6][0:1, qoff:qoff + N], onesb[:, 0:1],
                                                Pt[par][0:128, 0:N], start=first, stop=lastc)
                                if L == 0:
                                    for c in range(CH):
                                        t.matmul(pb[2 + c][0:128, qoff:qoff + N],
                                                 Vh[kpar][:, vt * DH + c * 128:vt * DH + (c + 1) * 128],
                                                 Pg[par][0:128, 0:N], start=first, stop=lastc)
                                    inst = t.matmul(pb[7][0:1, qoff:qoff + N], onesb[:, 0:1],
                                                    Pg[par][0:128, 0:N], start=first, stop=lastc)
                                inst.then_inc(s_pv.h, 1)

                            t.wait_ge(a_klp[kpar].h, klv2)
                            t.wait_ge(a_row.h, max(prow2, gqrow))
                            if L == 0:
                                t.wait_ge(a_gq.h, gqdma_l[0])
                            t.wait_ge(s_fin.h, finv0)
                            for ci, (j, kp_, qoff, N) in enumerate(chl):
                                sb = 4 if L == 0 else 4 + ci % 2
                                back = 1 if L == 0 else 2
                                if ci >= back:
                                    t.wait_ge(s_ex.h, exv[ci - back])
                                scol = j * TLOC + kp_ * 128
                                for c in range(CH):
                                    inst = t.matmul(
                                        pb[sb][0:128, 0:N],
                                        Kh[kpar][:, c * GLEN + scol:c * GLEN + scol + 128],
                                        QO[:, (h * CH + c) * TLOC + qbase + qoff:(h * CH + c) * TLOC + qbase + qoff + N],
                                        start=(c == 0), stop=(c == CH - 1))
                                if L == 0:
                                    inst = t.matmul(pb[5][0:128, 0:N],
                                                    kplT[:, scol:scol + 128],
                                                    gqr[:, qbase + qoff:qbase + qoff + N],
                                                    start=True, stop=True)
                                inst.then_inc(s_sm.h, 1)
                                if ci >= 1:
                                    pv(ci - 1)
                            pv(nch - 1)

                        @pbld.on("scalar")
                        def _(s, chl=chl, smv=smv, pvv=pvv, negc=negc, nch=nch, qg=qg):
                            for ci, (j, kp_, qoff, N) in enumerate(chl):
                                par = ci % 2
                                if ci >= 2:
                                    s.wait_ge(s_pv.h, pvv[ci - 2])
                                s.wait_ge(s_sm.h, smv[ci])
                                s.activation(Pt[par][0:128, 0:N],
                                             pb[4 if L == 0 else 4 + ci % 2][0:128, 0:N],
                                             AF.Exp, bias=statt[0:128, negc:negc + 1],
                                             scale=1.0).then_inc(s_ex.h, 1)
                                if L == 0:
                                    s.activation(Pg[par][0:128, 0:N], pb[5][0:128, 0:N],
                                                 AF.Exp,
                                                 bias=statt[0:128, SC["zero"]:SC["zero"] + 1],
                                                 scale=1.0).then_inc(s_ex.h, 1)

                        if L == 0:
                            @pbld.on("vector")
                            def _(v, chl=chl, exv=exv, mkv=mkv, qg=qg):
                                for ci, (j, kp_, qoff, N) in enumerate(chl):
                                    if kp_ < 4 * qg:
                                        continue
                                    par = ci % 2
                                    v.wait_ge(s_ex.h, exv[ci])
                                    v.tensor_tensor(Pt[par][0:128, 0:128], Pt[par][0:128, 0:128],
                                                    tiet[:, j * 128:(j + 1) * 128], OP.mult).then_inc(s_mk.h, 1)
                                    v.tensor_tensor(Pg[par][0:128, 0:128], Pg[par][0:128, 0:128],
                                                    tiet[:, j * 128:(j + 1) * 128], OP.mult).then_inc(s_mk.h, 1)
                                    v.drain()

                        a_mm.v += 1
                        fbcm = a_mm.v
                        a_row.v += 1
                        frow = a_row.v

                        @pbld.on("vector")
                        def _(v, h=h, pvend=pvv[nch - 1], frow=frow):
                            v.wait_ge(s_pv.h, pvend)
                            la = rowsf[0:1, 0 * TGW:1 * TGW]
                            lb = rowsf[0:1, 1 * TGW:2 * TGW]
                            v.tensor_copy(la, pb[6][0:1, 0:TGW])
                            if L == 0:
                                v.tensor_copy(lb, pb[7][0:1, 0:TGW])
                            v.drain()
                            v.reciprocal(la, la)
                            if L == 0:
                                v.reciprocal(lb, lb)
                            v.drain()
                            if L == 0:
                                v.tensor_single_scalar(la, la, hst[0:1, 8 + h:9 + h], OP.mult)
                                v.tensor_single_scalar(lb, lb, hst[0:1, h:h + 1], OP.mult)
                                v.drain()
                            v.sem_inc(a_row.h, 1)

                        @pbld.on("tensor")
                        def _(t, frow=frow):
                            t.wait_ge(a_row.h, frow)
                            inst = t.matmul(pb[4][:, 0:TGW], onesr[:, :],
                                            rowsf[0:1, 0 * TGW:1 * TGW], start=True, stop=True)
                            if L == 0:
                                inst = t.matmul(pb[5][:, 0:TGW], onesr[:, :],
                                                rowsf[0:1, 1 * TGW:2 * TGW], start=True, stop=True)
                            inst.then_inc(a_mm.h, 1)

                        @pbld.on("vector")
                        def _(v, h=h, qg=qg, fbcm=fbcm):
                            v.wait_ge(a_mm.h, fbcm)
                            v.tensor_copy(stg2[0][:, :], pb[4][:, 0:TGW])
                            if L == 0:
                                v.tensor_copy(stg2[1][:, :], pb[5][:, 0:TGW])
                            v.drain()
                            for c in range(CH):
                                qo = QO[:, (h * CH + c) * TLOC + qg * TGW:(h * CH + c) * TLOC + (qg + 1) * TGW]
                                if L == 0:
                                    v.tensor_tensor(cmb[0][:, :], pb[c][:, 0:TGW], stg2[0][:, :], OP.mult)
                                    v.tensor_tensor(cmb[1][:, :], pb[2 + c][:, 0:TGW], stg2[1][:, :], OP.mult)
                                    v.drain()
                                    inst = v.tensor_tensor(qo, cmb[0][:, :], cmb[1][:, :], OP.add)
                                    v.drain()
                                else:
                                    inst = v.tensor_tensor(qo, pb[0][:, 0:TGW], stg2[0][:, :], OP.mult)
                                    v.drain()
                            inst.then_inc(s_fin.h, 1)

                kvld_g[kpar] = (s_fin, s_fin.v)

            arena.pop()
            arena.push()
            wot = [arena.alloc([128, DC * 128], BF16) for _ in range(2)]
            wot_g = {0: None, 1: None}
            fin_all = s_fin.v
            xl_g = {0: None, 1: None}
            ob_g = {}
            for ec in range(DC):
                par = ec % 2
                gd_w = wot_g[par]
                a_wl[par].v += 16
                wlv = a_wl[par].v

                @pbld.on("gpsimd")
                def _(g, ec=ec, par=par, gd_w=gd_w, fin_all=fin_all):
                    g.wait_ge(s_fin.h, fin_all)
                    W(g, gd_w)
                    g.dma_start(out=wot[par][:, 0:DC * 128],
                                in_=wob[ec, :, :]).then_inc(a_wl[par].h, 16)

                for tg in range(NTG):
                    u = ec * NTG + tg
                    xpar = u % 2
                    gd_x = xl_g[xpar]
                    s_xl2[xpar].v += 16
                    xlv2 = s_xl2[xpar].v

                    @pbld.on("sync")
                    def _(s, ec=ec, tg=tg, xpar=xpar, gd_x=gd_x, fin_all=fin_all):
                        s.wait_ge(s_fin.h, fin_all)
                        W(s, gd_x)
                        s.dma_start(out=stg2[xpar][:, :],
                                    in_=xres[tg, ec, :, :]).then_inc(s_xl2[xpar].h, 16)

                    obank = 4 + u % 4
                    gd_ob = ob_g.get(u % 4)
                    s_om.v += 1
                    omv = s_om.v

                    @pbld.on("tensor")
                    def _(t, ec=ec, tg=tg, par=par, wlv=wlv, obank=obank, gd_ob=gd_ob):
                        t.wait_ge(a_wl[par].h, wlv)
                        W(t, gd_ob)
                        for cc in range(DC):
                            last = (cc == DC - 1) and (L == 0)
                            inst = t.matmul(pb[obank][:, 0:TGW],
                                            wot[par][:, cc * 128:(cc + 1) * 128],
                                            QO[:, cc * TLOC + tg * TGW:cc * TLOC + (tg + 1) * TGW],
                                            start=(cc == 0), stop=last)
                        if L == 1:
                            inst = t.matmul(pb[obank][:, 0:TGW], bor[0:1, ec * 128:(ec + 1) * 128],
                                            onesrb[0:1, 0:TGW], start=False, stop=True)
                        inst.then_inc(s_om.h, 1)

                    s_oe.v += 1
                    oev = s_oe.v
                    gd_st = stgb_g[xpar]

                    @pbld.on("vector")
                    def _(v, omv=omv, obank=obank, xpar=xpar, xlv2=xlv2, gd_st=gd_st):
                        v.wait_ge(s_om.h, omv)
                        v.wait_ge(s_xl2[xpar].h, xlv2)
                        W(v, gd_st)
                        v.tensor_tensor(cmb[xpar][:, :],
                                        pb[obank][:, 0:TGW], stg2[xpar][:, :],
                                        OP.add).then_inc(s_oe.h, 1)
                        v.drain()

                    s_os[xpar].v += 16
                    osv = s_os[xpar].v

                    @pbld.on("scalar")
                    def _(s, ec=ec, tg=tg, xpar=xpar, oev=oev):
                        s.wait_ge(s_oe.h, oev)
                        s.dma_start(out=xres[tg, ec, :, :],
                                    in_=cmb[xpar][:, :]).then_inc(s_os[xpar].h, 16)

                    xl_g[xpar] = (s_oe, oev)
                    ob_g[u % 4] = (s_oe, oev)
                    stgb_g[xpar] = (s_os[xpar], osv)
                wot_g[par] = (s_om, s_om.v)

            pbld.flush()
            arena.pop()
            arena.pop()
            xres_guards = [(s_os[0], s_os[0].v), (s_os[1], s_os[1].v)]

        sub = 0
        def prep_prior():
            return list(xres_guards) + [(f_ts[0], f_ts[0].v), (f_ts[1], f_ts[1].v)]

        if n_sublayers >= 1:
            emit_quant_prep(0, prior=prep_prior())
            emit_ffn(0, "f00")
        if n_sublayers >= 2:
            emit_attn(0)
        if n_sublayers >= 3:
            emit_ffn(0, "f01")
        if n_sublayers >= 4:
            emit_quant_prep(1, prior=prep_prior())
            emit_ffn(1, "f10")
        if n_sublayers >= 5:
            emit_attn(1)
        if n_sublayers >= 6:
            emit_ffn(1, "f11")

        arena.push()
        fmt = [arena.alloc([128, TGW], F32) for _ in range(2)]
        ytiles = [arena.alloc([128, D], F32) for _ in range(4)]
        p7mm, p7cp, yst = S("p7mm"), S("p7cp"), S("yst")
        fld = [S("fld0"), S("fld1")]

        @pbld.on("sync")
        def _(s):
            for sem, val in xres_guards:
                s.wait_ge(sem.h, val)
            for tg in range(NTG):
                for dc in range(DC):
                    i = tg * DC + dc
                    if i >= 2:
                        s.wait_ge(p7mm.h, i - 1)
                    fld[i % 2].v += 16
                    s.dma_start(out=fmt[i % 2][:, :],
                                in_=xres[tg, dc, :, :]).then_inc(fld[i % 2].h, 16)
                for tt in range(4):
                    k = tg * 4 + tt
                    s.wait_ge(p7cp.h, (tg + 1) * DC * 4)
                    yst.v += 16
                    s.dma_start(out=yout[k * 128:(k + 1) * 128, :],
                                in_=ytiles[tt][:, :]).then_inc(yst.h, 16)
            s.wait_ge(yst.h, yst.v)

        @pbld.on("tensor")
        def _(t):
            for tg in range(NTG):
                for dc in range(DC):
                    i = tg * DC + dc
                    t.wait_ge(fld[i % 2].h, 16 * (i // 2 + 1))
                    if i >= 4:
                        t.wait_ge(p7cp.h, (i - 3) * 4)
                    for tt in range(4):
                        inst = t.transpose(pb[4 + i % 4][:, tt * 128:(tt + 1) * 128],
                                           fmt[i % 2][:, tt * 128:(tt + 1) * 128],
                                           identt[:, :])
                    inst.then_inc(p7mm.h, 1)
        p7mm.v = NTG * DC

        @pbld.on("vector")
        def _(v):
            for tg in range(NTG):
                if tg >= 1:
                    v.wait_ge(yst.h, tg * 4 * 16)
                for dc in range(DC):
                    i = tg * DC + dc
                    v.wait_ge(p7mm.h, i + 1)
                    for tt in range(4):
                        v.tensor_copy(ytiles[tt][:, dc * 128:(dc + 1) * 128],
                                      pb[4 + i % 4][:, tt * 128:(tt + 1) * 128]).then_inc(p7cp.h, 1)
        p7cp.v = NTG * DC * 4
        pbld.flush()
        arena.pop()

    return nc


_CACHE = {}

def _get_nc(n_sublayers):
    if n_sublayers not in _CACHE:
        _CACHE[n_sublayers] = build_program(n_sublayers)
    return _CACHE[n_sublayers]


def _host_constants(rank):
    tiles = np.arange(NT) * 4 + rank
    pos = (tiles[:, None] * 128 + np.arange(128)[None, :]).reshape(-1)
    inv = (1.0 / (10000.0 ** (np.arange(0, DH0, 2, dtype=np.float32) / np.float32(DH0)))).astype(np.float32)
    f = (pos.astype(np.float32)[:, None] * inv[None, :]).astype(np.float32)
    cos = np.cos(f.astype(np.float64)).astype(np.float32)
    sin = np.sin(f.astype(np.float64)).astype(np.float32)
    cos = np.ascontiguousarray(np.concatenate([cos, cos], axis=1).T) * np.float32(LAM0)
    sin = np.ascontiguousarray(np.concatenate([sin, sin], axis=1).T) * np.float32(LAM0)
    trig = np.array([[sx, sy, sz] for sx in (1.0, -1.0) for sy in (1.0, -1.0) for sz in (1.0, -1.0)],
                    dtype=np.float32) / np.sqrt(3.0, dtype=np.float32)
    dirs = np.ascontiguousarray(trig.T / np.float32(LAM0)).astype(BF)
    tri = np.triu(np.ones((128, 128), dtype=np.float32))
    tie = np.zeros((NRANK, 128, 128), dtype=np.float32)
    for j in range(NRANK):
        if j < rank:
            tie[j] = 1.0
        elif j == rank:
            tie[j] = tri
    tie = np.ascontiguousarray(tie.transpose(1, 0, 2).reshape(128, NRANK * 128))
    return {
        "ident": np.eye(128, dtype=np.float32),
        "identb": np.eye(128, dtype=BF),
        "onesd": np.ones((1, 8192), dtype=BF),
        "rope_cos": cos, "rope_sin": sin,
        "dirs": dirs, "tiemask": tie.astype(BF),
    }


def kernel(L0_attn_x, L0_attn_Wq, L0_attn_Wk, L0_attn_Wv, L0_attn_Wo, L0_attn_head_scales,
           L0_ffn_w1, L0_ffn_w3, L0_ffn_w2,
           L1_attn_wq, L1_attn_bq, L1_attn_wk, L1_attn_bk, L1_attn_wv, L1_attn_bv,
           L1_attn_wo, L1_attn_bo, L1_ffn_w1, L1_ffn_w2, L1_ffn_w3):
    x = np.ascontiguousarray(np.asarray(L0_attn_x, dtype=np.float32))
    f32 = lambda a: np.ascontiguousarray(np.asarray(a, dtype=np.float32))

    def lhs_block(w, E):
        ww = f32(w).reshape(DC, 128, w.shape[1] // E, E)
        return np.ascontiguousarray(ww.transpose(2, 1, 0, 3).reshape(-1, 128, DC * E))

    def rhs_block(w):
        ww = f32(w).reshape(DC, 128, w.shape[1] // 512, 512)
        return np.ascontiguousarray(ww.transpose(2, 0, 1, 3).reshape(-1, 128 * 0 + 128, 0 * 512 + 512)) \
            if False else np.ascontiguousarray(ww.transpose(2, 0, 1, 3)).reshape(w.shape[1] // 512, 128, DC * 512) \
            if False else np.ascontiguousarray(ww.transpose(2, 0, 1, 3))

    def rhsb(w):
        ww = f32(w).reshape(DC, 128, w.shape[1] // 512, 512)
        return np.ascontiguousarray(ww.transpose(2, 1, 0, 3).reshape(-1, 128, DC * 512))

    weights = {
        "L0_Wqb": lhs_block(L0_attn_Wq, DH0), "L0_Wkb": lhs_block(L0_attn_Wk, DH0),
        "L0_Wvn": rhsb(L0_attn_Wv), "L0_Wob": lhs_block(L0_attn_Wo, 128),
        "L0_hs": f32(L0_attn_head_scales).reshape(1, 8),
        "L0_w1": f32(L0_ffn_w1), "L0_w3": f32(L0_ffn_w3), "L0_w2": f32(L0_ffn_w2),
        "L1_wqb": lhs_block(L1_attn_wq, DH1), "L1_wkb": lhs_block(L1_attn_wk, DH1),
        "L1_wvn": rhsb(L1_attn_wv), "L1_wob": lhs_block(L1_attn_wo, 128),
        "L1_bq": f32(L1_attn_bq).reshape(1, D), "L1_bk": f32(L1_attn_bk).reshape(1, D),
        "L1_bv": f32(L1_attn_bv).reshape(1, D), "L1_bo": f32(L1_attn_bo).reshape(1, D),
        "L1_w1": f32(L1_ffn_w1), "L1_w3": f32(L1_ffn_w3), "L1_w2": f32(L1_ffn_w2),
    }
    nc = _get_nc(_N_SUBLAYERS)
    in_maps = []
    for c in range(NCORE):
        b, r = c // NRANK, c % NRANK
        tiles = np.arange(NT) * 4 + r
        xloc = np.ascontiguousarray(x[b].reshape(64, 128, D)[tiles].reshape(TLOC, D))
        m = {"xloc": xloc}
        m.update(weights)
        m.update(_host_constants(r))
        in_maps.append(m)
    res = run_bass_kernel_spmd(nc, in_maps, core_ids=list(range(NCORE)))
    out = np.empty((B, T, D), dtype=np.float32)
    for c in range(NCORE):
        b, r = c // NRANK, c % NRANK
        tiles = np.arange(NT) * 4 + r
        out[b].reshape(64, 128, D)[tiles] = res.results[c]["yout"].reshape(NT, 128, D)
    return out


# version 02 (final)
# speedup vs baseline: 1.6791x; 1.6791x over previous
import contextlib
import math
import numpy as np
import ml_dtypes
import concourse.bass as bass
import concourse.mybir as mybir
from concourse.bass_utils import run_bass_kernel_spmd

F32 = mybir.dt.float32
BF16 = mybir.dt.bfloat16
U8 = mybir.dt.uint8
BF = ml_dtypes.bfloat16
AF = mybir.ActivationFunctionType
OP = mybir.AluOpType
AX = mybir.AxisListType.X
AXC = mybir.AxisListType.C
MAGIC = 1.5 * 2.0 ** 23

B, T, D = 2, 8192, 2048
NCORE, NRANK = 8, 4
NT = 16
TLOC = NT * 128
DC = D // 128
NTG, TGW = 4, 512
H0, KVH0, DH0 = 8, 2, 256
LAM0 = 0.25
H1, DH1 = 16, 128
LAM1 = float(2048.0 ** -0.25)
J0, JC0 = 5461, 43
J1, JC1 = 5632, 44
ENGS = ("sync", "tensor", "vector", "scalar", "gpsimd")

_N_SUBLAYERS = 6


class Sem:
    def __init__(self, nc, es, name):
        self.h = es.enter_context(nc.semaphore(name))
        self.v = 0


class PB:
    def __init__(self, block):
        self.block = block
        self.qs = {e: [] for e in ENGS}

    def on(self, eng):
        def deco(fn):
            self.qs[eng].append(fn)
        return deco

    def flush(self):
        for eng in ENGS:
            fns = self.qs[eng]
            if not fns:
                continue
            def run(E, fns=fns):
                for f in fns:
                    f(E)
            getattr(self.block, eng)(run)
        self.qs = {e: [] for e in ENGS}


class Arena:
    def __init__(self, nc, es, name, nbytes):
        self.t = es.enter_context(nc.sbuf_tensor(name, [128, nbytes], U8))
        self.off = 0
        self.nbytes = nbytes
        self.marks = []

    def alloc(self, shape, dtype, align=64):
        esz = {F32: 4, BF16: 2, U8: 1}[dtype]
        self.off = (self.off + align - 1) // align * align
        nb = shape[1] * esz
        assert self.off + nb <= self.nbytes, f"arena overflow: {self.off}+{nb}>{self.nbytes}"
        v = self.t[0:shape[0], self.off:self.off + nb].bitcast(dtype)
        self.off += nb
        return v

    def push(self):
        self.marks.append(self.off)

    def pop(self):
        self.off = self.marks.pop()


def build_program(n_sublayers):
    nc = bass.Bass("TRN2", target_bir_lowering=False, debug=False)

    din = {}
    def dram_in(name, shape, dtype=F32):
        din[name] = nc.dram_tensor(name, list(shape), dtype, kind="ExternalInput").ap()
        return din[name]

    xin = dram_in("xloc", [TLOC, D])
    dram_in("L0_Wqb", [H0, 128, DC * DH0])
    dram_in("L0_Wkb", [KVH0, 128, DC * DH0])
    dram_in("L0_Wvn", [KVH0 * DH0 // 512, 128, DC * 512])
    dram_in("L0_Wob", [DC, 128, DC * 128])
    dram_in("L0_hs", [1, 8])
    dram_in("L0_w1", [J0, D]); dram_in("L0_w3", [J0, D]); dram_in("L0_w2", [D, J0])
    dram_in("L1_wqb", [H1, 128, DC * DH1])
    dram_in("L1_wkb", [H1, 128, DC * DH1])
    dram_in("L1_wvn", [D // 512, 128, DC * 512])
    dram_in("L1_wob", [DC, 128, DC * 128])
    for b_ in ("bq", "bk", "bv", "bo"):
        dram_in("L1_" + b_, [1, D])
    dram_in("L1_w1", [J1, D]); dram_in("L1_w3", [J1, D]); dram_in("L1_w2", [D, J1])
    dram_in("ident", [128, 128])
    dram_in("identb", [128, 128], BF16)
    dram_in("rope_cos", [DH0, TLOC]); dram_in("rope_sin", [DH0, TLOC])
    dram_in("dirs", [3, H0], BF16)
    dram_in("tiemask", [128, NRANK * 128], BF16)
    dram_in("onesd", [1, 8192], BF16)

    yout = nc.dram_tensor("yout", [TLOC, D], F32, kind="ExternalOutput").ap()

    xres = nc.dram_tensor("xres", [NTG, DC, 128, TGW], F32).ap()
    wTs = {
        (0, 1): nc.dram_tensor("w1T0", [JC0, 128, DC * 128], BF16).ap(),
        (0, 3): nc.dram_tensor("w3T0", [JC0, 128, DC * 128], BF16).ap(),
        (0, 2): nc.dram_tensor("w2T0", [DC, 128, JC0 * 128], BF16).ap(),
        (1, 1): nc.dram_tensor("w1T1", [JC1, 128, DC * 128], BF16).ap(),
        (1, 3): nc.dram_tensor("w3T1", [JC1, 128, DC * 128], BF16).ap(),
        (1, 2): nc.dram_tensor("w2T1", [DC, 128, JC1 * 128], BF16).ap(),
    }
    cc_in = {0: nc.dram_tensor("cc0_in", [2, KVH0 * DH0 * TLOC + 128], BF16),
             1: nc.dram_tensor("cc1_in", [2, D * TLOC + 128], BF16)}
    cc_out = {0: nc.dram_tensor("cc0_out", [2 * NRANK, KVH0 * DH0 * TLOC + 128], BF16),
              1: nc.dram_tensor("cc1_out", [2 * NRANK, D * TLOC + 128], BF16)}

    es = contextlib.ExitStack()
    arena = Arena(nc, es, "arena", 168 * 1024)
    identt = es.enter_context(nc.sbuf_tensor("identt", [128, 128], F32))
    identb = es.enter_context(nc.sbuf_tensor("identbt", [128, 128], BF16))
    statt = es.enter_context(nc.sbuf_tensor("statt", [128, 64], F32))
    onesc = es.enter_context(nc.sbuf_tensor("onesc", [128, 8], F32))
    onesb = es.enter_context(nc.sbuf_tensor("onesb", [128, 8], BF16))
    onesr = es.enter_context(nc.sbuf_tensor("onesr", [1, 128], F32))
    onesrb = es.enter_context(nc.sbuf_tensor("onesrb", [1, 512], BF16))
    rowsf = es.enter_context(nc.sbuf_tensor("rowsf", [1, 8 * TGW], F32))

    tiet = es.enter_context(nc.sbuf_tensor("tiet", [128, NRANK * 128], BF16))
    dirst = es.enter_context(nc.sbuf_tensor("dirst", [3, 8], BF16))
    hst = es.enter_context(nc.sbuf_tensor("hst", [1, 16], F32))
    bqr = es.enter_context(nc.sbuf_tensor("bqr", [1, 3 * D], BF16))
    bor = es.enter_context(nc.sbuf_tensor("bor", [1, D], BF16))
    pb = [es.enter_context(nc.psum_tensor(f"pb{i}", [128, 512], F32)) for i in range(8)]
    pbh = [p[:, :].bitcast(BF16) for p in pb]

    sems = {}
    def S(name):
        if name not in sems:
            sems[name] = Sem(nc, es, name)
        return sems[name]

    SC = {"s1_0": 0, "s3_0": 1, "s2h_0": 2, "r_0_0": 3, "r_1_0": 4, "r_2_0": 5,
          "s1_1": 8, "s3_1": 9, "s2h_1": 10, "r_0_1": 11, "r_1_1": 12, "r_2_1": 13,
          "acc0": 16, "acc1": 17, "acc2": 18, "tmp": 19,
          "nmel": 20, "invn": 21, "s11": 22, "r11": 23, "negmagic": 24, "eps": 25,
          "negm0": 26, "negm1": 27, "zero": 28, "tmpr": 29,
          **{f"qmx{i}": 32 + i for i in range(16)},
          **{f"kmx{i}": 48 + i for i in range(16)}}
    def sc(name):
        return statt[:, SC[name]:SC[name] + 1]
    def sc1(name):
        return statt[0:1, SC[name]:SC[name] + 1]

    block = es.enter_context(nc.Block())
    pbld = PB(block)

    xres_guards = []

    with es:
        arena.push()
        xtiles = [arena.alloc([128, D], F32) for _ in range(4)]
        stg0 = [arena.alloc([128, TGW], F32) for _ in range(2)]
        xld, p0mm, p0cp = S("xld"), S("p0mm"), S("p0cp")
        init = S("init")
        p0st = [S("p0st0"), S("p0st1")]
        cld = S("cld")

        @pbld.on("sync")
        def _(s):
            s.dma_start(out=identb[:, :], in_=din["identb"][:, :]).then_inc(cld.h, 16)
            s.dma_start(out=tiet[:, :], in_=din["tiemask"][:, :]).then_inc(cld.h, 16)
            s.dma_start(out=dirst[:, :], in_=din["dirs"][:, :]).then_inc(cld.h, 16)
            s.dma_start(out=hst[0:1, 0:8], in_=din["L0_hs"][:, :]).then_inc(cld.h, 16)
            s.dma_start(out=identt[:, :], in_=din["ident"][:, :]).then_inc(cld.h, 16)
            for tg in range(NTG):
                for tt in range(4):
                    k = tg * 4 + tt
                    s.dma_start(out=xtiles[tt][:, :],
                                in_=xin[k * 128:(k + 1) * 128, :]).then_inc(xld.h, 16)
                for dc in range(DC):
                    i = tg * DC + dc
                    s.wait_ge(p0cp.h, i + 1)
                    s.dma_start(out=xres[tg, dc, :, :],
                                in_=stg0[i % 2][:, :]).then_inc(p0st[i % 2].h, 16)
        cld.v += 80
        xld.v += NTG * 4 * 16
        p0st[0].v = NTG * DC // 2 * 16
        p0st[1].v = NTG * DC // 2 * 16

        cldg = S("cldg")

        @pbld.on("gpsimd")
        def _(g):
            for i_, nm in enumerate(("bq", "bk", "bv")):
                g.dma_start(out=bqr[0:1, i_ * D:(i_ + 1) * D],
                            in_=din["L1_" + nm][:, :]).then_inc(cldg.h, 16)
            g.dma_start(out=bor[0:1, :], in_=din["L1_bo"][:, :]).then_inc(cldg.h, 16)
        cldg.v += 64

        @pbld.on("vector")
        def _(v):
            v.wait_ge(cld.h, 80)
            v.wait_ge(S("cldg").h, 64)
            v.tensor_scalar(hst[0:1, 8:16], hst[0:1, 0:8], -1.0, 1.0, OP.mult, OP.add)
            v.drain()
            v.memset(onesc[:, :], 1.0)
            v.memset(onesb[:, :], 1.0)
            v.memset(onesr[:, :], 1.0)
            v.memset(onesrb[:, :], 1.0)
            v.memset(statt[:, 24:25], -MAGIC)
            v.memset(statt[:, 25:26], 1e-5).then_inc(init.h, 1)
            for tg in range(NTG):
                for dc in range(DC):
                    i = tg * DC + dc
                    v.wait_ge(p0mm.h, i + 1)
                    if i >= 2:
                        v.wait_ge(p0st[i % 2].h, 16 * (i // 2))
                    v.tensor_copy(stg0[i % 2][:, :], pb[i % 2][:, :]).then_inc(p0cp.h, 1)
        p0cp.v = NTG * DC
        p0mm.v = NTG * DC

        @pbld.on("tensor")
        def _(t):
            t.wait_ge(cld.h, 80)
            t.wait_ge(S("init").h, 1)
            for tg in range(NTG):
                t.wait_ge(xld.h, (tg + 1) * 4 * 16)
                for dc in range(DC):
                    i = tg * DC + dc
                    if i >= 2:
                        t.wait_ge(p0cp.h, i - 1)
                    for tt in range(4):
                        inst = t.transpose(pb[i % 2][:, tt * 128:(tt + 1) * 128],
                                           xtiles[tt][:, dc * 128:(dc + 1) * 128],
                                           identt[:, :])
                    inst.then_inc(p0mm.h, 1)
        pbld.flush()
        arena.pop()
        xres_guards = [(p0st[0], p0st[0].v), (p0st[1], p0st[1].v)]


        f_td = [S("ftd0"), S("ftd1")]

        def emit_tree(g, buf, K, mode, out_row, tbw):
            W = TGW
            CH = 4 if mode == "max" else 2
            nch = (K + CH - 1) // CH
            tdv = [f_td[0].v, f_td[1].v]
            for c in range(nch):
                k0, k1 = c * CH, min((c + 1) * CH, K)
                kw = (k1 - k0) * W
                if c >= 2:
                    g.wait_ge(f_td[c % 2].h, tdv[c % 2])
                if mode == "max":
                    src = buf[:, k0 * W:k1 * W]
                    rb = (c % 2) * 4 * W
                else:
                    if mode == "negmax":
                        g.tensor_scalar_mul(tbw[:, 0:kw], buf[:, k0 * W:k1 * W], -1.0)
                    else:
                        g.tensor_tensor(tbw[:, 0:kw], buf[:, k0 * W:k1 * W],
                                        buf[:, k0 * W:k1 * W], OP.mult)
                    g.drain()
                    src = tbw[:, 0:kw]
                    rb = 2 * W + (c % 2) * 2 * W
                g.tensor_reduce(tbw[0:1, rb:rb + kw], src, AXC, OP.max)
                g.drain()
                g.dma_start(out=tbw[k0:k1, 8 * W:9 * W],
                            in_=tbw[0:1, rb:rb + kw]).then_inc(f_td[c % 2].h, 16)
                tdv[c % 2] += 16
            f_td[0].v, f_td[1].v = tdv
            g.wait_ge(f_td[0].h, tdv[0])
            g.wait_ge(f_td[1].h, tdv[1])
            g.tensor_reduce(out_row, tbw[0:K, 8 * W:9 * W], AXC, OP.max)
            g.drain()

        f_wl = [S("fwl0"), S("fwl1")]
        f_qv, f_qg, f_qs = S("fqv"), S("fqg"), S("fqs")
        f_tp, f_te = S("ftp"), S("fte")
        f_ts = [S("fts0"), S("fts1")]
        f_mq = S("fmq")
        f_vf = S("fvf")
        f_mm = S("fmm")
        f_row = S("frow")
        f_xl = S("fxl")
        f_nq = S("fnq")
        f_sl = [S(f"fsl{i}") for i in range(4)]
        f_hm = S("fhm")
        f_gs = S("fgs")
        f_ge = S("fge")
        f_gq = S("fgq")
        f_w2l = [S("fw2l0"), S("fw2l1")]
        f_om = S("fom")
        f_oe = S("foe")
        f_os = [S("fos0"), S("fos1")]

        def emit_quant_prep(L, prior=()):
            prior = list(prior)
            Jreal = J0 if L == 0 else J1
            JC = JC0 if L == 0 else JC1
            numel = float(Jreal * D)
            mats = [("1", din[f"L{L}_w1"], Jreal, D, wTs[(L, 1)], DC),
                    ("3", din[f"L{L}_w3"], Jreal, D, wTs[(L, 3)], DC),
                    ("2", din[f"L{L}_w2"], D, Jreal, wTs[(L, 2)], JC)]

            arena.push()
            wrow = [arena.alloc([128, 5632], F32) for _ in range(2)]
            t1q = arena.alloc([128, 5632], F32)
            t2q = arena.alloc([128, 5632], F32)
            ternq = arena.alloc([128, 5632], BF16)
            evt = [arena.alloc([128, max(DC, JC) * 128], BF16) for _ in range(2)]
            accb = arena.alloc([128, 128], F32)

            pieces = []
            for mi, (_, win, R, Cw, _, _) in enumerate(mats):
                nch = (R + 127) // 128
                sub = []
                for rc in range(nch):
                    r0, r1 = rc * 128, min((rc + 1) * 128, R)
                    for c0 in range(0, Cw, 4096):
                        c1 = min(c0 + 4096, Cw)
                        sub.append((mi, win, r0, r1, c0, c1))
                pieces.extend([(p, i == len(sub) - 1) for i, p in enumerate(sub)])

            base_wl = [f_wl[0].v, f_wl[1].v]
            base_mq = f_mq.v
            nld = [0, 0]
            ldv = []
            for i, ((mi, win, r0, r1, c0, c1), last) in enumerate(pieces):
                nld[i % 2] += 1
                ldv.append(base_wl[i % 2] + nld[i % 2] * 16)
            f_wl[0].v = base_wl[0] + nld[0] * 16
            f_wl[1].v = base_wl[1] + nld[1] * 16

            colof = list(range(len(pieces)))
            colrange = {}
            start = 0
            cur = None
            for i, ((mi, win, r0, r1, c0, c1), last) in enumerate(pieces):
                if cur is None:
                    cur, start = mi, i
                if last:
                    colrange[mi] = (start, i + 1)
                    cur = None
            assert len(pieces) <= 128

            @pbld.on("sync")
            def _(s, pieces=pieces, base_mq=base_mq, ldv=ldv, prior=prior):
                for sem, val in prior:
                    s.wait_ge(sem.h, val)
                for i, ((mi, win, r0, r1, c0, c1), last) in enumerate(pieces):
                    if i >= 2:
                        s.wait_ge(f_mq.h, base_mq + (i - 2) + 1)
                    s.dma_start(out=wrow[i % 2][0:r1 - r0, 0:c1 - c0],
                                in_=win[r0:r1, c0:c1]).then_inc(f_wl[i % 2].h, 16)

            base_accz = f_row.v + 1
            f_row.v += 1

            @pbld.on("vector")
            def _(v, prior=prior):
                for sem, val in prior:
                    v.wait_ge(sem.h, val)
                v.memset(accb[:, :], 0.0).then_inc(f_row.h, 1)

            @pbld.on("scalar")
            def _(s, pieces=pieces, ldv=ldv, base_mq=base_mq, colof=colof, base_accz=base_accz):
                s.wait_ge(S("init").h, 1)
                s.wait_ge(f_row.h, base_accz)
                for i, ((mi, win, r0, r1, c0, c1), last) in enumerate(pieces):
                    s.wait_ge(f_wl[i % 2].h, ldv[i])
                    co = colof[i]
                    s.activation(t1q[0:r1 - r0, 0:c1 - c0],
                                 wrow[i % 2][0:r1 - r0, 0:c1 - c0], AF.Abs,
                                 accum_out=accb[0:r1 - r0, co:co + 1]).then_inc(f_mq.h, 1)
                    s.drain()

            mq_end = {}
            cnt = base_mq
            for i, ((mi, win, r0, r1, c0, c1), last) in enumerate(pieces):
                cnt += 1
                if last:
                    mq_end[mi] = cnt
            f_mq.v = cnt

            base_rowred = f_row.v
            row_red = {mi: base_rowred + mi + 1 for mi in range(3)}

            @pbld.on("vector")
            def _(v, mq_end=mq_end, colrange=colrange):
                for mi in range(3):
                    v.wait_ge(f_mq.h, mq_end[mi])
                    c0_, c1_ = colrange[mi]
                    v.reduce_sum(sc(f"acc{mi}"), accb[:, c0_:c1_],
                                 axis=AX).then_inc(f_row.h, 1)
                    v.drain()
            f_row.v += 3

            base_mm = f_mm.v

            @pbld.on("tensor")
            def _(t, row_red=row_red):
                for mi in range(3):
                    t.wait_ge(f_row.h, row_red[mi])
                    t.matmul(pb[mi][0:1, 0:1], sc(f"acc{mi}"), onesc[:, 0:1],
                             start=True, stop=True).then_inc(f_mm.h, 1)
            f_mm.v += 3

            base_row = f_row.v

            @pbld.on("vector")
            def _(v, base_mm=base_mm, numel=numel, L=L):
                v.memset(sc1("nmel"), numel)
                v.drain()
                v.reciprocal(sc1("invn"), sc1("nmel"))
                v.drain()
                for mi in range(3):
                    v.wait_ge(f_mm.h, base_mm + mi + 1)
                    v.tensor_tensor(sc1("s11"), pb[mi][0:1, 0:1], sc1("invn"), OP.mult)
                    v.drain()
                    if L == 1:
                        v.tensor_scalar_max(sc1("s11"), sc1("s11"), 1e-5)
                        v.drain()
                    v.reciprocal(sc1("r11"), sc1("s11"))
                    v.drain()
                    v.tensor_copy(rowsf[0:1, mi * 2:mi * 2 + 1], sc1("s11"))
                    v.tensor_copy(rowsf[0:1, mi * 2 + 1:mi * 2 + 2],
                                  sc1("r11")).then_inc(f_row.h, 1)
                    v.drain()
            f_row.v += 3

            @pbld.on("tensor")
            def _(t, base_row=base_row):
                for mi in range(3):
                    t.wait_ge(f_row.h, base_row + mi + 1)
                    t.matmul(pb[3 + mi][:, 0:1], onesr[:, :],
                             rowsf[0:1, mi * 2:mi * 2 + 1], start=True, stop=True)
                    t.matmul(pb[3 + mi][:, 1:2], onesr[:, :],
                             rowsf[0:1, mi * 2 + 1:mi * 2 + 2],
                             start=True, stop=True).then_inc(f_mm.h, 1)
            base_mm2 = f_mm.v
            f_mm.v += 3

            @pbld.on("vector")
            def _(v, base_mm2=base_mm2, L=L):
                names = [f"s1_{L}", f"s3_{L}", f"s2h_{L}"]
                for mi in range(3):
                    v.wait_ge(f_mm.h, base_mm2 + mi + 1)
                    if mi == 2:
                        v.tensor_scalar_mul(sc(names[mi]), pb[3 + mi][:, 0:1], 0.5)
                    else:
                        v.tensor_copy(sc(names[mi]), pb[3 + mi][:, 0:1])
                    v.tensor_copy(sc(f"r_{mi}_{L}"),
                                  pb[3 + mi][:, 1:2]).then_inc(f_row.h, 1)
            prep_rows_done = f_row.v + 3
            f_row.v += 3

            chunks = []
            for mi, (tagm, win, R, Cw, wTd, IC) in enumerate(mats):
                nch = (R + 127) // 128
                for rc in range(nch):
                    r0, r1 = rc * 128, min((rc + 1) * 128, R)
                    chunks.append((mi, win, R, Cw, wTd, IC, rc, r0, r1))

            base_wl2 = [f_wl[0].v, f_wl[1].v]
            base_qv, base_qg, base_qs = f_qv.v, f_qg.v, f_qs.v
            base_tp, base_te = f_tp.v, f_te.v
            base_ts = [f_ts[0].v, f_ts[1].v]
            nld = [0, 0]
            ldv2 = []
            for i, ch in enumerate(chunks):
                nld[i % 2] += 1
                ldv2.append(base_wl2[i % 2] + nld[i % 2] * 16)
            f_wl[0].v = base_wl2[0] + nld[0] * 16
            f_wl[1].v = base_wl2[1] + nld[1] * 16

            tp_val, te_val = [], []
            tpc, tec = base_tp, base_te
            grp_of = []
            for i, (mi, win, R, Cw, wTd, IC, rc, r0, r1) in enumerate(chunks):
                ng = (IC + 7) // 8
                grp_of.append((tpc, tec, ng))
                tpc += ng
                tec += ng
                tp_val.append(tpc)
                te_val.append(tec)
            f_tp.v, f_te.v = tpc, tec

            stv = []
            nst = [0, 0]
            for i, ch in enumerate(chunks):
                nst[i % 2] += 1
                stv.append(base_ts[i % 2] + nst[i % 2] * 16)
            f_ts[0].v = base_ts[0] + nst[0] * 16
            f_ts[1].v = base_ts[1] + nst[1] * 16

            @pbld.on("sync")
            def _(s, chunks=chunks, base_qv=base_qv, prep_rows_done=prep_rows_done):
                s.wait_ge(f_row.h, prep_rows_done)
                for i, (mi, win, R, Cw, wTd, IC, rc, r0, r1) in enumerate(chunks):
                    if i >= 2:
                        s.wait_ge(f_qv.h, base_qv + i - 1)
                    s.dma_start(out=wrow[i % 2][0:r1 - r0, 0:Cw],
                                in_=win[r0:r1, 0:Cw]).then_inc(f_wl[i % 2].h, 16)

            @pbld.on("vector")
            def _(v, chunks=chunks, ldv2=ldv2, base_qg=base_qg, L=L):
                for i, (mi, win, R, Cw, wTd, IC, rc, r0, r1) in enumerate(chunks):
                    v.wait_ge(f_wl[i % 2].h, ldv2[i])
                    if i >= 1:
                        v.wait_ge(f_qg.h, base_qg + i)
                    v.tensor_scalar(t1q[0:r1 - r0, 0:Cw], wrow[i % 2][0:r1 - r0, 0:Cw],
                                    sc(f"r_{mi}_{L}")[0:r1 - r0, :], 1.0,
                                    OP.mult, OP.min).then_inc(f_qv.h, 1)
            f_qv.v = base_qv + len(chunks)

            @pbld.on("gpsimd")
            def _(g, chunks=chunks, base_qv=base_qv, base_qs=base_qs):
                for i, (mi, win, R, Cw, wTd, IC, rc, r0, r1) in enumerate(chunks):
                    g.wait_ge(f_qv.h, base_qv + i + 1)
                    if i >= 1:
                        g.wait_ge(f_qs.h, base_qs + i)
                    g.tensor_scalar(t2q[0:r1 - r0, 0:Cw], t1q[0:r1 - r0, 0:Cw],
                                    -1.0, MAGIC, OP.max, OP.add).then_inc(f_qg.h, 1)
            f_qg.v = base_qg + len(chunks)

            @pbld.on("scalar")
            def _(s, chunks=chunks, base_qg=base_qg, base_tp=base_tp, grp_of=grp_of,
                  stv=stv, te_val=te_val):
                for i, (mi, win, R, Cw, wTd, IC, rc, r0, r1) in enumerate(chunks):
                    s.wait_ge(f_qg.h, base_qg + i + 1)
                    if i >= 1:
                        s.wait_ge(f_tp.h, tp_val[i - 1])
                    s.add(ternq[0:r1 - r0, 0:Cw], t2q[0:r1 - r0, 0:Cw],
                          sc("negmagic")[0:r1 - r0, :]).then_inc(f_qs.h, 1)
                    tpc0, tec0, ng = grp_of[i]
                    rows = r1 - r0
                    if i >= 2:
                        s.wait_ge(f_ts[i % 2].h, stv[i - 2])
                    if i < 2:
                        s.memzero(evt[i % 2][:, :])
                        s.drain()
                    elif (mi < 2 and rows < 128) or (mi == 2 and Cw < IC * 128):
                        s.memzero(evt[i % 2][:, 0:IC * 128])
                        s.drain()
                    for g_ in range(ng):
                        gidx = tpc0 + g_
                        s.wait_ge(f_tp.h, gidx + 1)
                        bank = 4 + (gidx % 4)
                        w = min(g_ * 8 + 8, IC) - g_ * 8
                        if mi < 2 and rows == 128:
                            s.copy(evt[i % 2][:, g_ * 8 * 128:(g_ * 8 + w) * 128],
                                   pbh[bank][:, 0:w * 128]).then_inc(f_te.h, 1)
                        elif mi < 2:
                            for u in range(g_ * 8, g_ * 8 + w):
                                sl = (u - g_ * 8) * 128
                                inst = s.copy(evt[i % 2][0:128, u * 128:u * 128 + rows],
                                              pbh[bank][0:128, sl:sl + rows])
                            inst.then_inc(f_te.h, 1)
                        else:
                            full = all(min((u + 1) * 128, Cw) - u * 128 == 128
                                       for u in range(g_ * 8, g_ * 8 + w))
                            if full:
                                s.copy(evt[i % 2][:, g_ * 8 * 128:(g_ * 8 + w) * 128],
                                       pbh[bank][:, 0:w * 128]).then_inc(f_te.h, 1)
                            else:
                                for u in range(g_ * 8, g_ * 8 + w):
                                    sl = (u - g_ * 8) * 128
                                    c0, c1 = u * 128, min((u + 1) * 128, Cw)
                                    inst = s.copy(
                                        evt[i % 2][0:c1 - c0, u * 128:(u + 1) * 128],
                                        pbh[bank][0:c1 - c0, sl:sl + 128])
                                inst.then_inc(f_te.h, 1)
                        s.drain()
                    s.dma_start(out=wTd[rc, :, :],
                                in_=evt[i % 2][:, 0:IC * 128]).then_inc(f_ts[i % 2].h, 16)
            f_qs.v = base_qs + len(chunks)

            @pbld.on("tensor")
            def _(t, chunks=chunks, base_qs=base_qs, grp_of=grp_of, te_val=te_val):
                for i, (mi, win, R, Cw, wTd, IC, rc, r0, r1) in enumerate(chunks):
                    t.wait_ge(f_qs.h, base_qs + i + 1)
                    tpc0, tec0, ng = grp_of[i]
                    rows = r1 - r0
                    for g_ in range(ng):
                        gidx = tpc0 + g_
                        if gidx - base_tp >= 4:
                            t.wait_ge(f_te.h, base_te + (gidx - base_tp) - 3)
                        bank = 4 + (gidx % 4)
                        for u in range(g_ * 8, min(g_ * 8 + 8, IC)):
                            sl = (u - g_ * 8) * 128
                            c0, c1 = u * 128, min((u + 1) * 128, Cw)
                            if mi < 2:
                                inst = t.transpose(pbh[bank][0:128, sl:sl + rows],
                                                   ternq[0:rows, c0:c1],
                                                   identb[0:rows, 0:rows])
                            else:
                                inst = t.transpose(pbh[bank][0:c1 - c0, sl:sl + 128],
                                                   ternq[0:128, c0:c1],
                                                   identb[0:128, 0:128])
                        inst.then_inc(f_tp.h, 1)

            pbld.flush()
            arena.pop()

        def emit_ffn(L, tag):
            nonlocal xres_guards
            Jreal = J0 if L == 0 else J1
            JC = JC0 if L == 0 else JC1
            w1T, w3T, w2T = wTs[(L, 1)], wTs[(L, 3)], wTs[(L, 2)]
            s1c, s3c, s2hc = sc(f"s1_{L}"), sc(f"s3_{L}"), sc(f"s2h_{L}")

            arena.push()
            xres_tg = arena.alloc([128, DC * TGW], F32)
            n_tg = arena.alloc([128, DC * TGW], BF16)
            gbuf = arena.alloc([128, JC * TGW], BF16)
            rbb = arena.alloc([128, TGW], F32)
            mrbb = arena.alloc([128, TGW], F32)
            sxb = arena.alloc([128, TGW], F32)
            stg2 = [arena.alloc([128, TGW], F32) for _ in range(2)]
            g1b = [arena.alloc([128, TGW], BF16) for _ in range(2)]
            wTt = [arena.alloc([128, DC * 128], BF16) for _ in range(4)]
            w2Tt = [arena.alloc([128, JC * 128], BF16) for _ in range(2)]
            tbw = arena.alloc([128, 9 * TGW], F32)

            prior = list(xres_guards) + [(f_ts[0], f_ts[0].v), (f_ts[1], f_ts[1].v)]
            new_guards = []
            os_prev_end = [f_os[0].v, f_os[1].v]

            for tg in range(NTG):
                base_xl = f_xl.v
                oe_prev_all = f_oe.v

                @pbld.on("sync")
                def _(s, tg=tg, prior=prior, oe_prev_all=oe_prev_all):
                    if tg == 0:
                        for sem, val in prior:
                            s.wait_ge(sem.h, val)
                    s.wait_ge(f_oe.h, oe_prev_all)
                    for dc in range(DC):
                        s.dma_start(out=xres_tg[:, dc * TGW:(dc + 1) * TGW],
                                    in_=xres[tg, dc, :, :]).then_inc(f_xl.h, 16)
                f_xl.v += DC * 16
                xlv = f_xl.v

                base_mm = f_mm.v

                @pbld.on("tensor")
                def _(t, xlv=xlv):
                    t.wait_ge(f_xl.h, xlv)
                    for dc in range(DC):
                        inst = t.matmul(pb[4][0:1, 0:TGW], onesc[:, 0:1],
                                        xres_tg[:, dc * TGW:(dc + 1) * TGW],
                                        start=(dc == 0), stop=(dc == DC - 1))
                    inst.then_inc(f_mm.h, 1)
                f_mm.v += 1
                mm_mu = f_mm.v

                base_qs2 = f_qs.v

                @pbld.on("scalar")
                def _(s, xlv=xlv):
                    s.wait_ge(f_xl.h, xlv)
                    for dc in range(DC):
                        s.square(n_tg[:, dc * TGW:(dc + 1) * TGW],
                                 xres_tg[:, dc * TGW:(dc + 1) * TGW]).then_inc(f_qs.h, 1)
                f_qs.v += DC
                qs_x2 = f_qs.v

                @pbld.on("tensor")
                def _(t, base_qs2=base_qs2):
                    for dc in range(DC):
                        t.wait_ge(f_qs.h, base_qs2 + dc + 1)
                        inst = t.matmul(pb[5][0:1, 0:TGW], onesb[:, 0:1],
                                        n_tg[:, dc * TGW:(dc + 1) * TGW],
                                        start=(dc == 0), stop=(dc == DC - 1))
                    inst.then_inc(f_mm.h, 1)
                f_mm.v += 1
                mm_x2 = f_mm.v

                if L == 1:
                    @pbld.on("vector")
                    def _(v, xlv=xlv):
                        v.wait_ge(f_xl.h, xlv)
                        xv = xres_tg.rearrange("p (k w) -> p k w",
                                               k=DC).transpose([0, 2, 1])
                        v.tensor_reduce(tbw[:, 0:TGW].unsqueeze(2), xv, AX, OP.max)
                        v.drain()
                        v.tensor_reduce(tbw[:, TGW:2 * TGW].unsqueeze(2), xv, AX,
                                        OP.max, negate=True)
                        v.drain()
                        v.sem_inc(f_vf.h, 1)
                    f_vf.v += 1
                    vfv = f_vf.v

                    @pbld.on("gpsimd")
                    def _(g, vfv=vfv):
                        g.wait_ge(f_vf.h, vfv)
                        g.tensor_reduce(rowsf[0:1, 0 * TGW:1 * TGW], tbw[:, 0:TGW],
                                        AXC, OP.max)
                        g.drain()
                        g.tensor_reduce(rowsf[0:1, 1 * TGW:2 * TGW],
                                        tbw[:, TGW:2 * TGW], AXC, OP.max)
                        g.drain()
                        g.sem_inc(f_row.h, 1)
                    f_row.v += 1

                rvA = f_row.v + 1
                rvS = f_row.v + 2
                rvB = f_row.v + 3

                @pbld.on("vector")
                def _(v, mm_mu=mm_mu, mm_x2=mm_x2, rvS=rvS, L=L):
                    mu = rowsf[0:1, 2 * TGW:3 * TGW]
                    var = rowsf[0:1, 3 * TGW:4 * TGW]
                    r = rowsf[0:1, 4 * TGW:5 * TGW]
                    mr = rowsf[0:1, 5 * TGW:6 * TGW]
                    v.wait_ge(f_mm.h, mm_mu)
                    v.tensor_scalar_mul(mu, pb[4][0:1, 0:TGW], 1.0 / 2048.0)
                    v.wait_ge(f_mm.h, mm_x2)
                    v.tensor_scalar_mul(var, pb[5][0:1, 0:TGW], 1.0 / 2048.0)
                    v.drain()
                    v.tensor_tensor(rowsf[0:1, 6 * TGW:7 * TGW], mu, mu, OP.mult)
                    v.drain()
                    v.tensor_tensor(var, var, rowsf[0:1, 6 * TGW:7 * TGW],
                                    OP.subtract).then_inc(f_row.h, 1)
                    v.wait_ge(f_row.h, rvS)
                    v.reciprocal(r, rowsf[0:1, 7 * TGW:8 * TGW])
                    v.drain()
                    inst = v.tensor_tensor(mr, mu, r, OP.mult)
                    if L == 1:
                        Rx = rowsf[0:1, 0 * TGW:1 * TGW]
                        Rn = rowsf[0:1, 1 * TGW:2 * TGW]
                        v.tensor_tensor(Rx, Rx, mu, OP.subtract)
                        v.tensor_tensor(Rn, Rn, mu, OP.add)
                        v.drain()
                        v.tensor_tensor(Rx, Rx, Rn, OP.max)
                        v.drain()
                        v.tensor_tensor(Rx, Rx, r, OP.mult)
                        v.drain()
                        v.tensor_scalar_max(Rx, Rx, 1e-5)
                        v.drain()
                        v.reciprocal(Rn, Rx)
                        v.drain()
                        v.tensor_scalar_mul(Rn, Rn, 127.0)
                        v.tensor_scalar_mul(Rx, Rx, 1.0 / 127.0)
                        v.drain()
                        v.tensor_tensor(r, r, Rn, OP.mult)
                        inst = v.tensor_tensor(mr, mr, Rn, OP.mult)
                    inst.then_inc(f_row.h, 1)
                    v.drain()

                @pbld.on("scalar")
                def _(s, rvA=rvA):
                    s.wait_ge(f_row.h, rvA)
                    s.activation(rowsf[0:1, 7 * TGW:8 * TGW], rowsf[0:1, 3 * TGW:4 * TGW],
                                 AF.Sqrt, bias=sc1("eps")).then_inc(f_row.h, 1)
                f_row.v = rvB

                @pbld.on("tensor")
                def _(t, rvB=rvB, L=L):
                    t.wait_ge(f_row.h, rvB)
                    t.matmul(pb[6][:, 0:TGW], onesr[:, :], rowsf[0:1, 4 * TGW:5 * TGW],
                             start=True, stop=True)
                    inst = t.matmul(pb[7][:, 0:TGW], onesr[:, :], rowsf[0:1, 5 * TGW:6 * TGW],
                                    start=True, stop=True)
                    if L == 1:
                        inst = t.matmul(pb[4][:, 0:TGW], onesr[:, :],
                                        rowsf[0:1, 0 * TGW:1 * TGW], start=True, stop=True)
                    inst.then_inc(f_mm.h, 1)
                f_mm.v += 1
                mm_bc = f_mm.v

                @pbld.on("vector")
                def _(v, mm_bc=mm_bc, L=L):
                    v.wait_ge(f_mm.h, mm_bc)
                    v.tensor_copy(rbb[:, :], pb[6][:, 0:TGW])
                    inst = v.tensor_copy(mrbb[:, :], pb[7][:, 0:TGW])
                    if L == 1:
                        inst = v.tensor_copy(sxb[:, :], pb[4][:, 0:TGW])
                    inst.then_inc(f_row.h, 1)
                f_row.v += 1
                row_bc = f_row.v

                base_p1 = f_nq.v
                base_gq = f_gq.v

                @pbld.on("gpsimd")
                def _(g, row_bc=row_bc, xlv=xlv, base_nq=f_nq.v, ose=list(os_prev_end)):
                    g.wait_ge(f_row.h, row_bc)
                    for dc in range(DC):
                        if dc < 2:
                            g.wait_ge(f_os[dc % 2].h, ose[dc % 2])
                        else:
                            g.wait_ge(f_nq.h, base_nq + dc - 1)
                        g.tensor_tensor(stg2[dc % 2][:, :],
                                        xres_tg[:, dc * TGW:(dc + 1) * TGW],
                                        rbb[:, :], OP.mult).then_inc(f_gq.h, 1)
                f_gq.v += DC

                @pbld.on("vector")
                def _(v, base_gq=base_gq, mm_x2=mm_x2, L=L):
                    v.wait_ge(f_mm.h, mm_x2)
                    for dc in range(DC):
                        v.wait_ge(f_gq.h, base_gq + dc + 1)
                        nt = n_tg[:, dc * TGW:(dc + 1) * TGW]
                        if L == 0:
                            v.tensor_tensor(nt, stg2[dc % 2][:, :], mrbb[:, :],
                                            OP.subtract).then_inc(f_nq.h, 1)
                        else:
                            v.tensor_tensor(stg2[dc % 2][:, :], stg2[dc % 2][:, :],
                                            mrbb[:, :], OP.subtract)
                            v.drain()
                            v.tensor_scalar(stg2[dc % 2][:, :], stg2[dc % 2][:, :],
                                            MAGIC, MAGIC, OP.add, OP.subtract)
                            v.drain()
                            v.tensor_tensor(nt, stg2[dc % 2][:, :], sxb[:, :],
                                            OP.mult).then_inc(f_nq.h, 1)
                f_nq.v += DC
                nq_tg = f_nq.v

                base_sl = [s_.v for s_ in f_sl]
                base_hm = f_hm.v
                base_gs = f_gs.v
                base_ge = f_ge.v
                hm_val = {}
                for j in range(JC):
                    for m in range(2):
                        hm_val[(j, m)] = base_hm + j * 2 + m + 1
                gs_val = {j: base_gs + j + 1 for j in range(JC)}
                ge_val = {j: base_ge + j + 1 for j in range(JC)}

                @pbld.on("sync")
                def _(s, base_hm=base_hm, L=L, JC=JC, w1T=w1T, w3T=w3T):
                    for j in range(JC):
                        for m, wTd in ((0, w1T), (1, w3T)):
                            slot = m * 2 + (j % 2)
                            if j >= 2:
                                s.wait_ge(f_hm.h, base_hm + (j - 2) * 2 + m + 1)
                            s.dma_start(out=wTt[slot][:, :],
                                        in_=wTd[j, :, :]).then_inc(f_sl[slot].h, 16)

                slv = {}
                cnt_sl = list(base_sl)
                for j in range(JC):
                    for m in range(2):
                        slot = m * 2 + (j % 2)
                        cnt_sl[slot] += 16
                        slv[(j, m)] = cnt_sl[slot]
                for slot in range(4):
                    f_sl[slot].v = cnt_sl[slot]

                @pbld.on("tensor")
                def _(t, slv=slv, nq_tg=nq_tg, base_gs=base_gs, base_ge=base_ge, JC=JC):
                    t.wait_ge(f_nq.h, nq_tg)
                    for j in range(JC):
                        for m in range(2):
                            slot = m * 2 + (j % 2)
                            t.wait_ge(f_sl[slot].h, slv[(j, m)])
                            if j >= 2:
                                if m == 0:
                                    t.wait_ge(f_gs.h, base_gs + (j - 2) + 1)
                                else:
                                    t.wait_ge(f_ge.h, base_ge + (j - 2) + 1)
                            bank = m * 2 + (j % 2)
                            for dc in range(DC):
                                inst = t.matmul(pb[bank][:, 0:TGW],
                                                wTt[slot][:, dc * 128:(dc + 1) * 128],
                                                n_tg[:, dc * TGW:(dc + 1) * TGW],
                                                start=(dc == 0), stop=(dc == DC - 1))
                            inst.then_inc(f_hm.h, 1)
                f_hm.v += 2 * JC

                @pbld.on("scalar")
                def _(s, hm_val=hm_val, base_ge=base_ge, s1c=s1c, JC=JC):
                    for j in range(JC):
                        s.wait_ge(f_hm.h, hm_val[(j, 0)])
                        if j >= 2:
                            s.wait_ge(f_ge.h, base_ge + (j - 2) + 1)
                        s.activation(g1b[j % 2][:, :], pb[j % 2][:, 0:TGW], AF.Silu,
                                     scale=s1c).then_inc(f_gs.h, 1)
                f_gs.v += JC

                @pbld.on("vector")
                def _(v, hm_val=hm_val, gs_val=gs_val, s3c=s3c, JC=JC):
                    for j in range(JC):
                        v.wait_ge(f_hm.h, hm_val[(j, 1)])
                        v.wait_ge(f_gs.h, gs_val[j])
                        v.scalar_tensor_tensor(gbuf[:, j * TGW:(j + 1) * TGW],
                                               pb[2 + j % 2][:, 0:TGW], s3c,
                                               g1b[j % 2][:, :], OP.mult,
                                               OP.mult).then_inc(f_ge.h, 1)
                f_ge.v += JC
                ge_all = f_ge.v

                if L == 1:
                    rvt = f_row.v + 1
                    f_row.v += 1

                    @pbld.on("vector")
                    def _(v, ge_all=ge_all, JC=JC):
                        v.wait_ge(f_ge.h, ge_all)
                        gv = gbuf.rearrange("p (k w) -> p k w",
                                            k=JC).transpose([0, 2, 1])
                        v.tensor_reduce(tbw[:, 2 * TGW:3 * TGW].unsqueeze(2), gv, AX,
                                        OP.max, apply_absolute_value=True)
                        v.drain()
                        v.sem_inc(f_vf.h, 1)
                    f_vf.v += 1
                    vfg = f_vf.v

                    @pbld.on("gpsimd")
                    def _(g, vfg=vfg):
                        g.wait_ge(f_vf.h, vfg)
                        g.tensor_reduce(rowsf[0:1, 0 * TGW:1 * TGW],
                                        tbw[:, 2 * TGW:3 * TGW], AXC, OP.max)
                        g.drain()
                        g.tensor_tensor(rowsf[0:1, 0 * TGW:1 * TGW],
                                        rowsf[0:1, 0 * TGW:1 * TGW],
                                        rowsf[0:1, 0 * TGW:1 * TGW], OP.mult)
                        g.drain()
                        g.sem_inc(f_row.h, 1)

                    rvsq = f_row.v + 1
                    f_row.v += 1

                    @pbld.on("scalar")
                    def _(s, rvt=rvt):
                        s.wait_ge(f_row.h, rvt)
                        s.activation(rowsf[0:1, 1 * TGW:2 * TGW],
                                     rowsf[0:1, 0 * TGW:1 * TGW], AF.Sqrt,
                                     bias=sc1("zero")).then_inc(f_row.h, 1)

                    rvq = f_row.v + 1

                    @pbld.on("vector")
                    def _(v, rvsq=rvsq):
                        v.wait_ge(f_row.h, rvsq)
                        Rx = rowsf[0:1, 0 * TGW:1 * TGW]
                        Rn = rowsf[0:1, 1 * TGW:2 * TGW]
                        tmp = rowsf[0:1, 6 * TGW:7 * TGW]
                        v.tensor_scalar_max(Rn, Rn, 1e-5)
                        v.drain()
                        v.tensor_scalar_mul(Rx, Rn, 1.0 / 127.0)
                        v.drain()
                        v.reciprocal(tmp, Rn)
                        v.drain()
                        v.tensor_scalar_mul(Rn, tmp, 127.0).then_inc(f_row.h, 1)
                        v.drain()
                    f_row.v += 1

                    @pbld.on("tensor")
                    def _(t, rvq=rvq):
                        t.wait_ge(f_row.h, rvq)
                        t.matmul(pb[4][:, 0:TGW], onesr[:, :], rowsf[0:1, 1 * TGW:2 * TGW],
                                 start=True, stop=True)
                        t.matmul(pb[5][:, 0:TGW], onesr[:, :], rowsf[0:1, 0 * TGW:1 * TGW],
                                 start=True, stop=True).then_inc(f_mm.h, 1)
                    f_mm.v += 1
                    mm_gq = f_mm.v

                    @pbld.on("vector")
                    def _(v, mm_gq=mm_gq):
                        v.wait_ge(f_mm.h, mm_gq)
                        v.tensor_copy(rbb[:, :], pb[4][:, 0:TGW])
                        v.tensor_copy(mrbb[:, :], pb[5][:, 0:TGW]).then_inc(f_row.h, 1)
                    f_row.v += 1
                    row_gq = f_row.v

                    base_gq2 = f_gq.v

                    @pbld.on("gpsimd")
                    def _(g, row_gq=row_gq, JC=JC):
                        g.wait_ge(f_row.h, row_gq)
                        for j in range(JC):
                            g.tensor_tensor(stg2[j % 2][:, :],
                                            gbuf[:, j * TGW:(j + 1) * TGW],
                                            rbb[:, :], OP.mult)
                            g.drain()
                            g.tensor_scalar(stg2[j % 2][:, :], stg2[j % 2][:, :],
                                            MAGIC, MAGIC, OP.add, OP.subtract)
                            g.drain()
                            g.tensor_tensor(gbuf[:, j * TGW:(j + 1) * TGW],
                                            stg2[j % 2][:, :],
                                            mrbb[:, :], OP.mult).then_inc(f_gq.h, 1)
                            g.drain()
                    f_gq.v += JC
                    g_ready = ("gq", f_gq.v)
                else:
                    g_ready = ("ge", ge_all)

                base_w2l = [f_w2l[0].v, f_w2l[1].v]
                base_om = f_om.v
                base_oe = f_oe.v
                om_val = {ec: base_om + ec + 1 for ec in range(DC)}
                oe_val = {ec: base_oe + ec + 1 for ec in range(DC)}

                @pbld.on("sync")
                def _(s, base_om=base_om, w2T=w2T):
                    for ec in range(DC):
                        if ec >= 2:
                            s.wait_ge(f_om.h, base_om + ec - 1)
                        s.dma_start(out=w2Tt[ec % 2][:, :],
                                    in_=w2T[ec, :, :]).then_inc(f_w2l[ec % 2].h, 16)

                w2lv = {}
                cw = list(base_w2l)
                for ec in range(DC):
                    cw[ec % 2] += 16
                    w2lv[ec] = cw[ec % 2]
                f_w2l[0].v, f_w2l[1].v = cw[0], cw[1]

                @pbld.on("tensor")
                def _(t, w2lv=w2lv, g_ready=g_ready, base_oe=base_oe, JC=JC):
                    sem = f_gq if g_ready[0] == "gq" else f_ge
                    t.wait_ge(sem.h, g_ready[1])
                    for ec in range(DC):
                        t.wait_ge(f_w2l[ec % 2].h, w2lv[ec])
                        bank = ec % 8
                        if ec >= 8:
                            t.wait_ge(f_oe.h, base_oe + ec - 8 + 1)
                        for j in range(JC):
                            inst = t.matmul(pb[bank][:, 0:TGW],
                                            w2Tt[ec % 2][:, j * 128:(j + 1) * 128],
                                            gbuf[:, j * TGW:(j + 1) * TGW],
                                            start=(j == 0), stop=(j == JC - 1))
                        inst.then_inc(f_om.h, 1)
                f_om.v += DC

                os_val = {}
                for ec in range(DC):
                    f_os[ec % 2].v += 16
                    os_val[ec] = f_os[ec % 2].v

                @pbld.on("vector")
                def _(v, om_val=om_val, s2hc=s2hc, tg=tg, os_val=os_val):
                    for ec in range(DC):
                        v.wait_ge(f_om.h, om_val[ec])
                        if ec >= 2:
                            v.wait_ge(f_os[ec % 2].h, os_val[ec - 2])
                        v.scalar_tensor_tensor(stg2[ec % 2][:, :], pb[ec % 8][:, 0:TGW],
                                               s2hc, xres_tg[:, ec * TGW:(ec + 1) * TGW],
                                               OP.mult, OP.add).then_inc(f_oe.h, 1)
                f_oe.v += DC

                @pbld.on("scalar")
                def _(s, oe_val=oe_val, tg=tg):
                    for ec in range(DC):
                        s.wait_ge(f_oe.h, oe_val[ec])
                        s.dma_start(out=xres[tg, ec, :, :],
                                    in_=stg2[ec % 2][:, :]).then_inc(f_os[ec % 2].h, 16)

                os_prev_end = [f_os[0].v, f_os[1].v]
                new_guards = [(f_os[0], f_os[0].v), (f_os[1], f_os[1].v)]

            pbld.flush()
            arena.pop()
            xres_guards = new_guards

        a_xl = S("axl"); a_sq = S("asq"); a_row = S("arow"); a_mm = S("amm")
        a_p1 = S("ap1"); a_nq = S("anq")
        a_wl = [S("awl0"), S("awl1")]
        a_qp = S("aqp"); a_ev = S("aev")
        a_cs = [S("acs0"), S("acs1")]
        a_cc = S("acc"); a_kl = S("akl")
        a_klp = [S("aklp0"), S("aklp1")]
        a_ko = S("ako"); a_gq = S("agq"); a_kp = S("akp")
        s_sm = S("ssm"); s_ex = S("sex"); s_mk = S("smk"); s_pv = S("spv")
        s_fin = S("sfin"); s_pre = S("spre")
        s_oe = S("soe"); s_os = [S("sos0"), S("sos1")]
        s_ol = S("sol"); s_om = S("som"); s_xl2 = [S("sxl20"), S("sxl21")]

        def emit_attn(L):
            nonlocal xres_guards
            H = H0 if L == 0 else H1
            KVH = KVH0 if L == 0 else H1
            DH = DH0 if L == 0 else DH1
            CH = DH // 128
            KVW = KVH * DH
            NQG = NT // 4
            GLEN = NRANK * TLOC
            lam2 = (LAM0 * LAM0) if L == 0 else (LAM1 * LAM1)
            ccin = cc_in[L].ap()
            ccout = cc_out[L].ap()
            ktail = KVW * TLOC
            if L == 0:
                wqb, wkb, wvn, wob = din["L0_Wqb"], din["L0_Wkb"], din["L0_Wvn"], din["L0_Wob"]
            else:
                wqb, wkb, wvn, wob = din["L1_wqb"], din["L1_wkb"], din["L1_wvn"], din["L1_wob"]
            NEV = KVW // 512
            kS = ccin[0:1, 0:KVW * TLOC].rearrange("o (f t) -> (o f) t", t=TLOC)
            vS = ccin[1:2, 0:KVW * TLOC].rearrange("o (g t e) -> (o g t) e", e=DH, t=TLOC)
            koutS = [ccout[2 * r:2 * r + 1, 0:KVW * TLOC].rearrange("o (f t) -> (o f) t", t=TLOC)
                     for r in range(NRANK)]
            voutS = [ccout[2 * r + 1:2 * r + 2, 0:KVW * TLOC].rearrange("o (g t e) -> (o g t) e", e=DH, t=TLOC)
                     for r in range(NRANK)]

            arena.push()
            QO = arena.alloc([128, H * CH * TLOC], BF16)
            stg2 = [arena.alloc([128, TGW], F32) for _ in range(2)]
            stgb = [arena.alloc([128, TGW], BF16) for _ in range(2)]
            cmb = [arena.alloc([128, TGW], F32) for _ in range(2)]
            prior = list(xres_guards)

            wqt_g = {0: None, 1: None}
            stgb_g = {0: None, 1: None}
            bank_g = {0: None, 1: None}
            vbank_g = {0: None, 1: None}
            stat_g = [None]

            def W(eng, gd):
                if gd is None:
                    return
                if isinstance(gd, list):
                    for g2 in gd:
                        W(eng, g2)
                    return
                eng.wait_ge(gd[0].h, gd[1])

            arena.push()
            xres_tg = arena.alloc([128, DC * TGW], F32)
            xh_tg = arena.alloc([128, DC * TGW], BF16)
            rbb = arena.alloc([128, TGW], F32)
            mrbb = arena.alloc([128, TGW], F32)
            wqt = [arena.alloc([128, DC * 512], BF16) for _ in range(2)]
            sqs = arena.alloc([128, TGW], BF16)
            ropec = [arena.alloc([128, TGW], F32) for _ in range(2)] if L == 0 else [None] * 2
            ropes = [arena.alloc([128, TGW], F32) for _ in range(2)] if L == 0 else [None] * 2

            for tg in range(NTG):
                @pbld.on("sync")
                def _(s, tg=tg, prior=prior):
                    if tg == 0:
                        for sem, val in prior:
                            s.wait_ge(sem.h, val)
                    for dc in range(DC):
                        s.dma_start(out=xres_tg[:, dc * TGW:(dc + 1) * TGW],
                                    in_=xres[tg, dc, :, :]).then_inc(a_xl.h, 16)
                    if L == 0:
                        s.dma_start(out=ropec[0][:, :],
                                    in_=din["rope_cos"][0:128, tg * TGW:(tg + 1) * TGW]).then_inc(a_xl.h, 16)
                        s.dma_start(out=ropec[1][:, :],
                                    in_=din["rope_cos"][128:256, tg * TGW:(tg + 1) * TGW]).then_inc(a_xl.h, 16)
                        s.dma_start(out=ropes[0][:, :],
                                    in_=din["rope_sin"][0:128, tg * TGW:(tg + 1) * TGW]).then_inc(a_xl.h, 16)
                        s.dma_start(out=ropes[1][:, :],
                                    in_=din["rope_sin"][128:256, tg * TGW:(tg + 1) * TGW]).then_inc(a_xl.h, 16)
                a_xl.v += DC * 16 + (64 if L == 0 else 0)
                xlv = a_xl.v

                @pbld.on("tensor")
                def _(t, xlv=xlv):
                    t.wait_ge(a_xl.h, xlv)
                    for dc in range(DC):
                        inst = t.matmul(pb[4][0:1, 0:TGW], onesc[:, 0:1],
                                        xres_tg[:, dc * TGW:(dc + 1) * TGW],
                                        start=(dc == 0), stop=(dc == DC - 1))
                    inst.then_inc(a_mm.h, 1)
                a_mm.v += 1
                mm_mu = a_mm.v

                @pbld.on("scalar")
                def _(s, xlv=xlv):
                    s.wait_ge(a_xl.h, xlv)
                    for dc in range(DC):
                        s.square(xh_tg[:, dc * TGW:(dc + 1) * TGW],
                                 xres_tg[:, dc * TGW:(dc + 1) * TGW]).then_inc(a_sq.h, 1)
                a_sq.v += DC
                sq_x2 = a_sq.v

                @pbld.on("tensor")
                def _(t, sq_x2=sq_x2):
                    for dc in range(DC):
                        t.wait_ge(a_sq.h, sq_x2 - DC + dc + 1)
                        inst = t.matmul(pb[5][0:1, 0:TGW], onesb[:, 0:1],
                                        xh_tg[:, dc * TGW:(dc + 1) * TGW],
                                        start=(dc == 0), stop=(dc == DC - 1))
                    inst.then_inc(a_mm.h, 1)
                a_mm.v += 1
                mm_x2 = a_mm.v

                rvA, rvS, rvB = a_row.v + 1, a_row.v + 2, a_row.v + 3

                @pbld.on("vector")
                def _(v, mm_mu=mm_mu, mm_x2=mm_x2, rvS=rvS):
                    mu = rowsf[0:1, 2 * TGW:3 * TGW]
                    var = rowsf[0:1, 3 * TGW:4 * TGW]
                    r = rowsf[0:1, 4 * TGW:5 * TGW]
                    mr = rowsf[0:1, 5 * TGW:6 * TGW]
                    v.wait_ge(a_mm.h, mm_mu)
                    v.tensor_scalar_mul(mu, pb[4][0:1, 0:TGW], 1.0 / 2048.0)
                    v.wait_ge(a_mm.h, mm_x2)
                    v.tensor_scalar_mul(var, pb[5][0:1, 0:TGW], 1.0 / 2048.0)
                    v.drain()
                    v.tensor_tensor(rowsf[0:1, 6 * TGW:7 * TGW], mu, mu, OP.mult)
                    v.drain()
                    v.tensor_tensor(var, var, rowsf[0:1, 6 * TGW:7 * TGW],
                                    OP.subtract).then_inc(a_row.h, 1)
                    v.wait_ge(a_row.h, rvS)
                    v.reciprocal(r, rowsf[0:1, 7 * TGW:8 * TGW])
                    v.drain()
                    v.tensor_tensor(mr, mu, r, OP.mult).then_inc(a_row.h, 1)
                    v.drain()

                @pbld.on("scalar")
                def _(s, rvA=rvA):
                    s.wait_ge(a_row.h, rvA)
                    s.activation(rowsf[0:1, 7 * TGW:8 * TGW], rowsf[0:1, 3 * TGW:4 * TGW],
                                 AF.Sqrt, bias=sc1("eps")).then_inc(a_row.h, 1)
                a_row.v = rvB

                @pbld.on("tensor")
                def _(t, rvB=rvB):
                    t.wait_ge(a_row.h, rvB)
                    t.matmul(pb[6][:, 0:TGW], onesr[:, :], rowsf[0:1, 4 * TGW:5 * TGW],
                             start=True, stop=True)
                    t.matmul(pb[7][:, 0:TGW], onesr[:, :], rowsf[0:1, 5 * TGW:6 * TGW],
                             start=True, stop=True).then_inc(a_mm.h, 1)
                a_mm.v += 1
                mm_bc = a_mm.v

                @pbld.on("vector")
                def _(v, mm_bc=mm_bc):
                    v.wait_ge(a_mm.h, mm_bc)
                    v.tensor_copy(rbb[:, :], pb[6][:, 0:TGW])
                    v.tensor_copy(mrbb[:, :], pb[7][:, 0:TGW]).then_inc(a_row.h, 1)
                a_row.v += 1
                row_bc = a_row.v

                base_nq = a_nq.v

                @pbld.on("gpsimd")
                def _(g, row_bc=row_bc, base_nq=base_nq):
                    g.wait_ge(a_row.h, row_bc)
                    for dc in range(DC):
                        if dc >= 2:
                            g.wait_ge(a_nq.h, base_nq + dc - 1)
                        g.tensor_tensor(stg2[dc % 2][:, :],
                                        xres_tg[:, dc * TGW:(dc + 1) * TGW],
                                        rbb[:, :], OP.mult).then_inc(a_p1.h, 1)
                base_p1 = a_p1.v
                a_p1.v += DC

                @pbld.on("vector")
                def _(v, base_p1=base_p1, mm_x2=mm_x2):
                    v.wait_ge(a_mm.h, mm_x2)
                    for dc in range(DC):
                        v.wait_ge(a_p1.h, base_p1 + dc + 1)
                        v.tensor_tensor(xh_tg[:, dc * TGW:(dc + 1) * TGW],
                                        stg2[dc % 2][:, :], mrbb[:, :],
                                        OP.subtract).then_inc(a_nq.h, 1)
                a_nq.v += DC
                nq_tg = a_nq.v

                for kind, idxs, wsrc in (("q", list(range(H)), wqb),
                                         ("k", list(range(KVH)), wkb)):
                    for n_i, idx in enumerate(idxs):
                        par = (n_i + (0 if kind == "q" else H)) % 2
                        banks = [par * 2, par * 2 + 1]
                        gd_w = wqt_g[par]
                        a_wl[par].v += 16
                        wlv = a_wl[par].v

                        @pbld.on("gpsimd")
                        def _(g, idx=idx, par=par, gd_w=gd_w, wsrc=wsrc):
                            W(g, gd_w)
                            g.dma_start(out=wqt[par][:, 0:DC * DH],
                                        in_=wsrc[idx, :, :]).then_inc(a_wl[par].h, 16)

                        gd_b = bank_g[par]
                        a_qp.v += 1
                        qpv = a_qp.v

                        @pbld.on("tensor")
                        def _(t, idx=idx, par=par, banks=banks, wlv=wlv, gd_b=gd_b,
                              kind=kind, nq_tg=nq_tg):
                            t.wait_ge(a_nq.h, nq_tg)
                            t.wait_ge(a_wl[par].h, wlv)
                            W(t, gd_b)
                            for c in range(CH):
                                for dc in range(DC):
                                    last = (dc == DC - 1) and (L == 0)
                                    inst = t.matmul(pb[banks[c]][:, 0:TGW],
                                                    wqt[par][:, dc * DH + c * 128:dc * DH + c * 128 + 128],
                                                    xh_tg[:, dc * TGW:(dc + 1) * TGW],
                                                    start=(dc == 0), stop=last)
                                if L == 1:
                                    boff = 0 if kind == "q" else D
                                    br = bqr[0:1, boff + idx * 128:boff + idx * 128 + 128]
                                    inst = t.matmul(pb[banks[c]][:, 0:TGW], br,
                                                    onesrb[0:1, 0:TGW], start=False, stop=True)
                            inst.then_inc(a_qp.h, 1)

                        sqv, mmv = [], []
                        for c in range(CH):
                            a_sq.v += 1
                            sqv.append(a_sq.v)
                            a_mm.v += 1
                            mmv.append(a_mm.v)

                        @pbld.on("scalar")
                        def _(s, qpv=qpv, banks=banks, mmv=mmv):
                            s.wait_ge(a_qp.h, qpv)
                            for c in range(CH):
                                if c > 0:
                                    s.wait_ge(a_mm.h, mmv[c - 1])
                                s.square(sqs[:, :], pb[banks[c]][:, 0:TGW]).then_inc(a_sq.h, 1)
                                s.drain()

                        gd_stat = stat_g[0]

                        @pbld.on("tensor")
                        def _(t, sqv=sqv, gd_stat=gd_stat):
                            W(t, gd_stat)
                            for c in range(CH):
                                t.wait_ge(a_sq.h, sqv[c])
                                t.matmul(pb[6][0:1, 0:TGW], onesb[:, 0:1], sqs[:, :],
                                         start=(c == 0), stop=(c == CH - 1)).then_inc(a_mm.h, 1)

                        col = SC[f"qmx{idx}"] if kind == "q" else SC[f"kmx{idx}"]
                        a_row.v += 1
                        stat_g[0] = (a_row, a_row.v)

                        @pbld.on("vector")
                        def _(v, mmv=mmv, col=col, tg=tg):
                            v.wait_ge(a_mm.h, mmv[-1])
                            v.reduce_max(statt[0:1, SC["tmpr"]:SC["tmpr"] + 1],
                                         pb[6][0:1, 0:TGW], axis=AX)
                            v.drain()
                            if tg == 0:
                                v.tensor_copy(statt[0:1, col:col + 1],
                                              statt[0:1, SC["tmpr"]:SC["tmpr"] + 1]).then_inc(a_row.h, 1)
                            else:
                                v.tensor_tensor(statt[0:1, col:col + 1],
                                                statt[0:1, col:col + 1],
                                                statt[0:1, SC["tmpr"]:SC["tmpr"] + 1],
                                                OP.max).then_inc(a_row.h, 1)
                            v.drain()
                        statv = a_row.v

                        a_ev.v += 1
                        evv = a_ev.v
                        if kind == "q":
                            @pbld.on("vector")
                            def _(v, idx=idx, banks=banks, qpv=qpv, tg=tg):
                                v.wait_ge(a_qp.h, qpv)
                                if L == 0:
                                    qo0 = QO[:, (idx * CH + 0) * TLOC + tg * TGW:(idx * CH + 0) * TLOC + (tg + 1) * TGW]
                                    qo1 = QO[:, (idx * CH + 1) * TLOC + tg * TGW:(idx * CH + 1) * TLOC + (tg + 1) * TGW]
                                    v.tensor_tensor(stg2[0][:, :], pb[banks[0]][:, 0:TGW], ropec[0][:, :], OP.mult)
                                    v.tensor_tensor(stg2[1][:, :], pb[banks[1]][:, 0:TGW], ropes[0][:, :], OP.mult)
                                    v.drain()
                                    v.tensor_tensor(qo0, stg2[0][:, :], stg2[1][:, :], OP.subtract)
                                    v.drain()
                                    v.tensor_tensor(stg2[0][:, :], pb[banks[1]][:, 0:TGW], ropec[1][:, :], OP.mult)
                                    v.tensor_tensor(stg2[1][:, :], pb[banks[0]][:, 0:TGW], ropes[1][:, :], OP.mult)
                                    v.drain()
                                    inst = v.tensor_tensor(qo1, stg2[0][:, :], stg2[1][:, :], OP.add)
                                else:
                                    qo0 = QO[:, idx * TLOC + tg * TGW:idx * TLOC + (tg + 1) * TGW]
                                    inst = v.tensor_scalar_mul(qo0, pb[banks[0]][:, 0:TGW], LAM1)
                                inst.then_inc(a_ev.h, 1)
                                v.drain()
                        else:
                            kslot = 0 if CH == 2 else (n_i % 2)
                            gsd = [stgb_g[0], stgb_g[1]] if CH == 2 else [stgb_g[kslot]]
                            stv = []
                            for c in range(CH):
                                sl_ = c if CH == 2 else kslot
                                a_cs[sl_ % 2].v += 16
                                stv.append(a_cs[sl_ % 2].v)

                            @pbld.on("vector")
                            def _(v, banks=banks, qpv=qpv, gsd=gsd, kslot=kslot):
                                v.wait_ge(a_qp.h, qpv)
                                for g2 in gsd:
                                    W(v, g2)
                                if L == 0:
                                    v.tensor_tensor(stg2[0][:, :], pb[banks[0]][:, 0:TGW], ropec[0][:, :], OP.mult)
                                    v.tensor_tensor(stg2[1][:, :], pb[banks[1]][:, 0:TGW], ropes[0][:, :], OP.mult)
                                    v.drain()
                                    v.tensor_tensor(stgb[0][:, :], stg2[0][:, :], stg2[1][:, :], OP.subtract)
                                    v.drain()
                                    v.tensor_tensor(stg2[0][:, :], pb[banks[1]][:, 0:TGW], ropec[1][:, :], OP.mult)
                                    v.tensor_tensor(stg2[1][:, :], pb[banks[0]][:, 0:TGW], ropes[1][:, :], OP.mult)
                                    v.drain()
                                    inst = v.tensor_tensor(stgb[1][:, :], stg2[0][:, :], stg2[1][:, :], OP.add)
                                else:
                                    inst = v.tensor_scalar_mul(stgb[kslot][:, :], pb[banks[0]][:, 0:TGW], LAM1)
                                inst.then_inc(a_ev.h, 1)
                                v.drain()

                            @pbld.on("sync")
                            def _(s, idx=idx, evv=evv, tg=tg, kslot=kslot):
                                s.wait_ge(a_ev.h, evv)
                                for c in range(CH):
                                    dst = kS[idx * DH + c * 128:idx * DH + (c + 1) * 128,
                                             tg * TGW:(tg + 1) * TGW]
                                    s.dma_start(out=dst,
                                                in_=stgb[c if CH == 2 else kslot][0:128, :]).then_inc(a_cs[(c if CH == 2 else kslot) % 2].h, 16)
                            for c in range(CH):
                                stgb_g[(c if CH == 2 else kslot) % 2] = (a_cs[(c if CH == 2 else kslot) % 2], stv[c])

                        wqt_g[par] = (a_qp, qpv)
                        bank_g[par] = [(a_ev, evv), (a_sq, sqv[-1])]

                for ev in range(NEV):
                    par = ev % 2
                    gd_w = wqt_g[par]
                    a_wl[par].v += 16
                    wlv = a_wl[par].v

                    @pbld.on("gpsimd")
                    def _(g, ev=ev, par=par, gd_w=gd_w):
                        W(g, gd_w)
                        g.dma_start(out=wqt[par][:, 0:DC * 512],
                                    in_=wvn[ev, :, :]).then_inc(a_wl[par].h, 16)

                    qpv = None
                    for tt in range(4):
                        vbank = 4 + tt % 2
                        gd_b = vbank_g[tt % 2]
                        a_qp.v += 1
                        qpv = a_qp.v

                        @pbld.on("tensor")
                        def _(t, ev=ev, par=par, tt=tt, vbank=vbank, wlv=wlv, gd_b=gd_b,
                              nq_tg=nq_tg):
                            t.wait_ge(a_nq.h, nq_tg)
                            t.wait_ge(a_wl[par].h, wlv)
                            W(t, gd_b)
                            for dc in range(DC):
                                last = (dc == DC - 1) and (L == 0)
                                inst = t.matmul(pb[vbank][:, 0:TGW],
                                                xh_tg[:, dc * TGW + tt * 128:dc * TGW + tt * 128 + 128],
                                                wqt[par][:, dc * 512:(dc + 1) * 512],
                                                start=(dc == 0), stop=last)
                            if L == 1:
                                inst = t.matmul(pb[vbank][:, 0:TGW], onesrb[0:1, 0:128],
                                                bqr[0:1, 2 * D + ev * 512:2 * D + (ev + 1) * 512],
                                                start=False, stop=True)
                            inst.then_inc(a_qp.h, 1)

                        a_ev.v += 1
                        evv = a_ev.v
                        gsd = stgb_g[tt % 2]

                        @pbld.on("vector")
                        def _(v, qpv=qpv, vbank=vbank, tt=tt, gsd=gsd):
                            v.wait_ge(a_qp.h, qpv)
                            W(v, gsd)
                            v.tensor_copy(stgb[tt % 2][:, :], pb[vbank][:, 0:TGW]).then_inc(a_ev.h, 1)
                            v.drain()

                        nst = KVH if L == 0 else 4
                        a_cs[tt % 2].v += 16 * nst
                        stv = a_cs[tt % 2].v

                        @pbld.on("sync")
                        def _(s, ev=ev, tt=tt, evv=evv, tg=tg):
                            s.wait_ge(a_ev.h, evv)
                            if L == 0:
                                for g_ in range(KVH):
                                    dst = vS[g_ * TLOC + tg * TGW + tt * 128:
                                             g_ * TLOC + tg * TGW + (tt + 1) * 128, :]
                                    s.dma_start(out=dst,
                                                in_=stgb[tt % 2][:, g_ * DH:(g_ + 1) * DH]).then_inc(a_cs[tt % 2].h, 16)
                            else:
                                for sub in range(4):
                                    hh = ev * 4 + sub
                                    dst = vS[hh * TLOC + tg * TGW + tt * 128:
                                             hh * TLOC + tg * TGW + (tt + 1) * 128, :]
                                    s.dma_start(out=dst,
                                                in_=stgb[tt % 2][:, sub * 128:(sub + 1) * 128]).then_inc(a_cs[tt % 2].h, 16)
                        stgb_g[tt % 2] = (a_cs[tt % 2], stv)
                        vbank_g[tt % 2] = (a_ev, evv)
                    wqt_g[par] = (a_qp, qpv)

            a_row.v += 1
            rowv = a_row.v
            gsd = stgb_g[0]

            @pbld.on("vector")
            def _(v, gsd=gsd, KVH=KVH):
                W(v, gsd)
                v.memset(stgb[0][0:1, 0:128], 0.0)
                v.drain()
                for g_ in range(KVH):
                    v.tensor_copy(stgb[0][0:1, g_:g_ + 1],
                                  statt[0:1, SC[f"kmx{g_}"]:SC[f"kmx{g_}"] + 1])
                v.drain()
                v.sem_inc(a_row.h, 1)

            a_cs[0].v += 32
            cstv = a_cs[0].v

            @pbld.on("sync")
            def _(s, rowv=rowv):
                s.wait_ge(a_row.h, rowv)
                s.dma_start(out=ccin[0:1, ktail:ktail + 128],
                            in_=stgb[0][0:1, 0:128]).then_inc(a_cs[0].h, 16)
                s.dma_start(out=ccin[1:2, ktail:ktail + 128],
                            in_=stgb[0][0:1, 0:128]).then_inc(a_cs[0].h, 16)
            stgb_g[0] = (a_cs[0], cstv)

            a_cc.v += 1
            ccv = a_cc.v

            @pbld.on("gpsimd")
            def _(g, cs0=a_cs[0].v, cs1=a_cs[1].v):
                g.wait_ge(a_cs[0].h, cs0)
                g.wait_ge(a_cs[1].h, cs1)
                g.collective_compute(
                    "AllGather", OP.bypass,
                    replica_groups=[[0, 1, 2, 3], [4, 5, 6, 7]],
                    ins=[ccin[:, :]], outs=[ccout[:, :]],
                ).then_inc(a_cc.h, 1)

            a_kl.v += 16 * NRANK
            klv = a_kl.v

            @pbld.on("sync")
            def _(s, ccv=ccv):
                s.wait_ge(a_cc.h, ccv)
                for r_ in range(NRANK):
                    s.dma_start(out=stgb[0][0:1, r_ * 128:(r_ + 1) * 128],
                                in_=ccout[2 * r_:2 * r_ + 1, ktail:ktail + 128]).then_inc(a_kl.h, 16)

            a_row.v += 1
            tails_done = a_row.v

            @pbld.on("vector")
            def _(v, klv=klv, KVH=KVH):
                v.wait_ge(a_kl.h, klv)
                for g_ in range(KVH):
                    cg = SC[f"kmx{g_}"]
                    v.tensor_copy(statt[0:1, cg:cg + 1], stgb[0][0:1, g_:g_ + 1])
                    v.drain()
                    for r_ in range(1, NRANK):
                        v.tensor_tensor(statt[0:1, cg:cg + 1], statt[0:1, cg:cg + 1],
                                        stgb[0][0:1, r_ * 128 + g_:r_ * 128 + g_ + 1], OP.max)
                        v.drain()
                v.memset(statt[:, SC["zero"]:SC["zero"] + 1], 0.0)
                v.sem_inc(a_row.h, 1)

            arena.pop()

            arena.push()
            nkv = 1 if L == 0 else 2
            Kh = [arena.alloc([128, CH * GLEN], BF16) for _ in range(nkv)]
            Vh = [arena.alloc([128, (GLEN // 128) * DH], BF16) for _ in range(nkv)]
            Pt = [arena.alloc([128, TGW], BF16) for _ in range(2)]
            if L == 0:
                Pg = [arena.alloc([128, TGW], BF16) for _ in range(2)]
                kplT = arena.alloc([2, GLEN], BF16)
                gqr = arena.alloc([2, TLOC], BF16)
                strow = arena.alloc([1, TLOC], BF16)

            if L == 0:
                a_ko.v += 16
                rowk1 = a_ko.v

                @pbld.on("sync")
                def _(s, ccv=ccv):
                    s.wait_ge(a_cc.h, ccv)
                    s.dma_start(out=kplT[1:2, :], in_=din["onesd"][0:1, 0:GLEN]).then_inc(a_ko.h, 16)
            else:
                rowk1 = 0

            def chunks_for(qg):
                out = []
                for j in range(NRANK):
                    kmax = (4 * qg + 3) if L == 0 else (NT - 1)
                    for kp_ in range(kmax + 1):
                        qoff = (kp_ - 4 * qg) * 128 if (L == 0 and kp_ >= 4 * qg) else 0
                        out.append((j, kp_, qoff, 512 - qoff))
                return out

            kv_groups = list(range(KVH0)) if L == 0 else list(range(H1))
            kvld_g = {0: None, 1: None}
            for gi, g_ in enumerate(kv_groups):
                kpar = gi % nkv
                gd_kv = kvld_g[kpar]
                a_klp[kpar].v += 16 * (CH * NRANK + NRANK * NT)
                klv2 = a_klp[kpar].v

                @pbld.on("sync")
                def _(s, g_=g_, kpar=kpar, gd_kv=gd_kv, ccv=ccv):
                    s.wait_ge(a_cc.h, ccv)
                    W(s, gd_kv)
                    for c in range(CH):
                        for r_ in range(NRANK):
                            s.dma_start(out=Kh[kpar][:, c * GLEN + r_ * TLOC:c * GLEN + (r_ + 1) * TLOC],
                                        in_=koutS[r_][g_ * DH + c * 128:g_ * DH + (c + 1) * 128, :]).then_inc(a_klp[kpar].h, 16)
                    for r_ in range(NRANK):
                        for kp_ in range(NT):
                            vt = r_ * NT + kp_
                            s.dma_start(out=Vh[kpar][:, vt * DH:(vt + 1) * DH],
                                        in_=voutS[r_][g_ * TLOC + kp_ * 128:g_ * TLOC + (kp_ + 1) * 128, :]).then_inc(a_klp[kpar].h, 16)

                qheads = [4 * g_ + u for u in range(4)] if L == 0 else [g_]
                for h in qheads:
                    negc = SC["negm0"] if h % 2 == 0 else SC["negm1"]
                    a_row.v += 1
                    prow = a_row.v

                    @pbld.on("vector")
                    def _(v, h=h, g_=g_, tails_done=tails_done, lam2=lam2, finv=s_fin.v):
                        v.wait_ge(a_row.h, tails_done)
                        v.wait_ge(s_fin.h, finv)
                        v.tensor_tensor(statt[0:1, SC["tmpr"]:SC["tmpr"] + 1],
                                        statt[0:1, SC[f"qmx{h}"]:SC[f"qmx{h}"] + 1],
                                        statt[0:1, SC[f"kmx{g_}"]:SC[f"kmx{g_}"] + 1], OP.add)
                        v.drain()
                        v.tensor_scalar_mul(statt[0:1, SC["tmpr"]:SC["tmpr"] + 1],
                                            statt[0:1, SC["tmpr"]:SC["tmpr"] + 1],
                                            -0.5 * lam2).then_inc(a_row.h, 1)
                        v.drain()

                    a_mm.v += 1
                    pmm = a_mm.v

                    @pbld.on("tensor")
                    def _(t, prow=prow, finv=s_fin.v):
                        t.wait_ge(a_row.h, prow)
                        t.wait_ge(s_fin.h, finv)
                        t.matmul(pb[7][:, 0:1], onesr[:, :],
                                 statt[0:1, SC["tmpr"]:SC["tmpr"] + 1],
                                 start=True, stop=True).then_inc(a_mm.h, 1)

                    a_row.v += 1
                    prow2 = a_row.v

                    @pbld.on("vector")
                    def _(v, pmm=pmm, negc=negc):
                        v.wait_ge(a_mm.h, pmm)
                        v.tensor_copy(statt[:, negc:negc + 1], pb[7][:, 0:1]).then_inc(a_row.h, 1)
                        v.drain()

                    gqrow = 0
                    gqdma_l = [0]
                    if L == 0:
                        nsl = GLEN // TGW
                        kpmm0 = a_mm.v
                        a_mm.v += nsl

                        base_kp = a_kp.v

                        @pbld.on("tensor")
                        def _(t, h=h, kpar=kpar, klv2=klv2, finv=s_fin.v, rowk1=rowk1,
                              prow2=prow2, base_kp=base_kp):
                            t.wait_ge(a_klp[kpar].h, klv2)
                            t.wait_ge(a_ko.h, rowk1)
                            t.wait_ge(a_row.h, prow2)
                            t.wait_ge(s_fin.h, finv)
                            for sl in range(GLEN // TGW):
                                if sl >= 2:
                                    t.wait_ge(a_kp.h, base_kp + sl - 1)
                                t.matmul(pb[6 + sl % 2][0:1, 0:TGW], dirst[:, h:h + 1],
                                         Kh[kpar][0:3, sl * TGW:(sl + 1) * TGW],
                                         start=True, stop=True).then_inc(a_mm.h, 1)

                        a_row.v += 1
                        kprow = a_row.v

                        @pbld.on("vector")
                        def _(v, kpmm0=kpmm0):
                            for sl in range(GLEN // TGW):
                                v.wait_ge(a_mm.h, kpmm0 + sl + 1)
                                v.tensor_copy(kplT[0:1, sl * TGW:(sl + 1) * TGW],
                                              pb[6 + sl % 2][0:1, 0:TGW]).then_inc(a_kp.h, 1)
                            v.drain()
                            v.reduce_max(statt[0:1, SC["tmpr"]:SC["tmpr"] + 1],
                                         kplT[0:1, :], axis=AX)
                            v.tensor_reduce(statt[0:1, SC["tmpr"] + 1:SC["tmpr"] + 2],
                                            kplT[0:1, :], axis=AX, op=OP.min)
                            v.drain()
                            v.sem_inc(a_row.h, 1)
                        a_kp.v += GLEN // TGW

                        nql = TLOC // TGW
                        qpmm0 = a_mm.v
                        a_mm.v += nql

                        base_qp2 = a_kp.v

                        @pbld.on("tensor")
                        def _(t, h=h, qpmm0=qpmm0, kprow=kprow, base_qp2=base_qp2):
                            t.wait_ge(a_row.h, kprow)
                            for sl in range(TLOC // TGW):
                                if sl >= 1:
                                    t.wait_ge(a_kp.h, base_qp2 + sl)
                                t.matmul(pb[6][0:1, 0:TGW], dirst[:, h:h + 1],
                                         QO[0:3, (h * CH) * TLOC + sl * TGW:(h * CH) * TLOC + (sl + 1) * TGW],
                                         start=True, stop=True).then_inc(a_mm.h, 1)

                        a_row.v += 1
                        gqrow = a_row.v

                        @pbld.on("vector")
                        def _(v, qpmm0=qpmm0):
                            qp = rowsf[0:1, 0:TGW]
                            mg = rowsf[0:1, TGW:2 * TGW]
                            t2 = rowsf[0:1, 2 * TGW:3 * TGW]
                            for sl in range(TLOC // TGW):
                                v.wait_ge(a_mm.h, qpmm0 + sl + 1)
                                v.tensor_copy(qp, pb[6][0:1, 0:TGW]).then_inc(a_kp.h, 1)
                                v.drain()
                                v.tensor_single_scalar(mg, qp, statt[0:1, SC["tmpr"]:SC["tmpr"] + 1], OP.mult)
                                v.tensor_single_scalar(t2, qp, statt[0:1, SC["tmpr"] + 1:SC["tmpr"] + 2], OP.mult)
                                v.drain()
                                v.tensor_tensor(mg, mg, t2, OP.max)
                                v.drain()
                                v.tensor_scalar_mul(mg, mg, -1.0)
                                v.tensor_copy(gqr[0:1, sl * TGW:(sl + 1) * TGW], qp)
                                v.drain()
                                v.tensor_copy(strow[0:1, sl * TGW:(sl + 1) * TGW], mg)
                                v.drain()
                            v.sem_inc(a_row.h, 1)

                        a_kp.v += TLOC // TGW
                        a_gq.v += 16
                        gqdma = a_gq.v

                        @pbld.on("sync")
                        def _(s, gqrow_v=a_row.v):
                            s.wait_ge(a_row.h, gqrow_v)
                            s.dma_start(out=gqr[1:2, :], in_=strow[0:1, :]).then_inc(a_gq.h, 16)
                        gqdma_l[0] = gqdma

                    for qg in range(NQG):
                        chl = chunks_for(qg)
                        nch = len(chl)
                        smv, exv, mkv, pvv = {}, {}, {}, {}
                        for ci, (j, kp_, qoff, N) in enumerate(chl):
                            s_sm.v += 1
                            smv[ci] = s_sm.v
                            s_ex.v += 2 if L == 0 else 1
                            exv[ci] = s_ex.v
                            if L == 0 and kp_ >= 4 * qg:
                                s_mk.v += 2
                                mkv[ci] = s_mk.v
                            s_pv.v += 1
                            pvv[ci] = s_pv.v
                        finv0 = s_fin.v
                        s_fin.v += 1
                        finv2 = s_fin.v

                        @pbld.on("tensor")
                        def _(t, h=h, g_=g_, qg=qg, chl=chl, smv=smv, exv=exv, mkv=mkv,
                              pvv=pvv, kpar=kpar, klv2=klv2, prow2=prow2, gqrow=gqrow,
                              finv0=finv0, nch=nch, gqdma_l=tuple(gqdma_l)):
                            qbase = qg * 512

                            def pv(ci):
                                j, kp_, qoff, N = chl[ci]
                                tie = (L == 0 and kp_ >= 4 * qg)
                                if tie:
                                    t.wait_ge(s_mk.h, mkv[ci])
                                else:
                                    t.wait_ge(s_ex.h, exv[ci])
                                vt = j * NT + kp_
                                par = ci % 2
                                first = (ci == 0)
                                lastc = (ci == nch - 1)
                                for c in range(CH):
                                    t.matmul(pb[c][0:128, qoff:qoff + N],
                                             Vh[kpar][:, vt * DH + c * 128:vt * DH + (c + 1) * 128],
                                             Pt[par][0:128, 0:N], start=first, stop=lastc)
                                inst = t.matmul(pb[6][0:1, qoff:qoff + N], onesb[:, 0:1],
                                                Pt[par][0:128, 0:N], start=first, stop=lastc)
                                if L == 0:
                                    for c in range(CH):
                                        t.matmul(pb[2 + c][0:128, qoff:qoff + N],
                                                 Vh[kpar][:, vt * DH + c * 128:vt * DH + (c + 1) * 128],
                                                 Pg[par][0:128, 0:N], start=first, stop=lastc)
                                    inst = t.matmul(pb[7][0:1, qoff:qoff + N], onesb[:, 0:1],
                                                    Pg[par][0:128, 0:N], start=first, stop=lastc)
                                inst.then_inc(s_pv.h, 1)

                            t.wait_ge(a_klp[kpar].h, klv2)
                            t.wait_ge(a_row.h, max(prow2, gqrow))
                            if L == 0:
                                t.wait_ge(a_gq.h, gqdma_l[0])
                            t.wait_ge(s_fin.h, finv0)
                            for ci, (j, kp_, qoff, N) in enumerate(chl):
                                sb = 4 if L == 0 else 4 + ci % 2
                                back = 1 if L == 0 else 2
                                if ci >= back:
                                    t.wait_ge(s_ex.h, exv[ci - back])
                                scol = j * TLOC + kp_ * 128
                                for c in range(CH):
                                    inst = t.matmul(
                                        pb[sb][0:128, 0:N],
                                        Kh[kpar][:, c * GLEN + scol:c * GLEN + scol + 128],
                                        QO[:, (h * CH + c) * TLOC + qbase + qoff:(h * CH + c) * TLOC + qbase + qoff + N],
                                        start=(c == 0), stop=(c == CH - 1))
                                if L == 0:
                                    inst = t.matmul(pb[5][0:128, 0:N],
                                                    kplT[:, scol:scol + 128],
                                                    gqr[:, qbase + qoff:qbase + qoff + N],
                                                    start=True, stop=True)
                                inst.then_inc(s_sm.h, 1)
                                if ci >= 1:
                                    pv(ci - 1)
                            pv(nch - 1)

                        @pbld.on("scalar")
                        def _(s, chl=chl, smv=smv, pvv=pvv, negc=negc, nch=nch, qg=qg):
                            for ci, (j, kp_, qoff, N) in enumerate(chl):
                                par = ci % 2
                                if ci >= 2:
                                    s.wait_ge(s_pv.h, pvv[ci - 2])
                                s.wait_ge(s_sm.h, smv[ci])
                                s.activation(Pt[par][0:128, 0:N],
                                             pb[4 if L == 0 else 4 + ci % 2][0:128, 0:N],
                                             AF.Exp, bias=statt[0:128, negc:negc + 1],
                                             scale=1.0).then_inc(s_ex.h, 1)
                                if L == 0:
                                    s.activation(Pg[par][0:128, 0:N], pb[5][0:128, 0:N],
                                                 AF.Exp,
                                                 bias=statt[0:128, SC["zero"]:SC["zero"] + 1],
                                                 scale=1.0).then_inc(s_ex.h, 1)

                        if L == 0:
                            @pbld.on("vector")
                            def _(v, chl=chl, exv=exv, mkv=mkv, qg=qg):
                                for ci, (j, kp_, qoff, N) in enumerate(chl):
                                    if kp_ < 4 * qg:
                                        continue
                                    par = ci % 2
                                    v.wait_ge(s_ex.h, exv[ci])
                                    v.tensor_tensor(Pt[par][0:128, 0:128], Pt[par][0:128, 0:128],
                                                    tiet[:, j * 128:(j + 1) * 128], OP.mult).then_inc(s_mk.h, 1)
                                    v.tensor_tensor(Pg[par][0:128, 0:128], Pg[par][0:128, 0:128],
                                                    tiet[:, j * 128:(j + 1) * 128], OP.mult).then_inc(s_mk.h, 1)
                                    v.drain()

                        a_mm.v += 1
                        fbcm = a_mm.v
                        a_row.v += 1
                        frow = a_row.v

                        @pbld.on("vector")
                        def _(v, h=h, pvend=pvv[nch - 1], frow=frow):
                            v.wait_ge(s_pv.h, pvend)
                            la = rowsf[0:1, 0 * TGW:1 * TGW]
                            lb = rowsf[0:1, 1 * TGW:2 * TGW]
                            v.tensor_copy(la, pb[6][0:1, 0:TGW])
                            if L == 0:
                                v.tensor_copy(lb, pb[7][0:1, 0:TGW])
                            v.drain()
                            v.reciprocal(la, la)
                            if L == 0:
                                v.reciprocal(lb, lb)
                            v.drain()
                            if L == 0:
                                v.tensor_single_scalar(la, la, hst[0:1, 8 + h:9 + h], OP.mult)
                                v.tensor_single_scalar(lb, lb, hst[0:1, h:h + 1], OP.mult)
                                v.drain()
                            v.sem_inc(a_row.h, 1)

                        @pbld.on("tensor")
                        def _(t, frow=frow):
                            t.wait_ge(a_row.h, frow)
                            inst = t.matmul(pb[4][:, 0:TGW], onesr[:, :],
                                            rowsf[0:1, 0 * TGW:1 * TGW], start=True, stop=True)
                            if L == 0:
                                inst = t.matmul(pb[5][:, 0:TGW], onesr[:, :],
                                                rowsf[0:1, 1 * TGW:2 * TGW], start=True, stop=True)
                            inst.then_inc(a_mm.h, 1)

                        @pbld.on("vector")
                        def _(v, h=h, qg=qg, fbcm=fbcm):
                            v.wait_ge(a_mm.h, fbcm)
                            v.tensor_copy(stg2[0][:, :], pb[4][:, 0:TGW])
                            if L == 0:
                                v.tensor_copy(stg2[1][:, :], pb[5][:, 0:TGW])
                            v.drain()
                            for c in range(CH):
                                qo = QO[:, (h * CH + c) * TLOC + qg * TGW:(h * CH + c) * TLOC + (qg + 1) * TGW]
                                if L == 0:
                                    v.tensor_tensor(cmb[0][:, :], pb[c][:, 0:TGW], stg2[0][:, :], OP.mult)
                                    v.tensor_tensor(cmb[1][:, :], pb[2 + c][:, 0:TGW], stg2[1][:, :], OP.mult)
                                    v.drain()
                                    inst = v.tensor_tensor(qo, cmb[0][:, :], cmb[1][:, :], OP.add)
                                    v.drain()
                                else:
                                    inst = v.tensor_tensor(qo, pb[0][:, 0:TGW], stg2[0][:, :], OP.mult)
                                    v.drain()
                            inst.then_inc(s_fin.h, 1)

                kvld_g[kpar] = (s_fin, s_fin.v)

            arena.pop()
            arena.push()
            wot = [arena.alloc([128, DC * 128], BF16) for _ in range(2)]
            wot_g = {0: None, 1: None}
            fin_all = s_fin.v
            xl_g = {0: None, 1: None}
            ob_g = {}
            for ec in range(DC):
                par = ec % 2
                gd_w = wot_g[par]
                a_wl[par].v += 16
                wlv = a_wl[par].v

                @pbld.on("gpsimd")
                def _(g, ec=ec, par=par, gd_w=gd_w, fin_all=fin_all):
                    g.wait_ge(s_fin.h, fin_all)
                    W(g, gd_w)
                    g.dma_start(out=wot[par][:, 0:DC * 128],
                                in_=wob[ec, :, :]).then_inc(a_wl[par].h, 16)

                for tg in range(NTG):
                    u = ec * NTG + tg
                    xpar = u % 2
                    gd_x = xl_g[xpar]
                    s_xl2[xpar].v += 16
                    xlv2 = s_xl2[xpar].v

                    @pbld.on("sync")
                    def _(s, ec=ec, tg=tg, xpar=xpar, gd_x=gd_x, fin_all=fin_all):
                        s.wait_ge(s_fin.h, fin_all)
                        W(s, gd_x)
                        s.dma_start(out=stg2[xpar][:, :],
                                    in_=xres[tg, ec, :, :]).then_inc(s_xl2[xpar].h, 16)

                    obank = 4 + u % 4
                    gd_ob = ob_g.get(u % 4)
                    s_om.v += 1
                    omv = s_om.v

                    @pbld.on("tensor")
                    def _(t, ec=ec, tg=tg, par=par, wlv=wlv, obank=obank, gd_ob=gd_ob):
                        t.wait_ge(a_wl[par].h, wlv)
                        W(t, gd_ob)
                        for cc in range(DC):
                            last = (cc == DC - 1) and (L == 0)
                            inst = t.matmul(pb[obank][:, 0:TGW],
                                            wot[par][:, cc * 128:(cc + 1) * 128],
                                            QO[:, cc * TLOC + tg * TGW:cc * TLOC + (tg + 1) * TGW],
                                            start=(cc == 0), stop=last)
                        if L == 1:
                            inst = t.matmul(pb[obank][:, 0:TGW], bor[0:1, ec * 128:(ec + 1) * 128],
                                            onesrb[0:1, 0:TGW], start=False, stop=True)
                        inst.then_inc(s_om.h, 1)

                    s_oe.v += 1
                    oev = s_oe.v
                    gd_st = stgb_g[xpar]

                    @pbld.on("vector")
                    def _(v, omv=omv, obank=obank, xpar=xpar, xlv2=xlv2, gd_st=gd_st):
                        v.wait_ge(s_om.h, omv)
                        v.wait_ge(s_xl2[xpar].h, xlv2)
                        W(v, gd_st)
                        v.tensor_tensor(cmb[xpar][:, :],
                                        pb[obank][:, 0:TGW], stg2[xpar][:, :],
                                        OP.add).then_inc(s_oe.h, 1)
                        v.drain()

                    s_os[xpar].v += 16
                    osv = s_os[xpar].v

                    @pbld.on("scalar")
                    def _(s, ec=ec, tg=tg, xpar=xpar, oev=oev):
                        s.wait_ge(s_oe.h, oev)
                        s.dma_start(out=xres[tg, ec, :, :],
                                    in_=cmb[xpar][:, :]).then_inc(s_os[xpar].h, 16)

                    xl_g[xpar] = (s_oe, oev)
                    ob_g[u % 4] = (s_oe, oev)
                    stgb_g[xpar] = (s_os[xpar], osv)
                wot_g[par] = (s_om, s_om.v)

            pbld.flush()
            arena.pop()
            arena.pop()
            xres_guards = [(s_os[0], s_os[0].v), (s_os[1], s_os[1].v)]

        sub = 0
        def prep_prior():
            return list(xres_guards) + [(f_ts[0], f_ts[0].v), (f_ts[1], f_ts[1].v)]

        if n_sublayers >= 1:
            emit_quant_prep(0, prior=prep_prior())
            emit_ffn(0, "f00")
        if n_sublayers >= 2:
            emit_attn(0)
        if n_sublayers >= 3:
            emit_ffn(0, "f01")
        if n_sublayers >= 4:
            emit_quant_prep(1, prior=prep_prior())
            emit_ffn(1, "f10")
        if n_sublayers >= 5:
            emit_attn(1)
        if n_sublayers >= 6:
            emit_ffn(1, "f11")

        arena.push()
        fmt = [arena.alloc([128, TGW], F32) for _ in range(2)]
        ytiles = [arena.alloc([128, D], F32) for _ in range(4)]
        p7mm, p7cp, yst = S("p7mm"), S("p7cp"), S("yst")
        fld = [S("fld0"), S("fld1")]

        @pbld.on("sync")
        def _(s):
            for sem, val in xres_guards:
                s.wait_ge(sem.h, val)
            for tg in range(NTG):
                for dc in range(DC):
                    i = tg * DC + dc
                    if i >= 2:
                        s.wait_ge(p7mm.h, i - 1)
                    fld[i % 2].v += 16
                    s.dma_start(out=fmt[i % 2][:, :],
                                in_=xres[tg, dc, :, :]).then_inc(fld[i % 2].h, 16)
                for tt in range(4):
                    k = tg * 4 + tt
                    s.wait_ge(p7cp.h, (tg + 1) * DC * 4)
                    yst.v += 16
                    s.dma_start(out=yout[k * 128:(k + 1) * 128, :],
                                in_=ytiles[tt][:, :]).then_inc(yst.h, 16)
            s.wait_ge(yst.h, yst.v)

        @pbld.on("tensor")
        def _(t):
            for tg in range(NTG):
                for dc in range(DC):
                    i = tg * DC + dc
                    t.wait_ge(fld[i % 2].h, 16 * (i // 2 + 1))
                    if i >= 4:
                        t.wait_ge(p7cp.h, (i - 3) * 4)
                    for tt in range(4):
                        inst = t.transpose(pb[4 + i % 4][:, tt * 128:(tt + 1) * 128],
                                           fmt[i % 2][:, tt * 128:(tt + 1) * 128],
                                           identt[:, :])
                    inst.then_inc(p7mm.h, 1)
        p7mm.v = NTG * DC

        @pbld.on("vector")
        def _(v):
            for tg in range(NTG):
                if tg >= 1:
                    v.wait_ge(yst.h, tg * 4 * 16)
                for dc in range(DC):
                    i = tg * DC + dc
                    v.wait_ge(p7mm.h, i + 1)
                    for tt in range(4):
                        v.tensor_copy(ytiles[tt][:, dc * 128:(dc + 1) * 128],
                                      pb[4 + i % 4][:, tt * 128:(tt + 1) * 128]).then_inc(p7cp.h, 1)
        p7cp.v = NTG * DC * 4
        pbld.flush()
        arena.pop()

    return nc


_CACHE = {}

def _get_nc(n_sublayers):
    if n_sublayers not in _CACHE:
        _CACHE[n_sublayers] = build_program(n_sublayers)
    return _CACHE[n_sublayers]


def _host_constants(rank):
    tiles = np.arange(NT) * 4 + rank
    pos = (tiles[:, None] * 128 + np.arange(128)[None, :]).reshape(-1)
    inv = (1.0 / (10000.0 ** (np.arange(0, DH0, 2, dtype=np.float32) / np.float32(DH0)))).astype(np.float32)
    f = (pos.astype(np.float32)[:, None] * inv[None, :]).astype(np.float32)
    cos = np.cos(f.astype(np.float64)).astype(np.float32)
    sin = np.sin(f.astype(np.float64)).astype(np.float32)
    cos = np.ascontiguousarray(np.concatenate([cos, cos], axis=1).T) * np.float32(LAM0)
    sin = np.ascontiguousarray(np.concatenate([sin, sin], axis=1).T) * np.float32(LAM0)
    trig = np.array([[sx, sy, sz] for sx in (1.0, -1.0) for sy in (1.0, -1.0) for sz in (1.0, -1.0)],
                    dtype=np.float32) / np.sqrt(3.0, dtype=np.float32)
    dirs = np.ascontiguousarray(trig.T / np.float32(LAM0)).astype(BF)
    tri = np.triu(np.ones((128, 128), dtype=np.float32))
    tie = np.zeros((NRANK, 128, 128), dtype=np.float32)
    for j in range(NRANK):
        if j < rank:
            tie[j] = 1.0
        elif j == rank:
            tie[j] = tri
    tie = np.ascontiguousarray(tie.transpose(1, 0, 2).reshape(128, NRANK * 128))
    return {
        "ident": np.eye(128, dtype=np.float32),
        "identb": np.eye(128, dtype=BF),
        "onesd": np.ones((1, 8192), dtype=BF),
        "rope_cos": cos, "rope_sin": sin,
        "dirs": dirs, "tiemask": tie.astype(BF),
    }


def kernel(L0_attn_x, L0_attn_Wq, L0_attn_Wk, L0_attn_Wv, L0_attn_Wo, L0_attn_head_scales,
           L0_ffn_w1, L0_ffn_w3, L0_ffn_w2,
           L1_attn_wq, L1_attn_bq, L1_attn_wk, L1_attn_bk, L1_attn_wv, L1_attn_bv,
           L1_attn_wo, L1_attn_bo, L1_ffn_w1, L1_ffn_w2, L1_ffn_w3):
    x = np.ascontiguousarray(np.asarray(L0_attn_x, dtype=np.float32))
    f32 = lambda a: np.ascontiguousarray(np.asarray(a, dtype=np.float32))

    def lhs_block(w, E):
        ww = f32(w).reshape(DC, 128, w.shape[1] // E, E)
        return np.ascontiguousarray(ww.transpose(2, 1, 0, 3).reshape(-1, 128, DC * E))

    def rhs_block(w):
        ww = f32(w).reshape(DC, 128, w.shape[1] // 512, 512)
        return np.ascontiguousarray(ww.transpose(2, 0, 1, 3).reshape(-1, 128 * 0 + 128, 0 * 512 + 512)) \
            if False else np.ascontiguousarray(ww.transpose(2, 0, 1, 3)).reshape(w.shape[1] // 512, 128, DC * 512) \
            if False else np.ascontiguousarray(ww.transpose(2, 0, 1, 3))

    def rhsb(w):
        ww = f32(w).reshape(DC, 128, w.shape[1] // 512, 512)
        return np.ascontiguousarray(ww.transpose(2, 1, 0, 3).reshape(-1, 128, DC * 512))

    weights = {
        "L0_Wqb": lhs_block(L0_attn_Wq, DH0), "L0_Wkb": lhs_block(L0_attn_Wk, DH0),
        "L0_Wvn": rhsb(L0_attn_Wv), "L0_Wob": lhs_block(L0_attn_Wo, 128),
        "L0_hs": f32(L0_attn_head_scales).reshape(1, 8),
        "L0_w1": f32(L0_ffn_w1), "L0_w3": f32(L0_ffn_w3), "L0_w2": f32(L0_ffn_w2),
        "L1_wqb": lhs_block(L1_attn_wq, DH1), "L1_wkb": lhs_block(L1_attn_wk, DH1),
        "L1_wvn": rhsb(L1_attn_wv), "L1_wob": lhs_block(L1_attn_wo, 128),
        "L1_bq": f32(L1_attn_bq).reshape(1, D), "L1_bk": f32(L1_attn_bk).reshape(1, D),
        "L1_bv": f32(L1_attn_bv).reshape(1, D), "L1_bo": f32(L1_attn_bo).reshape(1, D),
        "L1_w1": f32(L1_ffn_w1), "L1_w3": f32(L1_ffn_w3), "L1_w2": f32(L1_ffn_w2),
    }
    nc = _get_nc(_N_SUBLAYERS)
    in_maps = []
    for c in range(NCORE):
        b, r = c // NRANK, c % NRANK
        tiles = np.arange(NT) * 4 + r
        xloc = np.ascontiguousarray(x[b].reshape(64, 128, D)[tiles].reshape(TLOC, D))
        m = {"xloc": xloc}
        m.update(weights)
        m.update(_host_constants(r))
        in_maps.append(m)
    res = run_bass_kernel_spmd(nc, in_maps, core_ids=list(range(NCORE)))
    out = np.empty((B, T, D), dtype=np.float32)
    for c in range(NCORE):
        b, r = c // NRANK, c % NRANK
        tiles = np.arange(NT) * 4 + r
        out[b].reshape(64, 128, D)[tiles] = res.results[c]["yout"].reshape(NT, 128, D)
    return out
